# version 02
# baseline (speedup 1.0000x reference)
import numpy as np
import concourse.bass as bass
import concourse.mybir as mybir
from concourse.alu_op_type import AluOpType as ALU

F32 = mybir.dt.float32
BF16 = mybir.dt.bfloat16
I32 = mybir.dt.int32
AF = mybir.ActivationFunctionType
AX = mybir.AxisListType

ENGINES = ("sync", "tensor", "vector", "scalar", "gpsimd")
DMA_METHODS = {"dma_start", "dma_start_transpose"}


class Prog:
    def __init__(self, nc):
        self.nc = nc
        self.streams = {e: [] for e in ENGINES}
        self.counts = {}
        self.semnames = []
        self.last_tok = {e: None for e in ENGINES}
        self.nops = 0

    def sem(self, name):
        if name not in self.counts:
            self.counts[name] = 0
            self.semnames.append(name)
        return name

    def tok(self, name):
        return (name, self.counts[name])

    def op(self, eng, meth, *args, waits=(), inc=None, incn=None, **kw):
        self.nops += 1
        if incn is None:
            incn = 16 if meth in DMA_METHODS else 1
        tok = None
        if inc is not None:
            self.sem(inc)
            self.counts[inc] += incn
            tok = (inc, self.counts[inc])
            self.last_tok[eng] = tok
        wmax = {}
        for (s, v) in waits:
            if v > wmax.get(s, -1):
                wmax[s] = v
        wlist = tuple(wmax.items())
        incv = incn if inc is not None else None

        def f(h, meth=meth, args=args, kw=kw, wlist=wlist, inc=inc, incv=incv, sems=None):
            for (s, v) in wlist:
                h.wait_ge(self._semh[s], v)
            r = getattr(h, meth)(*args, **kw)
            if inc is not None:
                r.then_inc(self._semh[inc], incv)
        self.streams[eng].append(f)
        return tok

    def raw(self, eng, fn):
        self.streams[eng].append(lambda h, fn=fn: fn(h))

    def wait(self, eng, *toks):
        wmax = {}
        for (s, v) in toks:
            if v > wmax.get(s, -1):
                wmax[s] = v
        wlist = tuple(wmax.items())

        def f(h, wlist=wlist):
            for (s, v) in wlist:
                h.wait_ge(self._semh[s], v)
        self.streams[eng].append(f)

    def finalize(self, block, sem_handles):
        self._semh = sem_handles
        for e in ENGINES:
            stream = self.streams[e]
            dec = getattr(block, e)

            def runner(h, stream=stream):
                for f in stream:
                    f(h)
            dec(runner)


def ceil_div(a, b):
    return (a + b - 1) // b


"""Megakernel phases for mk_seq_preln_d2 (per-core programs, raw bass).

Per core: x [T=2048, D=2048] f32 (one batch element). Layouts:
token-major [tok128, d-free]; channel-major "T" [d128, tok-free];
guarded transposed tiles [128, GP+T], cols 0..GP-1 zero, data at +GP.

Sync: framework tokens; same-engine FIFO trusted; every cross-engine RAW/WAR
edge gets an explicit token wait.
"""
import numpy as np


T = 2048
D = 2048
K = D // 128
NB = D // 128
TIL = T // 128
G4 = T // 512
GP = 16
GW = GP + T




def alloc_psum(nc, es):
    ps = {}
    for nm in ("A0", "A1", "B0", "B1"):
        ps[nm] = es.enter_context(nc.psum_tensor("ps" + nm, [128, 512], F32))
    ps["BIGa"] = es.enter_context(nc.psum_tensor("psBIGa", [128, 1024], F32))
    ps["BIGb"] = es.enter_context(nc.psum_tensor("psBIGb", [128, 1024], F32))
    return ps

def emit_common(P, nc, sb):
    t_io = P.op("gpsimd", "iota", sb["m_bcastT"][:].bitcast(I32), pattern=[[-1, T]],
                base=T - 1, channel_multiplier=0, inc="cG")
    t_mb = P.op("vector", "tensor_copy", sb["m_bcastT"][:], sb["m_bcastT"][:].bitcast(I32),
                waits=[t_io], inc="cV")
    P.op("vector", "tensor_scalar", sb["s1a"][:], sb["s1a"][:], 0.0, 0.0,
         op0=ALU.mult, op1=ALU.add, accum_out=sb["s2a"][:], inc="cV")
    t_fl = P.op("vector", "scalar_tensor_tensor", sb["s1a"][:], sb["s1a"][:], 0.0,
                sb["s1a"][:], op0=ALU.mult, op1=ALU.mult, accum_out=sb["s2a"][:],
                inc="cV")
    return [t_mb, t_fl]


def load_cols(P, vec_dram, dst, n_cols, queue="sync", inc="qS", waits=()):
    toks = []
    for k in range(n_cols):
        toks.append(P.op(queue, "dma_start", out=dst[:, k:k + 1],
                         in_=vec_dram[k * 128:(k + 1) * 128].rearrange("(p one) -> p one", one=1),
                         waits=(waits if k == 0 else ()), inc=inc))
    return toks


def bcast_row(P, vec_dram, dst, n, queue="gpsimd", inc="qG", waits=()):
    src = vec_dram[0:n].partition_broadcast(128)
    return P.op(queue, "dma_start", out=dst[:, 0:n], in_=src, waits=waits, inc=inc)


class LN:
    def __init__(self, P, sb):
        self.P = P
        self.sb = sb
        self.g = {}

    def full(self, x_ap, out_ap, sc1_ap, sc2_ap, width, eps, tag, waits):
        P, sb = self.P, self.sb
        s1 = sb["s1" + tag]; s2 = sb["s2" + tag]; mean = sb["mn" + tag]
        nv = sb["nv" + tag]; ve = sb["ve" + tag]; rstd = sb["rs" + tag]; nmrs = sb["nm" + tag]
        g = self.g.get(tag, {})
        waits = list(waits)
        t1 = P.op("vector", "tensor_scalar", sc1_ap, x_ap, 1.0, 0.0,
                  op0=ALU.mult, op1=ALU.add, accum_out=s1[:], waits=waits, inc="cV")
        t2 = P.op("scalar", "activation", sc2_ap, x_ap, AF.Square, accum_out=s2[:],
                  waits=waits + ([g["tn"]] if g.get("tn") else []), inc="cS")
        tm = P.op("vector", "tensor_scalar", mean[:], s1[:], 1.0 / width, None,
                  op0=ALU.mult, waits=[t1], inc="cV")
        tq = P.op("scalar", "mul", s2[:], s2[:], 1.0 / width, waits=[t2], inc="cS")
        tn = P.op("vector", "scalar_tensor_tensor", nv[:], mean[:], mean[:], s2[:],
                  op0=ALU.mult, op1=ALU.subtract, waits=[tq, tm], inc="cV")
        tv = P.op("vector", "tensor_scalar", ve[:], nv[:], -1.0, eps,
                  op0=ALU.mult, op1=ALU.add,
                  waits=[tn] + ([g["tr"]] if g.get("tr") else []), inc="cV")
        tl = P.op("scalar", "activation", ve[:], ve[:], AF.Ln, waits=[tv], inc="cS")
        tr = P.op("scalar", "activation", rstd[:], ve[:], AF.Exp, scale=-0.5,
                  waits=[tl] + ([g["tz"]] if g.get("tz") else []), inc="cS")
        tz = P.op("vector", "tensor_scalar", nmrs[:], mean[:], rstd[:], -1.0,
                  op0=ALU.mult, op1=ALU.mult,
                  waits=[tr] + ([g["tu"]] if g.get("tu") else []), inc="cV")
        tu = P.op("scalar", "activation", out_ap, x_ap, AF.Identity,
                  scale=rstd[:], bias=nmrs[:], waits=[tz, tr], inc="cS")
        self.g[tag] = {"tn": tn, "tr": tr, "tz": tz, "tu": tu}
        return tu, tz


def emit_p0(P, nc, w, sc, sb, ps, pre_toks):
    EPSI = 1e-12
    EPSO = 1e-5
    ln = LN(P, sb)

    tg1 = bcast_row(P, w["ln1_g"], sb["gA"], D)
    tb1 = bcast_row(P, w["ln1_b"], sb["bA"], D)
    tg2 = bcast_row(P, w["tm_g"], sb["gB"], D)
    tb2 = bcast_row(P, w["tm_b"], sb["bB"], D)
    t_tf = load_cols(P, w["time_first"], sb["tf_cols"], K)
    t_td = load_cols(P, w["time_decay"], sb["td_cols"], K)
    t_bo = load_cols(P, w["bo"], sb["boT_cols"], K)
    t_itf = P.op("vector", "tensor_scalar", sb["itf_cols"][:], sb["tf_cols"][:],
                 -1.0, 1.0, op0=ALU.mult, op1=ALU.add, waits=t_tf, inc="cV")

    def ln_affine_tail(gts, guards, w_first):
        tgA, tbA, tgB, tbB = gts
        tub, _ = ln.full(sb["x2_f32"][:], sb["u_f32"][:], sb["x2_f32"][:], sb["u_f32"][:],
                         D, EPSI, "b",
                         w_first + ([guards["u_rd"]] if guards.get("u_rd") else []))
        tx1 = P.op("vector", "scalar_tensor_tensor", sb["x1_f32"][:], sb["u_f32"][:],
                   1.0, sb["gA"][:], op0=ALU.mult, op1=ALU.mult,
                   waits=[tub, tgA] + ([guards["x1_rd"]] if guards.get("x1_rd") else []),
                   inc="cV")
        tx1b = P.op("vector", "tensor_tensor", sb["x1_f32"][:], sb["x1_f32"][:],
                    sb["bA"][:], op=ALU.add, waits=[tbA], inc="cV")
        tuc, _ = ln.full(sb["x1_f32"][:], sb["u_f32"][:], sb["x1_f32"][:], sb["u_f32"][:],
                         D, EPSI, "a", [tx1b, tub])
        t3 = P.op("vector", "scalar_tensor_tensor", sb["x1_f32"][:], sb["u_f32"][:],
                  1.0, sb["gB"][:], op0=ALU.mult, op1=ALU.mult,
                  waits=[tuc, tgB], inc="cV")
        t4 = P.op("vector", "tensor_tensor", sb["x1_f32"][:], sb["x1_f32"][:], sb["bB"][:],
                  op=ALU.add,
                  waits=[tbB] + ([guards["sp_rd"]] if guards.get("sp_rd") else []),
                  inc="cV")
        return t4, t3

    xw = [None, None]
    guards = {"u_rd": None, "x1_rd": None, "sp_rd": None}
    for i in range(TIL):
        buf = i % 2
        xs = sb["xin0"] if buf == 0 else sb["xin1"]
        tld = P.op("sync", "dma_start", out=xs[:], in_=w["x_in"][i * 128:(i + 1) * 128, :],
                   waits=([xw[buf]] if xw[buf] else []), inc="qS")
        tpre, _ = ln.full(xs[:], sb["x2_f32"][:], xs[:], sb["x2_f32"][:], D, EPSO, "a",
                          [tld] + ([guards["u_rd"]] if guards.get("u_rd") else []))
        t4, t3 = ln_affine_tail((tg1, tb1, tg2, tb2), guards, [tpre])
        guards["u_rd"] = t3
        guards["x1_rd"] = t4
        tsp = P.op("gpsimd", "dma_start", out=sc["spill_a"][i * 128:(i + 1) * 128, :],
                   in_=sb["x1_f32"][:], waits=[t4], inc="qG")
        guards["sp_rd"] = tsp
        xw[buf] = ln.g["a"]["tu"]
    t_spill_a_done = P.tok("qG")
    t_stageB_dve = P.tok("cV")
    t_stageB_act = P.tok("cS")

    tz0 = P.op("vector", "memset", sb["xT_g"][:], 0.0, inc="cV")
    xT_tok = []
    for k in range(K):
        view = sb["xT_g"][:, k * GW + GP: k * GW + GP + T]
        xT_tok.append(P.op("sync", "dma_start_transpose", out=view,
                           in_=sc["spill_a"][:, k * 128:(k + 1) * 128],
                           waits=[t_spill_a_done, tz0], inc="qS"))

    def xlnT(k, c0, c1):
        return sb["xT_g"][:, k * GW + GP + c0: k * GW + GP + c1]

    def xsT(k, c0, c1):
        return sb["xT_g"][:, k * GW + GP + c0 - 1: k * GW + GP + c1 - 1]

    wbuf_w = [None, None]

    def load_colblock(Wdram, nb, buf, extra=()):
        wcb = sb["wcb0"] if buf == 0 else sb["wcb1"]
        cbsrc = Wdram[:, nb * 128:(nb + 1) * 128].rearrange("(k p) n -> p k n", p=128)
        tw = P.op("gpsimd", "dma_start", out=wcb[:].rearrange("p (k n) -> p k n", n=128), in_=cbsrc,
                  waits=([wbuf_w[buf]] if wbuf_w[buf] else []) + list(extra), inc="qG")
        return wcb, tw

    pv_ev = [None, None]
    tdec_guard = None
    for nb in range(NB):
        buf = nb % 2
        wcb, tw = load_colblock(w["Wv"], nb, buf)
        for g in range(G4):
            pbuf = g % 2
            pv = ps["A0"] if pbuf == 0 else ps["A1"]
            sig = sb["sig0"] if pbuf == 0 else sb["sig1"]
            tdec = P.op("scalar", "activation", sb["decT"][:], sb["m_bcastT"][:, g * 512:(g + 1) * 512],
                        AF.Exp, scale=sb["td_cols"][:, nb:nb + 1],
                        waits=pre_toks + t_td + ([tdec_guard] if tdec_guard else []), inc="cS")
            mwaits = [tw] + xT_tok + ([pv_ev[pbuf]] if pv_ev[pbuf] else [])
            for k in range(K):
                tok_mm = P.op("tensor", "matmul", pv[:, 0:512],
                              wcb[:, k * 128:(k + 1) * 128], xlnT(k, g * 512, (g + 1) * 512),
                              start=(k == 0), stop=(k == K - 1),
                              waits=(mwaits if k == 0 else []),
                              inc=("cT" if k == K - 1 else None))
            tev = P.op("vector", "scalar_tensor_tensor", sig[:],
                       sb["decT"][:], 1.0, pv[:, 0:512],
                       op0=ALU.mult, op1=ALU.mult,
                       accum_out=sb["vmixP"][:, nb * G4 + g: nb * G4 + g + 1],
                       waits=[tok_mm, tdec], inc="cV")
            pv_ev[pbuf] = tev
            tdec_guard = tev
        wbuf_w[buf] = tok_mm
    t_red = P.op("vector", "tensor_reduce",
                 sb["vmixT"][:].rearrange("p (nb one) -> p nb one", one=1),
                 sb["vmixP"][:].rearrange("p (nb g) -> p nb g", g=G4),
                 axis=AX.X, op=ALU.add, waits=[tdec_guard], inc="cV")
    t_vbf = P.op("vector", "tensor_copy", sb["vmixT_bf"][:], sb["vmixT"][:],
                 waits=[t_red], inc="cV")

    p2_ev = [None, None]
    for nb in range(NB):
        buf = nb % 2
        wcb, tw = load_colblock(w["Wo"], nb, buf)
        pbuf = nb % 2
        pt = ps["B0"] if pbuf == 0 else ps["B1"]
        mwaits = [tw, t_vbf] + ([p2_ev[pbuf]] if p2_ev[pbuf] else [])
        for k in range(K):
            tok_mm = P.op("tensor", "matmul", pt[:, 0:1],
                          wcb[:, k * 128:(k + 1) * 128], sb["vmixT_bf"][:, k:k + 1],
                          start=(k == 0), stop=(k == K - 1),
                          waits=(mwaits if k == 0 else []),
                          inc=("cT" if k == K - 1 else None))
        tev = P.op("scalar", "activation", sb["twoT_cols"][:, nb:nb + 1], pt[:, 0:1],
                   AF.Identity, bias=sb["boT_cols"][:, nb:nb + 1],
                   waits=[tok_mm] + t_bo, inc="cS")
        p2_ev[pbuf] = tev
        wbuf_w[buf] = tok_mm
    t_two = P.tok("cS")

    pr_ev = [None, None]
    tou_guard = None
    wrp_guard = None
    for nb in range(NB):
        buf = nb % 2
        wcb, tw = load_colblock(w["Wr"], nb, buf)
        tdr = None
        for k in range(K):
            tdr = P.op("vector", "tensor_scalar", sb["wrp"][:, k * 128:(k + 1) * 128],
                       wcb[:, k * 128:(k + 1) * 128], sb["tf_cols"][:, k:k + 1], None,
                       op0=ALU.mult,
                       waits=(([tw, t_itf] + ([wrp_guard] if wrp_guard else [])) if k == 0 else []))
        tdr2 = P.op("vector", "tensor_tensor", wcb[:], wcb[:], sb["wrp"][:],
                    op=ALU.subtract, inc="cV")
        for g in range(G4):
            pbuf = g % 2
            pr = ps["A0"] if pbuf == 0 else ps["A1"]
            mwaits = [tdr2] + ([pr_ev[pbuf]] if pr_ev[pbuf] else [])
            for k in range(K):
                P.op("tensor", "matmul", pr[:, 0:512],
                     sb["wrp"][:, k * 128:(k + 1) * 128], xlnT(k, g * 512, (g + 1) * 512),
                     start=(k == 0), stop=False,
                     waits=(mwaits if k == 0 else []))
                tok_mm = P.op("tensor", "matmul", pr[:, 0:512],
                              wcb[:, k * 128:(k + 1) * 128], xsT(k, g * 512, (g + 1) * 512),
                              start=False, stop=(k == K - 1),
                              inc=("cT" if k == K - 1 else None))
            sig = sb["sig0"] if pbuf == 0 else sb["sig1"]
            tsg = P.op("scalar", "activation", sig[:], pr[:, 0:512], AF.Sigmoid,
                       waits=[tok_mm] + ([pr_ev[pbuf]] if pr_ev[pbuf] else []), inc="cS")
            ttou = P.op("vector", "tensor_scalar", sb["touT_buf"][:, g * 512:(g + 1) * 512],
                        sig[:], sb["twoT_cols"][:, nb:nb + 1], None,
                        op0=ALU.mult,
                        waits=[tsg, t_two] + ([tou_guard] if (tou_guard and g == 0) else []),
                        inc="cV")
            pr_ev[pbuf] = ttou
        tsp = P.op("sync", "dma_start", out=sc["tou_sp"][nb * 128:(nb + 1) * 128, :],
                   in_=sb["touT_buf"][:], waits=[P.tok("cV")], inc="qS")
        tou_guard = tsp
        wbuf_w[buf] = tok_mm
        wrp_guard = tok_mm
    t_tou_done = P.tok("qS")

    tg3 = bcast_row(P, w["ln2_g"], sb["gA"], D, waits=[t_stageB_dve])
    tb3 = bcast_row(P, w["ln2_b"], sb["bA"], D, waits=[t_stageB_dve])
    tg4 = bcast_row(P, w["cm_g"], sb["gB"], D, waits=[t_stageB_dve])
    tb4 = bcast_row(P, w["cm_b"], sb["bB"], D, waits=[t_stageB_dve])
    xw = [None, None]
    tuw = [None, None]
    guards = {"u_rd": None, "x1_rd": None, "sp_rd": None}
    twr_prev = None
    for i in range(TIL):
        buf = i % 2
        xs = sb["xin0"] if buf == 0 else sb["xin1"]
        tub = sb["tou0"] if buf == 0 else sb["tou1"]
        tld = P.op("sync", "dma_start", out=xs[:], in_=w["x_in"][i * 128:(i + 1) * 128, :],
                   waits=([xw[buf]] if xw[buf] else [t_stageB_act]), inc="qS")
        tto = P.op("sync", "dma_start_transpose", out=tub[:],
                   in_=sc["tou_sp"][:, i * 128:(i + 1) * 128],
                   waits=[t_tou_done] + ([tuw[buf]] if tuw[buf] else []), inc="qS")
        tpre, _ = ln.full(xs[:], sb["u_f32"][:], xs[:], sb["u_f32"][:], D, EPSO, "a",
                          [tld] + ([guards["u_rd"]] if guards.get("u_rd") else []) +
                          ([twr_prev] if twr_prev else []))
        tx2 = P.op("vector", "tensor_tensor", sb["x2_f32"][:], sb["u_f32"][:], tub[:],
                   op=ALU.add, waits=[tpre, tto] +
                   ([guards["x1_rd"]] if guards.get("x1_rd") else []), inc="cV")
        txr = P.op("vector", "tensor_tensor", sb["u_f32"][:], xs[:], sb["x2_f32"][:],
                   op=ALU.add, waits=[tx2], inc="cV")
        twr = P.op("sync", "dma_start", out=sc["xres"][i * 128:(i + 1) * 128, :],
                   in_=sb["u_f32"][:], waits=[txr], inc="qS")
        twr_prev = twr
        t4, t3 = ln_affine_tail((tg3, tb3, tg4, tb4), guards, [txr, twr])
        guards["u_rd"] = t3
        guards["x1_rd"] = t4
        tsp = P.op("gpsimd", "dma_start", out=sc["spill_b"][i * 128:(i + 1) * 128, :],
                   in_=sb["x1_f32"][:], waits=[t4], inc="qG")
        guards["sp_rd"] = tsp
        xw[buf] = txr
        tuw[buf] = tx2
    t_spill_b_done = P.tok("qG")

    t_allmm = P.tok("cT")
    tz1 = P.op("vector", "memset", sb["xT_g"][:], 0.0, waits=[t_allmm], inc="cV")
    xT_tok = []
    for k in range(K):
        view = sb["xT_g"][:, k * GW + GP: k * GW + GP + T]
        xT_tok.append(P.op("sync", "dma_start_transpose", out=view,
                           in_=sc["spill_b"][:, k * 128:(k + 1) * 128],
                           waits=[t_spill_b_done, tz1], inc="qS"))

    pr_ev = [None, None]
    tou_guard = P.tok("qS")
    for nb in range(NB):
        buf = nb % 2
        wcb, tw = load_colblock(w["Wcr"], nb, buf)
        for g in range(G4):
            pbuf = g % 2
            pr = ps["A0"] if pbuf == 0 else ps["A1"]
            mwaits = [tw] + xT_tok + ([pr_ev[pbuf]] if pr_ev[pbuf] else [])
            for k in range(K):
                tok_mm = P.op("tensor", "matmul", pr[:, 0:512],
                              wcb[:, k * 128:(k + 1) * 128], xlnT(k, g * 512, (g + 1) * 512),
                              start=(k == 0), stop=(k == K - 1),
                              waits=(mwaits if k == 0 else []),
                              inc=("cT" if k == K - 1 else None))
            tsg = P.op("scalar", "activation", sb["touT_buf"][:, g * 512:(g + 1) * 512],
                       pr[:, 0:512], AF.Sigmoid,
                       waits=[tok_mm] + ([tou_guard] if g == 0 else []) +
                             ([pr_ev[pbuf]] if pr_ev[pbuf] else []),
                       inc="cS")
            pr_ev[pbuf] = tsg
        tsp = P.op("sync", "dma_start", out=sc["rc_sp"][nb * 128:(nb + 1) * 128, :],
                   in_=sb["touT_buf"][:], waits=[P.tok("cS")], inc="qS")
        tou_guard = tsp
        wbuf_w[buf] = tok_mm
    t_rc_done = P.tok("qS")

    NS = 16
    NBH = 4
    KH = 4
    kc_guard = None
    wcv_guard = None
    vst_w = [None, None]
    for e in range(NS):
        for nbh in range(NBH):
            nb_abs = e * NBH + nbh
            buf = nbh % 2
            wcb, tw = load_colblock(w["Wck"], nb_abs, buf)
            for g in range(G4):
                pbuf = g % 2
                pk = ps["B0"] if pbuf == 0 else ps["B1"]
                mwaits = [tw] + xT_tok + ([p2_ev[pbuf]] if p2_ev[pbuf] else [])
                for k in range(K):
                    tok_mm = P.op("tensor", "matmul", pk[:, 0:512],
                                  wcb[:, k * 128:(k + 1) * 128], xsT(k, g * 512, (g + 1) * 512),
                                  start=(k == 0), stop=(k == K - 1),
                                  waits=(mwaits if k == 0 else []),
                                  inc=("cT" if k == K - 1 else None))
                sig = sb["sig0"] if pbuf == 0 else sb["sig1"]
                trl = P.op("scalar", "activation", sig[:], pk[:, 0:512], AF.Relu,
                           waits=[tok_mm] + ([p2_ev[pbuf]] if p2_ev[pbuf] else []), inc="cS")
                tsq = P.op("vector", "tensor_tensor",
                           sb["kcT"][:, nbh * T + g * 512: nbh * T + (g + 1) * 512],
                           sig[:], sig[:], op=ALU.mult,
                           waits=[trl] + ([kc_guard] if (kc_guard and nbh == 0 and g == 0) else []),
                           inc="cV")
                p2_ev[pbuf] = tsq
            wbuf_w[buf] = tok_mm
        t_kc_ready = P.tok("cV")
        twv = P.op("gpsimd", "dma_start", out=sb["wcv_s"][:].rearrange("p (k n) -> p k n", n=2048),
                   in_=w["Wcv"][e * 512:(e + 1) * 512, :].rearrange("(k p) n -> p k n", p=128),
                   waits=([wcv_guard] if wcv_guard else []), inc="qG")
        pv_ev2 = [None, None]
        for i in range(TIL):
            pbuf = i % 2
            pvc = ps["BIGa"] if pbuf == 0 else ps["BIGb"]
            vstage = sb["vst0"] if pbuf == 0 else sb["vst1"]
            tevs = []
            for half in range(2):
                mwaits = [twv, t_kc_ready] + ([pv_ev2[pbuf]] if pv_ev2[pbuf] else [])
                for k2 in range(KH):
                    for nchunk in range(2):
                        nc_abs = half * 2 + nchunk
                        tok_mm = P.op("tensor", "matmul", pvc[:, nchunk * 512:(nchunk + 1) * 512],
                                      sb["kcT"][:, k2 * T + i * 128: k2 * T + (i + 1) * 128],
                                      sb["wcv_s"][:, k2 * 2048 + nc_abs * 512: k2 * 2048 + (nc_abs + 1) * 512],
                                      start=(k2 == 0), stop=(k2 == KH - 1),
                                      waits=(mwaits if (k2 == 0 and nchunk == 0) else []),
                                      inc=("cT" if (k2 == KH - 1 and nchunk == 1) else None))
                tev = P.op("scalar", "activation", vstage[:, half * 1024:(half + 1) * 1024],
                           pvc[:], AF.Identity,
                           waits=[tok_mm] + ([vst_w[pbuf]] if vst_w[pbuf] and half == 0 else []),
                           inc="cS")
                pv_ev2[pbuf] = tev
                tevs.append(tev)
            kw = dict(accum_op=ALU.add) if e > 0 else {}
            tacc = P.op("gpsimd", "dma_start", out=sc["vc_acc"][i * 128:(i + 1) * 128, :],
                        in_=vstage[:], waits=tevs, inc="qG", **kw)
            vst_w[pbuf] = tacc
        kc_guard = tok_mm
        wcv_guard = tok_mm
    t_vc_done = P.tok("qG")

    xw = [None, None]
    rw = [None, None]
    vw = [None, None]
    for i in range(TIL):
        buf = i % 2
        xs = sb["u_f32"] if buf == 0 else sb["x1_f32"]
        rcb = sb["tou0"] if buf == 0 else sb["tou1"]
        vcb = sb["vst0"] if buf == 0 else sb["vst1"]
        tldx = P.op("sync", "dma_start", out=xs[:], in_=sc["xres"][i * 128:(i + 1) * 128, :],
                    waits=([xw[buf]] if xw[buf] else [P.tok("cV"), P.tok("cS")]), inc="qS")
        tldr = P.op("sync", "dma_start_transpose", out=rcb[:],
                    in_=sc["rc_sp"][:, i * 128:(i + 1) * 128],
                    waits=[t_rc_done] + ([rw[buf]] if rw[buf] else []), inc="qS")
        tldv = P.op("sync", "dma_start", out=vcb[:], in_=sc["vc_acc"][i * 128:(i + 1) * 128, :],
                    waits=[t_vc_done] + ([vw[buf]] if vw[buf] else [P.tok("cS")]), inc="qS")
        tm1 = P.op("vector", "tensor_tensor", sb["x2_f32"][:], vcb[:], rcb[:], op=ALU.mult,
                   waits=[tldr, tldv] + ([xw[buf]] if xw[buf] else []), inc="cV")
        tm2 = P.op("vector", "tensor_tensor", sb["x2_f32"][:], xs[:], sb["x2_f32"][:],
                   op=ALU.add, waits=[tldx], inc="cV")
        two = P.op("sync", "dma_start", out=sc["xres"][i * 128:(i + 1) * 128, :],
                   in_=sb["x2_f32"][:], waits=[tm2], inc="qS")
        xw[buf] = tm2
        rw[buf] = tm1
        vw[buf] = tm1
    return P.tok("qS")


def alloc_p0(nc, es, pfx="p0_"):
    sb = {}
    def S(name, shape, dt):
        sb[name] = es.enter_context(nc.sbuf_tensor(pfx + name, shape, dt))
        return sb[name]
    S("m_bcastT", [128, T], F32)
    S("gA", [128, D], BF16); S("bA", [128, D], BF16)
    S("gB", [128, D], BF16); S("bB", [128, D], BF16)
    S("tf_cols", [128, K], F32); S("itf_cols", [128, K], F32)
    S("td_cols", [128, K], F32); S("boT_cols", [128, K], F32)
    S("xin0", [128, D], F32); S("xin1", [128, D], F32)
    S("u_f32", [128, D], F32); S("x1_f32", [128, D], F32); S("x2_f32", [128, D], F32)
    S("xT_g", [128, K * GW], BF16)
    S("decT", [128, 512], F32)
    S("wcb0", [128, K * 128], BF16); S("wcb1", [128, K * 128], BF16)
    S("wrp", [128, K * 128], BF16)
    S("vmixP", [128, NB * G4], F32); S("vmixT", [128, K], F32); S("vmixT_bf", [128, K], BF16)
    S("twoT_cols", [128, NB], F32)
    S("sig0", [128, 512], F32); S("sig1", [128, 512], F32)
    S("touT_buf", [128, T], BF16)
    S("tou0", [128, D], BF16); S("tou1", [128, D], BF16)
    S("kcT", [128, 4 * T], BF16)
    S("wcv_s", [128, 4 * 2048], BF16)
    S("vst0", [128, D], F32); S("vst1", [128, D], F32)
    for tag in ("a", "b"):
        for c in ("s1", "s2", "mn", "nv", "ve", "rs", "nm"):
            S(c + tag, [128, 1], F32)
    return sb


class Chain:
    def __init__(self, P, eng_default="vector"):
        self.P = P
        self.t = None

    def op(self, eng, meth, *args, waits=(), **kw):
        ws = list(waits) + ([self.t] if self.t else [])
        inc = {"vector": "cV", "scalar": "cS", "gpsimd": "cG", "sync": "qS",
               "tensor": "cT"}[eng]
        self.t = self.P.op(eng, meth, *args, waits=ws, inc=inc, **kw)
        return self.t


def build_identity(P, sb):
    t1 = P.op("gpsimd", "iota", sb["ident_f32"][:].bitcast(I32), pattern=[[1, 128]],
              base=0, channel_multiplier=-1, inc="cG")
    t2 = P.op("vector", "tensor_scalar", sb["ident_f32"][:], sb["ident_f32"][:].bitcast(I32),
              0, None, op0=ALU.is_equal, waits=[t1], inc="cV")
    return t2


def emit_p1(P, nc, w, sc, sb, ps, pre_toks):
    EPSO = 1e-5
    ln = LN(P, sb)
    NSW = 2
    SWT = T // NSW
    KH2 = 1408 // 128

    t_id = build_identity(P, sb)
    t_gw = P.op("sync", "dma_start", out=sb["gwraw"][0:16, :], in_=w["gate_w"][:, :], inc="qS")
    gw_toks = []
    for k in range(K):
        tt = P.op("tensor", "transpose", ps["A0"][:, 0:16], sb["gwraw"][0:16, k * 128:(k + 1) * 128],
                  sb["ident_f32"][0:16, 0:16], waits=[t_gw, t_id] + gw_toks[-1:], inc="cT")
        te = P.op("vector", "tensor_copy", sb["gwT"][:, k * 16:(k + 1) * 16], ps["A0"][:, 0:16],
                  waits=[tt], inc="cV")
        gw_toks.append(te)

    xw = [None, None]
    guards_u = None
    sp_rd = None
    for i in range(TIL):
        buf = i % 2
        xs = sb["xin0"] if buf == 0 else sb["xin1"]
        tld = P.op("sync", "dma_start", out=xs[:], in_=sc["xres"][i * 128:(i + 1) * 128, :],
                   waits=([xw[buf]] if xw[buf] else []), inc="qS")
        tpre, _ = ln.full(xs[:], sb["u_f32"][:], xs[:], sb["u_f32"][:], D, EPSO, "a",
                          [tld] + ([guards_u] if guards_u else []) + ([sp_rd] if sp_rd else []))
        tsp = P.op("gpsimd", "dma_start", out=sc["spill_a"][i * 128:(i + 1) * 128, :],
                   in_=sb["u_f32"][:], waits=[tpre], inc="qG")
        sp_rd = tsp
        guards_u = tpre
        xw[buf] = tpre
    t_spill_u_done = P.tok("qG")

    vst_w = [None, None]
    hT_guard = None
    w2_guard = None
    route_guard = None
    wbuf_w = [None, None]
    p_ev = {"A0": None, "A1": None, "Ba": None, "Bb": None}
    for s in range(NSW):
        uT_tok = []
        for k in range(K):
            tt = P.op("sync", "dma_start_transpose",
                      out=sb["uT"][:, k * SWT:(k + 1) * SWT],
                      in_=sc["spill_a"][s * SWT:(s + 1) * SWT, k * 128:(k + 1) * 128],
                      waits=[t_spill_u_done] + ([hT_guard] if hT_guard and k == 0 else []),
                      inc="qS")
            uT_tok.append(tt)
        for ii in range(SWT // 128):
            i_abs = s * (SWT // 128) + ii
            pbuf = ii % 2
            pr = ps["A0"] if pbuf == 0 else ps["A1"]
            prname = "A0" if pbuf == 0 else "A1"
            mwaits = uT_tok + gw_toks + ([p_ev[prname]] if p_ev[prname] else [])
            for k in range(K):
                tok_mm = P.op("tensor", "matmul", pr[:, 0:16],
                              sb["uT"][:, k * SWT + ii * 128: k * SWT + (ii + 1) * 128],
                              sb["gwT"][:, k * 16:(k + 1) * 16],
                              start=(k == 0), stop=(k == K - 1),
                              waits=(mwaits if k == 0 else []),
                              inc=("cT" if k == K - 1 else None))
            ch = Chain(P)
            sco = sb["rsc0"]
            ch.op("scalar", "activation", sco[:], pr[:, 0:16], AF.Sigmoid,
                  waits=[tok_mm] + ([route_guard] if route_guard else []))
            p_ev[prname] = ch.t
            sg = sco[:].rearrange("p (g j) -> p g j", j=4)
            ev = sco[:].rearrange("p (g two) -> p g two", two=2)
            ch.op("vector", "tensor_tensor", sb["r8a"][:].rearrange("p (g one) -> p g one", one=1),
                  ev[:, :, 0:1], ev[:, :, 1:2], op=ALU.max)
            ch.op("vector", "tensor_tensor", sb["r8b"][:].rearrange("p (g one) -> p g one", one=1),
                  ev[:, :, 0:1], ev[:, :, 1:2], op=ALU.min)
            mxp = sb["r8a"][:].rearrange("p (g two) -> p g two", two=2)
            mnp = sb["r8b"][:].rearrange("p (g two) -> p g two", two=2)
            ch.op("vector", "tensor_tensor", sb["r4a"][:].rearrange("p (g one) -> p g one", one=1),
                  mxp[:, :, 0:1], mxp[:, :, 1:2], op=ALU.max)
            ch.op("vector", "tensor_tensor", sb["r4b"][:].rearrange("p (g one) -> p g one", one=1),
                  mxp[:, :, 0:1], mxp[:, :, 1:2], op=ALU.min)
            ch.op("vector", "tensor_tensor", sb["r4c"][:].rearrange("p (g one) -> p g one", one=1),
                  mnp[:, :, 0:1], mnp[:, :, 1:2], op=ALU.max)
            ch.op("vector", "tensor_tensor", sb["r4b"][:], sb["r4b"][:], sb["r4c"][:], op=ALU.max)
            ch.op("vector", "tensor_tensor", sb["r4a"][:], sb["r4a"][:], sb["r4b"][:], op=ALU.add)
            for gi in range(4):
                ch.op("vector", "tensor_scalar", sb["r4b"][:], sb["r4a"][:],
                      sb["r4a"][:, gi:gi + 1], None, op0=ALU.is_gt)
                ch.op("vector", "tensor_reduce", sb["r1a"][:], sb["r4b"][:], axis=AX.X, op=ALU.add)
                if gi > 0:
                    ch.op("vector", "tensor_scalar", sb["r4c"][:, 0:gi], sb["r4a"][:, 0:gi],
                          sb["r4a"][:, gi:gi + 1], None, op0=ALU.is_equal)
                    ch.op("vector", "tensor_reduce", sb["r1b"][:], sb["r4c"][:, 0:gi],
                          axis=AX.X, op=ALU.add)
                    ch.op("vector", "tensor_tensor", sb["r1a"][:], sb["r1a"][:], sb["r1b"][:],
                          op=ALU.add)
                ch.op("vector", "tensor_scalar", sb["gmask"][:, gi:gi + 1], sb["r1a"][:],
                      1.5, None, op0=ALU.is_lt)
            ch.op("vector", "tensor_tensor", sb["mske"][:].rearrange("p (g j) -> p g j", j=4),
                  sg, sb["gmask"][:].unsqueeze(2).broadcast_to([128, 4, 4]), op=ALU.mult)
            for e in range(16):
                ch.op("vector", "tensor_scalar", sb["r16a"][:], sb["mske"][:],
                      sb["mske"][:, e:e + 1], None, op0=ALU.is_gt)
                ch.op("vector", "tensor_reduce", sb["r1a"][:], sb["r16a"][:], axis=AX.X, op=ALU.add)
                if e > 0:
                    ch.op("vector", "tensor_scalar", sb["r16a"][:, 0:e], sb["mske"][:, 0:e],
                          sb["mske"][:, e:e + 1], None, op0=ALU.is_equal)
                    ch.op("vector", "tensor_reduce", sb["r1b"][:], sb["r16a"][:, 0:e],
                          axis=AX.X, op=ALU.add)
                    ch.op("vector", "tensor_tensor", sb["r1a"][:], sb["r1a"][:], sb["r1b"][:],
                          op=ALU.add)
                ch.op("vector", "tensor_scalar", sb["sel"][:, e:e + 1], sb["r1a"][:],
                      1.5, None, op0=ALU.is_lt)
            ch.op("vector", "tensor_tensor", sb["mske"][:], sb["mske"][:], sb["sel"][:], op=ALU.mult)
            ch.op("vector", "tensor_reduce", sb["r1a"][:], sb["mske"][:], axis=AX.X, op=ALU.add)
            ch.op("vector", "tensor_scalar", sb["r1a"][:], sb["r1a"][:], 1e-20, None, op0=ALU.add)
            ch.op("vector", "reciprocal", sb["r1b"][:], sb["r1a"][:])
            ch.op("vector", "tensor_scalar", sb["combine"][:, i_abs * 16:(i_abs + 1) * 16],
                  sb["mske"][:], sb["r1b"][:], None, op0=ALU.mult)
            route_guard = ch.t
        t_routing = P.tok("cV")

        for ei in range(17):
            shared = (ei == 0)
            e = ei - 1
            w1d = w["sw1"][0] if shared else w["w1"][e]
            w2d = w["sw2"][0] if shared else w["w2"][e]
            for nb in range(KH2):
                buf = nb % 2
                wcb = sb["wcb0"] if buf == 0 else sb["wcb1"]
                cbsrc = w1d[:, nb * 128:(nb + 1) * 128].rearrange("(k p) n -> p k n", p=128)
                tw = P.op("gpsimd", "dma_start",
                          out=wcb[:].rearrange("p (k n) -> p k n", n=128), in_=cbsrc,
                          waits=([wbuf_w[buf]] if wbuf_w[buf] else []), inc="qG")
                for g2 in range(SWT // 512):
                    pbuf = g2 % 2
                    pk = ps["B0"] if pbuf == 0 else ps["B1"]
                    pkn = "Ba" if pbuf == 0 else "Bb"
                    mwaits = [tw] + uT_tok + ([p_ev[pkn]] if p_ev[pkn] else []) + \
                             ([hT_guard] if hT_guard and nb == 0 and g2 == 0 else [])
                    for k in range(K):
                        tok_mm = P.op("tensor", "matmul", pk[:, 0:512],
                                      wcb[:, k * 128:(k + 1) * 128],
                                      sb["uT"][:, k * SWT + g2 * 512: k * SWT + (g2 + 1) * 512],
                                      start=(k == 0), stop=(k == K - 1),
                                      waits=(mwaits if k == 0 else []),
                                      inc=("cT" if k == K - 1 else None))
                    tsl = P.op("scalar", "activation",
                               sb["hT"][:, nb * SWT + g2 * 512: nb * SWT + (g2 + 1) * 512],
                               pk[:, 0:512], AF.Silu, waits=[tok_mm], inc="cS")
                    p_ev[pkn] = tsl
                wbuf_w[buf] = tok_mm
            t_hT = P.tok("cS")
            tw2 = P.op("gpsimd", "dma_start",
                       out=sb["w2sb"][:].rearrange("p (k n) -> p k n", n=2048),
                       in_=w2d[:, :].rearrange("(k p) n -> p k n", p=128),
                       waits=([w2_guard] if w2_guard else []), inc="qG")
            for ii in range(SWT // 128):
                i_abs = s * (SWT // 128) + ii
                pbuf = ii % 2
                pvc = ps["BIGa"] if pbuf == 0 else ps["BIGb"]
                vstage = sb["vst0"] if pbuf == 0 else sb["vst1"]
                tevs = []
                for half in range(2):
                    mwaits = [tw2, t_hT] + ([p_ev["G" + str(pbuf)]] if p_ev.get("G" + str(pbuf)) else [])
                    for k2 in range(KH2):
                        for nch in range(2):
                            nc_abs = half * 2 + nch
                            tok_mm = P.op("tensor", "matmul", pvc[:, nch * 512:(nch + 1) * 512],
                                          sb["hT"][:, k2 * SWT + ii * 128: k2 * SWT + (ii + 1) * 128],
                                          sb["w2sb"][:, k2 * 2048 + nc_abs * 512: k2 * 2048 + (nc_abs + 1) * 512],
                                          start=(k2 == 0), stop=(k2 == KH2 - 1),
                                          waits=(mwaits if (k2 == 0 and nch == 0) else []),
                                          inc=("cT" if (k2 == KH2 - 1 and nch == 1) else None))
                    if shared:
                        tev = P.op("scalar", "activation", vstage[:, half * 1024:(half + 1) * 1024],
                                   pvc[:], AF.Identity,
                                   waits=[tok_mm] + ([vst_w[pbuf]] if vst_w[pbuf] and half == 0 else []),
                                   inc="cS")
                    else:
                        tev = P.op("scalar", "activation", vstage[:, half * 1024:(half + 1) * 1024],
                                   pvc[:], AF.Identity,
                                   scale=sb["combine"][:, i_abs * 16 + e: i_abs * 16 + e + 1],
                                   waits=[tok_mm, t_routing] +
                                         ([vst_w[pbuf]] if vst_w[pbuf] and half == 0 else []),
                                   inc="cS")
                    p_ev["G" + str(pbuf)] = tev
                    tevs.append(tev)
                kw = {} if shared else dict(accum_op=ALU.add)
                tacc = P.op("gpsimd", "dma_start",
                            out=sc["moe_acc"][i_abs * 128:(i_abs + 1) * 128, :],
                            in_=vstage[:], waits=tevs, inc="qG", **kw)
                vst_w[pbuf] = tacc
            hT_guard = tok_mm
            w2_guard = tok_mm
    t_acc_done = P.tok("qG")

    xw = [None, None]
    vw = [None, None]
    for i in range(TIL):
        buf = i % 2
        xs = sb["xin0"] if buf == 0 else sb["xin1"]
        vb = sb["vst0"] if buf == 0 else sb["vst1"]
        tldx = P.op("sync", "dma_start", out=xs[:], in_=sc["xres"][i * 128:(i + 1) * 128, :],
                    waits=([xw[buf]] if xw[buf] else [P.tok("cS")]), inc="qS")
        tldv = P.op("sync", "dma_start", out=vb[:], in_=sc["moe_acc"][i * 128:(i + 1) * 128, :],
                    waits=[t_acc_done] + ([vw[buf]] if vw[buf] else [P.tok("cS")]), inc="qS")
        tm = P.op("vector", "tensor_tensor", sb["x2_f32"][:], xs[:], vb[:], op=ALU.add,
                  waits=[tldx, tldv] + ([xw[buf]] if xw[buf] else []), inc="cV")
        two = P.op("sync", "dma_start", out=sc["xres_out"][i * 128:(i + 1) * 128, :],
                   in_=sb["x2_f32"][:], waits=[tm], inc="qS")
        xw[buf] = tm
        vw[buf] = tm
    return P.tok("qS")


def alloc_p1(nc, es, pfx="p1_"):
    sb = {}
    def S(name, shape, dt):
        sb[name] = es.enter_context(nc.sbuf_tensor(pfx + name, shape, dt))
        return sb[name]
    S("ident_f32", [128, 128], F32)
    S("gwraw", [128, 2048], F32)
    S("gwT", [128, 16 * 16], BF16)
    S("xin0", [128, D], F32); S("xin1", [128, D], F32)
    S("u_f32", [128, D], F32); S("x2_f32", [128, D], F32)
    S("uT", [128, 16 * 1024], BF16)
    S("hT", [128, 11 * 1024], BF16)
    S("wcb0", [128, 16 * 128], BF16); S("wcb1", [128, 16 * 128], BF16)
    S("w2sb", [128, 11 * 2048], BF16)
    S("vst0", [128, D], F32); S("vst1", [128, D], F32)
    S("rsc0", [128, 16], F32)
    S("r8a", [128, 8], F32); S("r8b", [128, 8], F32)
    S("r4a", [128, 4], F32); S("r4b", [128, 4], F32); S("r4c", [128, 4], F32)
    S("r1a", [128, 1], F32); S("r1b", [128, 1], F32)
    S("gmask", [128, 4], F32); S("mske", [128, 16], F32); S("r16a", [128, 16], F32)
    S("sel", [128, 16], F32)
    S("combine", [128, 16 * 16], F32)
    for tag in ("a", "b"):
        for c in ("s1", "s2", "mn", "nv", "ve", "rs", "nm"):
            S(c + tag, [128, 1], F32)
    return sb


NH = 64
HD = 64
DST = 128
DI = 4096
CONVD = 4352
DINPROJ = 8512
CH = 128
NCH = T // CH


def emit_p2(P, nc, w, sc, sb, ps, es):
    EPSO = 1e-5
    EPSR = 1e-5
    ln = LN(P, sb)
    t_id = build_identity(P, sb)
    t_idb = P.op("vector", "tensor_copy", sb["ident_bf"][:], sb["ident_f32"][:],
                 waits=[t_id], inc="cV")

    t_Ac = P.op("sync", "dma_start", out=sb["A_col"][0:64, :],
                in_=w["A"][0:64].rearrange("(p one) -> p one", one=1), inc="qS")
    t_db = P.op("sync", "dma_start", out=sb["dtb_col"][0:64, :],
                in_=w["dt_bias"][0:64].rearrange("(p one) -> p one", one=1), inc="qS")
    t_Db = bcast_row(P, w["D"], sb["D_b"], 64)
    t_tr = P.op("gpsimd", "iota", sb["tril"][:].bitcast(I32), pattern=[[1, 128]],
                base=0, channel_multiplier=-1, inc="cG")
    t_tr = P.op("vector", "tensor_scalar", sb["tril"][:], sb["tril"][:].bitcast(I32),
                0, None, op0=ALU.is_lt, waits=[t_tr], inc="cV")
    t_tr = P.op("vector", "tensor_scalar", sb["tril"][:], sb["tril"][:],
                -1e30, None, op0=ALU.mult, waits=[t_tr], inc="cV")
    t_on = P.op("vector", "memset", sb["ones64"][:], 1.0, inc="cV")
    t_o1 = P.op("vector", "memset", sb["onesrow"][:], 1.0, inc="cV")

    with contextlib_ExitStack() as es1:
        s1 = {}
        def S(name, shape, dt):
            s1[name] = es1.enter_context(nc.sbuf_tensor("p2a_" + name, shape, dt))
        S("xin0", [128, D], F32); S("xin1", [128, D], F32)
        S("u_f32", [128, D], F32)
        S("uT", [128, K * T], BF16)
        S("wcb0", [128, K * 128], BF16); S("wcb1", [128, K * 128], BF16)
        S("stg0", [128, 2048], BF16); S("stg1", [128, 2048], BF16)
        S("cv0", [128, GP + T], BF16); S("cv1", [128, GP + T], BF16)
        S("cw4", [128, 34 * 4], F32); S("cb1", [128, 34], F32)
        S("cvc", [128, T], BF16)
        S("cvo0", [128, T], BF16); S("cvo1", [128, T], BF16)

        xw = [None, None]
        gu = None
        sp_rd = None
        for i in range(TIL):
            buf = i % 2
            xs = s1["xin0"] if buf == 0 else s1["xin1"]
            tld = P.op("sync", "dma_start", out=xs[:], in_=sc["xres"][i * 128:(i + 1) * 128, :],
                       waits=([xw[buf]] if xw[buf] else []), inc="qS")
            tpre, _ = ln.full(xs[:], s1["u_f32"][:], xs[:], s1["u_f32"][:], D, EPSO, "a",
                              [tld] + ([gu] if gu else []) + ([sp_rd] if sp_rd else []))
            tsp = P.op("gpsimd", "dma_start", out=sc["spill_a"][i * 128:(i + 1) * 128, :],
                       in_=s1["u_f32"][:], waits=[tpre], inc="qG")
            sp_rd = tsp
            gu = tpre
            xw[buf] = tpre
        t_sp_u = P.tok("qG")

        uT_tok = []
        for k in range(K):
            uT_tok.append(P.op("sync", "dma_start_transpose",
                               out=s1["uT"][:, k * T:(k + 1) * T],
                               in_=sc["spill_a"][:, k * 128:(k + 1) * 128],
                               waits=[t_sp_u], inc="qS"))

        wbuf_w = [None, None]
        p_ev = [None, None]
        stg_rd = [None, None]
        for nb in range(66):
            buf = nb % 2
            wcb = s1["wcb0"] if buf == 0 else s1["wcb1"]
            tw = P.op("gpsimd", "dma_start",
                      out=wcb[:].rearrange("p (k n) -> p k n", n=128),
                      in_=w["W_in"][:, nb * 128:(nb + 1) * 128].rearrange("(k p) n -> p k n", p=128),
                      waits=([wbuf_w[buf]] if wbuf_w[buf] else []), inc="qG")
            stg = s1["stg0"] if buf == 0 else s1["stg1"]
            evs = []
            for g in range(G4):
                pbuf = g % 2
                pk = ps["A0"] if pbuf == 0 else ps["A1"]
                mwaits = [tw] + uT_tok + ([p_ev[pbuf]] if p_ev[pbuf] else []) + \
                         ([stg_rd[buf]] if stg_rd[buf] and g == 0 else [])
                for k in range(K):
                    tok_mm = P.op("tensor", "matmul", pk[:, 0:512],
                                  wcb[:, k * 128:(k + 1) * 128],
                                  s1["uT"][:, k * T + g * 512: k * T + (g + 1) * 512],
                                  start=(k == 0), stop=(k == K - 1),
                                  waits=(mwaits if k == 0 else []),
                                  inc=("cT" if k == K - 1 else None))
                tev = P.op("scalar", "activation", stg[:, g * 512:(g + 1) * 512],
                           pk[:, 0:512], AF.Identity, waits=[tok_mm], inc="cS")
                p_ev[pbuf] = tev
                evs.append(tev)
            dstt = sc["zT_sp"] if nb < 32 else sc["xbcT_sp"]
            roff = nb * 128 if nb < 32 else (nb - 32) * 128
            tsp = P.op("sync", "dma_start", out=dstt[roff:roff + 128, :], in_=stg[:],
                       waits=evs, inc="qS")
            stg_rd[buf] = tsp
            wbuf_w[buf] = tok_mm
        tw = P.op("gpsimd", "dma_start",
                  out=s1["wcb0"][:, 0:1024].rearrange("p (k n) -> p k n", n=64),
                  in_=w["W_in"][:, 8448:8512].rearrange("(k p) n -> p k n", p=128),
                  waits=([wbuf_w[0]] if wbuf_w[0] else []), inc="qG")
        for g in range(G4):
            pbuf = g % 2
            pk = ps["A0"] if pbuf == 0 else ps["A1"]
            mwaits = [tw] + uT_tok + ([p_ev[pbuf]] if p_ev[pbuf] else [])
            for k in range(K):
                tok_mm = P.op("tensor", "matmul", pk[0:64, 0:512],
                              s1["wcb0"][:, k * 64:(k + 1) * 64],
                              s1["uT"][:, k * T + g * 512: k * T + (g + 1) * 512],
                              start=(k == 0), stop=(k == K - 1),
                              waits=(mwaits if k == 0 else []),
                              inc=("cT" if k == K - 1 else None))
            tev = P.op("scalar", "activation", sb["dtT_raw"][0:64, g * 512:(g + 1) * 512],
                       pk[0:64, 0:512], AF.Identity, waits=[tok_mm], inc="cS")
            p_ev[pbuf] = tev
        t_dt_raw = P.tok("cS")
        t_zxbc_sp = P.tok("qS")

        t_cwall = P.op("sync", "dma_start", out=s1["cw4"][:].rearrange("p (r k) -> p r k", k=4),
                       in_=w["conv_w"][0:4352, :].rearrange("(r p) k -> p r k", p=128), inc="qS")
        t_cball = P.op("sync", "dma_start", out=s1["cb1"][:],
                       in_=w["conv_b"][0:4352].rearrange("(r p) -> p r", p=128), inc="qS",
                       allow_slow_non_contiguous=True)
        cvw = [None, None]
        cvw2 = [None, None]
        cvo_rd = None
        ch_dve = None
        prev_tsl = None
        for r in range(34):
            buf = r % 2
            xg = s1["cv0"] if buf == 0 else s1["cv1"]
            tzz = P.op("vector", "memset", xg[:, 0:GP], 0.0,
                       waits=([cvw[buf]] if cvw[buf] else []), inc="cV")
            tld = P.op("sync", "dma_start", out=xg[:, GP:GP + T],
                       in_=sc["xbcT_sp"][r * 128:(r + 1) * 128, :],
                       waits=[t_zxbc_sp, tzz] + ([cvw[buf]] if cvw[buf] else []), inc="qS")
            ch = Chain(P)
            ch.op("vector", "tensor_scalar", s1["cvc"][:], xg[:, GP - 3:GP - 3 + T],
                  s1["cw4"][:, r * 4:r * 4 + 1], None, op0=ALU.mult,
                  waits=[tzz, tld, t_cwall] + ([cvo_rd] if cvo_rd else []))
            for kk in range(1, 4):
                ch.op("vector", "scalar_tensor_tensor", s1["cvc"][:],
                      xg[:, GP - 3 + kk:GP - 3 + kk + T], s1["cw4"][:, r * 4 + kk:r * 4 + kk + 1],
                      s1["cvc"][:], op0=ALU.mult, op1=ALU.add)
            ch_dve = ch.t
            cvo = s1["cvo0"] if buf == 0 else s1["cvo1"]
            tsl = P.op("scalar", "activation", cvo[:], s1["cvc"][:], AF.Silu,
                       bias=s1["cb1"][:, r:r + 1],
                       waits=[ch_dve, t_cball] + ([cvw2[buf]] if cvw2[buf] else []), inc="cS")
            prev_tsl = tsl
            if r < 32:
                tso = P.op("sync", "dma_start", out=sc["xbcP_sp"][r * 128:(r + 1) * 128, :],
                           in_=cvo[:], waits=[tsl], inc="qS")
            elif r == 32:
                tso = P.op("vector", "tensor_copy", sb["BmT"][:], cvo[:],
                           waits=[tsl], inc="cV")
            else:
                tso = P.op("vector", "tensor_copy", sb["CmT"][:], cvo[:],
                           waits=[tsl], inc="cV")
            cvw[buf] = ch_dve
            cvw2[buf] = tso
            cvo_rd = tso
        t_conv = [P.tok("qS"), P.tok("cV"), P.tok("cS")]

        ch = Chain(P)
        ch.op("vector", "tensor_scalar", sb["dtT"][0:64, :], sb["dtT_raw"][0:64, :],
              sb["dtb_col"][0:64, :], None, op0=ALU.add, waits=[t_dt_raw, t_db])
        t_ab = P.op("scalar", "activation", sb["dAT"][0:64, :], sb["dtT"][0:64, :],
                    AF.Abs, waits=[ch.t], inc="cS")
        t_e = P.op("scalar", "activation", sb["dAT"][0:64, :], sb["dAT"][0:64, :],
                   AF.Exp, scale=-1.0, waits=[t_ab], inc="cS")
        t_l = P.op("scalar", "activation", sb["dAT"][0:64, :], sb["dAT"][0:64, :],
                   AF.Ln, bias=1.0, waits=[t_e], inc="cS")
        ch.op("vector", "tensor_scalar", sb["dtT"][0:64, :], sb["dtT"][0:64, :],
              0.0, None, op0=ALU.max, waits=[t_l])
        ch.op("vector", "tensor_tensor", sb["dtT"][0:64, :], sb["dtT"][0:64, :],
              sb["dAT"][0:64, :], op=ALU.add)
        ch.op("vector", "tensor_scalar", sb["dAT"][0:64, :], sb["dtT"][0:64, :],
              sb["A_col"][0:64, :], None, op0=ALU.mult, waits=[t_Ac])
        t_dA = ch.t

    barrier_all(P)
    with contextlib_ExitStack() as es2:
        s2 = {}
        def S2(name, shape, dt):
            s2[name] = es2.enter_context(nc.sbuf_tensor("p2b_" + name, shape, dt))
        S2("x_ch0", [128, DI], BF16); S2("x_ch1", [128, DI], BF16)
        S2("z_ch0", [128, DI], BF16); S2("z_ch1", [128, DI], BF16)
        S2("dtx", [128, DI], BF16); S2("ddtx", [128, DI], BF16)
        S2("Y_sb", [128, DI], F32)
        S2("state", [128, DI], F32); S2("state_bf", [128, DI], BF16)
        S2("Lb0", [128, 1024], F32); S2("Lb1", [128, 1024], F32)
        S2("Mp0", [128, 1024], BF16); S2("Mp1", [128, 1024], BF16)
        S2("S_sb", [128, 128], BF16)
        S2("Bm_tm", [128, 128], BF16)
        S2("Bm_f", [128, 128], F32)
        S2("csT_c", [128, 128], F32)
        S2("decT_c", [128, 128], F32)
        S2("dt_tm", [128, 64], F32)
        S2("cs_tm", [128, 64], F32)
        S2("negcs_tm", [128, 64], F32)
        S2("expcs_tm", [128, 64], F32)
        S2("dec_tm", [128, 64], F32)
        S2("cd_b", [128, 64], F32)
        S2("cdrow", [128, 64], F32)
        S2("junk4", [128, DI], BF16)
        S2("ssq", [128, 1], F32); S2("rs_c", [128, 1], F32)
        t_z0 = P.op("vector", "memset", s2["state"][:], 0.0, inc="cV")
        t_z1 = P.op("vector", "memset", s2["state_bf"][:], 0.0, inc="cV")
        xz_w = [None, None, None, None]
        yg_rd = None
        pv = {}
        for c in range(NCH):
            cb = c % 2
            x_ch = s2["x_ch0"] if cb == 0 else s2["x_ch1"]
            z_ch = s2["z_ch0"] if cb == 0 else s2["z_ch1"]
            tldx = P.op("sync", "dma_start_transpose", out=x_ch[:],
                        in_=sc["xbcP_sp"][0:DI, c * CH:(c + 1) * CH],
                        waits=t_conv + ([xz_w[cb]] if xz_w[cb] else []), inc="qS")
            tldz = P.op("sync", "dma_start_transpose", out=z_ch[:],
                        in_=sc["zT_sp"][0:DI, c * CH:(c + 1) * CH],
                        waits=[t_zxbc_sp] + ([xz_w[2 + cb]] if xz_w[2 + cb] else []), inc="qS")
            t_cs = P.op("vector", "tensor_tensor_scan", s2["csT_c"][0:64, :],
                        sb["ones64"][0:64, :], sb["dAT"][0:64, c * CH:(c + 1) * CH],
                        0.0, op0=ALU.mult, op1=ALU.add,
                        waits=[t_dA, t_on] + pv.get("cs_rd", []), inc="cV")
            t_dc = P.op("vector", "tensor_scalar", s2["decT_c"][0:64, :], s2["csT_c"][0:64, :],
                        s2["csT_c"][0:64, CH - 1:CH], -1.0, op0=ALU.subtract, op1=ALU.mult,
                        waits=[t_cs] + pv.get("dec_rd", []), inc="cV")
            t_dec = P.op("scalar", "activation", s2["decT_c"][0:64, :], s2["decT_c"][0:64, :],
                         AF.Exp, waits=[t_dc], inc="cS")
            t_csw = P.op("sync", "dma_start",
                         out=sc["csrow_sp"][c, :].rearrange("(p f) -> p f", p=64),
                         in_=s2["csT_c"][0:64, :], waits=[t_cs], inc="qS")
            trg = pv.get("pC0", pv.get("trB0"))
            trw = []
            for (src, dst, wtok) in ((sb["dtT"][0:64, c * CH:(c + 1) * CH], s2["dt_tm"], [t_dA]),
                                     (s2["csT_c"][0:64, :], s2["cs_tm"], [t_cs]),
                                     (s2["decT_c"][0:64, :], s2["dec_tm"], [t_dec])):
                tt = P.op("tensor", "transpose", ps["B0"][:, 0:64], src,
                          sb["ident_f32"][0:64, 0:64],
                          waits=wtok + [t_id] + ([pv.get("trB0")] if pv.get("trB0") else []) +
                                ([trg] if trg else []),
                          inc="cT")
                trg = None
                ew = []
                if dst is s2["cs_tm"] and pv.get("cste_g"):
                    ew = [pv["cste_g"]]
                te = P.op("vector", "tensor_copy", dst[:], ps["B0"][:, 0:64],
                          waits=[tt] + ew, inc="cV")
                pv["trB0"] = te
                trw.append(te)
            t_dttm, t_cstm, t_dectm = trw
            t_ecs = P.op("scalar", "activation", s2["expcs_tm"][:], s2["cs_tm"][:], AF.Exp,
                         waits=[t_cstm] + pv.get("ecs_rd", []), inc="cS")
            pv["cste_g"] = t_ecs
            t_ncs = P.op("vector", "tensor_scalar", s2["negcs_tm"][:], s2["cs_tm"][:],
                         -1.0, None, op0=ALU.mult,
                         waits=[t_cstm] + pv.get("ncs_rd", []), inc="cV")
            tt = P.op("tensor", "transpose", ps["B1"][0:1, 0:64],
                      s2["csT_c"][0:64, CH - 1:CH], sb["ident_f32"][0:64, 0:64],
                      waits=[t_cs] + ([pv.get("trB1")] if pv.get("trB1") else []) +
                            ([pv.get("pC1")] if pv.get("pC1") else []),
                      inc="cT")
            t_cdr = P.op("scalar", "activation", s2["cdrow"][0:1, 0:64], ps["B1"][0:1, 0:64],
                         AF.Exp, waits=[tt] + pv.get("cdrow_rd", []), inc="cS")
            tt2 = P.op("tensor", "matmul", ps["B1"][:, 64:128],
                       sb["onesrow"][0:1, :], s2["cdrow"][0:1, 0:64],
                       start=True, stop=True,
                       waits=[t_cdr, t_o1] + ([pv.get("cdb_ev")] if pv.get("cdb_ev") else []),
                       inc="cT")
            pv["cdrow_rd"] = [tt2]
            t_cdb = P.op("vector", "tensor_copy", s2["cd_b"][:], ps["B1"][:, 64:128],
                         waits=[tt2] + pv.get("cdb_rd", []), inc="cV")
            pv["cdb_ev"] = t_cdb
            pv["trB1"] = t_cdb
            tmmS = P.op("tensor", "matmul", ps["A0"][:, 0:128],
                        sb["BmT"][:, c * CH:(c + 1) * CH], sb["CmT"][:, c * CH:(c + 1) * CH],
                        start=True, stop=True,
                        waits=t_conv + ([pv.get("S_ev")] if pv.get("S_ev") else []), inc="cT")
            t_S = P.op("vector", "tensor_copy", s2["S_sb"][:], ps["A0"][:, 0:128],
                       waits=[tmmS] + pv.get("S_rd", []), inc="cV")
            pv["S_ev"] = t_S
            t_bf = P.op("vector", "tensor_copy", s2["Bm_f"][:], sb["BmT"][:, c * CH:(c + 1) * CH],
                        waits=t_conv + pv.get("Bmf_rd", []), inc="cV")
            tmmB = P.op("tensor", "transpose", ps["A1"][:, 0:128], s2["Bm_f"][:],
                        sb["ident_f32"][:],
                        waits=[t_bf] + ([pv.get("Bm_ev")] if pv.get("Bm_ev") else []), inc="cT")
            pv["Bmf_rd"] = [tmmB]
            t_Bm = P.op("vector", "tensor_copy", s2["Bm_tm"][:], ps["A1"][:, 0:128],
                        waits=[tmmB] + pv.get("Bm_rd", []), inc="cV")
            pv["Bm_ev"] = t_Bm
            t_dtx = P.op("vector", "tensor_tensor",
                         s2["dtx"][:].rearrange("p (h q) -> p h q", q=HD),
                         x_ch[:].rearrange("p (h q) -> p h q", q=HD),
                         s2["dt_tm"][:].unsqueeze(2).broadcast_to([128, NH, HD]),
                         op=ALU.mult, waits=[tldx, t_dttm] + pv.get("dtx_rd", []), inc="cV")
            t_ddtx = P.op("vector", "tensor_tensor",
                          s2["ddtx"][:].rearrange("p (h q) -> p h q", q=HD),
                          s2["dtx"][:].rearrange("p (h q) -> p h q", q=HD),
                          s2["dec_tm"][:].unsqueeze(2).broadcast_to([128, NH, HD]),
                          op=ALU.mult, waits=[t_dtx, t_dectm] + pv.get("ddtx_rd", []), inc="cV")
            yg_evs = []
            last_texp = None
            for g8 in range(8):
                gb = g8 % 2
                Lb = s2["Lb0"] if gb == 0 else s2["Lb1"]
                Mp = s2["Mp0"] if gb == 0 else s2["Mp1"]
                tbc = P.op("gpsimd", "dma_start", out=Lb[:],
                           in_=sc["csrow_sp"][c, g8 * 1024:(g8 + 1) * 1024].partition_broadcast(128),
                           waits=[t_csw] + ([pv.get("Lb" + str(gb))] if pv.get("Lb" + str(gb)) else []),
                           inc="qG")
                t_msk = P.op("vector", "tensor_tensor",
                             Lb[:].rearrange("p (h i) -> p h i", i=CH),
                             Lb[:].rearrange("p (h i) -> p h i", i=CH),
                             sb["tril"][:].unsqueeze(1).broadcast_to([128, 8, CH]),
                             op=ALU.add, waits=[tbc, t_tr], inc="cV")
                exps = []
                for h8 in range(8):
                    h = g8 * 8 + h8
                    texp = P.op("scalar", "activation", Mp[:, h8 * CH:(h8 + 1) * CH],
                                Lb[:, h8 * CH:(h8 + 1) * CH], AF.Exp,
                                bias=s2["negcs_tm"][:, h:h + 1],
                                waits=[t_msk, t_ncs] +
                                      ([pv.get("Mp" + str(gb))] if pv.get("Mp" + str(gb)) and h8 == 0 else []),
                                inc="cS")
                    exps.append(texp)
                last_texp = exps[-1]
                t_mp = P.op("vector", "tensor_tensor",
                            Mp[:].rearrange("p (h i) -> p h i", i=CH),
                            Mp[:].rearrange("p (h i) -> p h i", i=CH),
                            s2["S_sb"][:].unsqueeze(1).broadcast_to([128, 8, CH]),
                            op=ALU.mult, waits=exps + [t_S], inc="cV")
                pv["Lb" + str(gb)] = t_mp
                pA = ps["BIGa"] if gb == 0 else ps["BIGb"]
                mwA = [t_mp, t_dtx] + ([pv.get("pA" + str(gb))] if pv.get("pA" + str(gb)) else [])
                for h8 in range(8):
                    h = g8 * 8 + h8
                    P.op("tensor", "matmul", pA[:, h8 * HD:(h8 + 1) * HD],
                         Mp[:, h8 * CH:(h8 + 1) * CH],
                         s2["dtx"][:, h * HD:(h + 1) * HD],
                         start=(h8 == 0), stop=False,
                         waits=(mwA if h8 == 0 else []), skip_group_check=True)
                for h8 in range(8):
                    h = g8 * 8 + h8
                    tmmA = P.op("tensor", "matmul", pA[:, 512 + h8 * HD: 512 + (h8 + 1) * HD],
                                sb["CmT"][:, c * CH:(c + 1) * CH],
                                s2["state_bf"][:, h * HD:(h + 1) * HD],
                                start=(h8 == 0), stop=(h8 == 7), skip_group_check=True,
                                waits=(([t_z1] if c == 0 else [pv["sbf"]]) if h8 == 0 else []),
                                inc=("cT" if h8 == 7 else None))
                pv["Mp" + str(gb)] = tmmA
                t_y1 = P.op("vector", "tensor_tensor",
                            s2["Y_sb"][:, g8 * 512:(g8 + 1) * 512].rearrange("p (h q) -> p h q", q=HD),
                            pA[:, 512:1024].rearrange("p (h q) -> p h q", q=HD),
                            s2["expcs_tm"][:, g8 * 8:(g8 + 1) * 8].unsqueeze(2).broadcast_to([128, 8, HD]),
                            op=ALU.mult, waits=[tmmA, t_ecs], inc="cV")
                t_y2 = P.op("vector", "tensor_tensor", s2["Y_sb"][:, g8 * 512:(g8 + 1) * 512],
                            s2["Y_sb"][:, g8 * 512:(g8 + 1) * 512], pA[:, 0:512],
                            op=ALU.add, waits=[t_y1], inc="cV")
                pv["pA" + str(gb)] = t_y2
                yg_evs.append(t_y2)
            pv["ecs_rd"] = [yg_evs[-1]]
            pv["ncs_rd"] = [last_texp]
            st_upds = []
            for sk in range(8):
                skb = sk % 2
                pC = ps["B0"] if skb == 0 else ps["B1"]
                extra = [pv.get("pC" + str(skb))] if pv.get("pC" + str(skb)) else []
                if skb == 0:
                    extra += [pv["trB0"]]
                else:
                    extra += [t_cdb, t_cdr]
                tmmC = P.op("tensor", "matmul", pC[:, 0:512],
                            s2["Bm_tm"][:], s2["ddtx"][:, sk * 512:(sk + 1) * 512],
                            start=True, stop=True,
                            waits=[t_Bm, t_ddtx] + extra, inc="cT")
                t_su1 = P.op("vector", "tensor_tensor",
                             s2["state"][:, sk * 512:(sk + 1) * 512].rearrange("p (h q) -> p h q", q=HD),
                             s2["state"][:, sk * 512:(sk + 1) * 512].rearrange("p (h q) -> p h q", q=HD),
                             s2["cd_b"][:, sk * 8:(sk + 1) * 8].unsqueeze(2).broadcast_to([128, 8, HD]),
                             op=ALU.mult,
                             waits=[t_cdb, tmmC, pv["Mp0"], pv["Mp1"]] + ([t_z0] if c == 0 else []),
                             inc="cV")
                t_su2 = P.op("vector", "tensor_tensor", s2["state"][:, sk * 512:(sk + 1) * 512],
                             s2["state"][:, sk * 512:(sk + 1) * 512], pC[:, 0:512],
                             op=ALU.add, waits=[t_su1], inc="cV")
                pv["pC" + str(skb)] = t_su2
                st_upds.append(t_su2)
            t_sbf = P.op("vector", "tensor_copy", s2["state_bf"][:], s2["state"][:],
                         waits=st_upds + [pv["Mp0"], pv["Mp1"]], inc="cV")
            pv["sbf"] = t_sbf
            pv["cs_rd"] = [t_sbf, t_csw, t_cdb]
            pv["dec_rd"] = [t_dectm]
            pv["dtx_rd"] = [t_sbf]
            pv["ddtx_rd"] = [t_sbf]
            pv["cdb_rd"] = [t_sbf]
            pv["Bm_rd"] = [t_sbf]
            pv["S_rd"] = [pv["Lb0"], pv["Lb1"]]
            ch = Chain(P)
            ch.op("vector", "tensor_tensor",
                  s2["junk4"][:].rearrange("p (h q) -> p h q", q=HD),
                  x_ch[:].rearrange("p (h q) -> p h q", q=HD),
                  sb["D_b"][:, 0:64].unsqueeze(2).broadcast_to([128, NH, HD]),
                  op=ALU.mult, waits=[t_Db, tldx] + yg_evs + ([yg_rd] if yg_rd else []))
            ch.op("vector", "tensor_tensor", s2["Y_sb"][:], s2["Y_sb"][:], s2["junk4"][:],
                  op=ALU.add)
            t_slz = P.op("scalar", "activation", s2["junk4"][:], z_ch[:], AF.Silu,
                         waits=[tldz, ch.t], inc="cS")
            ch.op("vector", "tensor_tensor", s2["Y_sb"][:], s2["Y_sb"][:], s2["junk4"][:],
                  op=ALU.mult, waits=[t_slz])
            t_yg = ch.t
            xz_w[cb] = t_yg
            xz_w[2 + cb] = t_slz
            t_sq = P.op("scalar", "activation", s2["junk4"][:], s2["Y_sb"][:], AF.Square,
                        accum_out=s2["ssq"][:], waits=[t_yg], inc="cS")
            t_ms = P.op("vector", "tensor_scalar", s2["ssq"][:], s2["ssq"][:], 1.0 / DI, EPSR,
                        op0=ALU.mult, op1=ALU.add, waits=[t_sq], inc="cV")
            t_ln2 = P.op("scalar", "activation", s2["ssq"][:], s2["ssq"][:], AF.Ln,
                         waits=[t_ms], inc="cS")
            t_rs = P.op("scalar", "activation", s2["rs_c"][:], s2["ssq"][:], AF.Exp,
                        scale=-0.5, waits=[t_ln2] + pv.get("rs_rd", []), inc="cS")
            t_ygs = P.op("vector", "tensor_scalar", s2["junk4"][:], s2["Y_sb"][:],
                         s2["rs_c"][:], None, op0=ALU.mult, waits=[t_rs, t_sq], inc="cV")
            pv["rs_rd"] = [t_ygs]
            t_ygw = P.op("sync", "dma_start", out=sc["yg_sp"][c * CH:(c + 1) * CH, :],
                         in_=s2["junk4"][:], waits=[t_ygs], inc="qS")
            yg_rd = t_ygw
        if "dbg2" in sc:
            tv = P.tok("cV"); tsx = P.tok("cS"); ttx = P.tok("cT")
            P.op("gpsimd", "dma_start", out=sc["dbg2"][:, 0:1024], in_=s2["Mp1"][:],
                 waits=[tv, tsx, ttx], inc="qG")
            P.op("gpsimd", "dma_start", out=sc["dbg2"][:, 1024:5120], in_=s2["dtx"][:],
                 waits=[tv, tsx, ttx], inc="qG")
            P.op("gpsimd", "dma_start", out=sc["dbg2"][:, 5120:9216], in_=s2["Y_sb"][:],
                 waits=[tv, tsx, ttx], inc="qG")
            P.op("gpsimd", "dma_start", out=sc["dbg2"][:, 9216:9280], in_=s2["negcs_tm"][:],
                 waits=[tv, tsx, ttx], inc="qG")
            P.op("gpsimd", "dma_start", out=sc["dbg2"][:, 9280:9344], in_=s2["expcs_tm"][:],
                 waits=[tv, tsx, ttx], inc="qG")
            P.op("gpsimd", "dma_start", out=sc["dbg2"][:, 9344:10368], in_=s2["Lb1"][:],
                 waits=[tv, tsx, ttx], inc="qG")
            P.wait("gpsimd", P.tok("qG"))
        t_yg_done = P.tok("qS")

    barrier_all(P)
    with contextlib_ExitStack() as es3:
        s3 = {}
        def S3(name, shape, dt):
            s3[name] = es3.enter_context(nc.sbuf_tensor("p2c_" + name, shape, dt))
        S3("ygT", [128, 32 * 512], BF16)
        S3("wocb0", [128, 32 * 128], BF16); S3("wocb1", [128, 32 * 128], BF16)
        S3("nwcol", [128, 32], F32)
        S3("ostg0", [128, 512], BF16); S3("ostg1", [128, 512], BF16)
        S3("xin0", [128, D], F32); S3("xin1", [128, D], F32)
        S3("o0", [128, D], BF16); S3("o1", [128, D], BF16)
        S3("x2_f32", [128, D], F32)
        t_nw = load_cols(P, w["norm_w"], s3["nwcol"], 32)
        wocb_w = [None, None]
        p_ev2 = [None, None]
        ostg_rd = [None, None]
        ygT_rd = None
        for q in range(4):
            ygT_tok = []
            for k2 in range(32):
                tt = P.op("sync", "dma_start_transpose",
                          out=s3["ygT"][:, k2 * 512:(k2 + 1) * 512],
                          in_=sc["yg_sp"][q * 512:(q + 1) * 512, k2 * 128:(k2 + 1) * 128],
                          waits=[t_yg_done] + ([ygT_rd] if ygT_rd and k2 == 0 else []),
                          inc="qS")
                ygT_tok.append(tt)
            for nb in range(16):
                buf = nb % 2
                wocb = s3["wocb0"] if buf == 0 else s3["wocb1"]
                tw = P.op("gpsimd", "dma_start",
                          out=wocb[:].rearrange("p (k n) -> p k n", n=128),
                          in_=w["W_out"][:, nb * 128:(nb + 1) * 128].rearrange("(k p) n -> p k n", p=128),
                          waits=([wocb_w[buf]] if wocb_w[buf] else []), inc="qG")
                tnw = None
                for k2 in range(32):
                    tnw = P.op("vector", "tensor_scalar", wocb[:, k2 * 128:(k2 + 1) * 128],
                               wocb[:, k2 * 128:(k2 + 1) * 128], s3["nwcol"][:, k2:k2 + 1], None,
                               op0=ALU.mult,
                               waits=(([tw] + t_nw) if k2 == 0 else []),
                               inc=("cV" if k2 == 31 else None))
                pbuf = nb % 2
                pk = ps["A0"] if pbuf == 0 else ps["A1"]
                mwaits = [tnw] + ygT_tok + ([p_ev2[pbuf]] if p_ev2[pbuf] else [])
                for k2 in range(32):
                    tok_mm = P.op("tensor", "matmul", pk[:, 0:512],
                                  wocb[:, k2 * 128:(k2 + 1) * 128],
                                  s3["ygT"][:, k2 * 512:(k2 + 1) * 512],
                                  start=(k2 == 0), stop=(k2 == 31),
                                  waits=(mwaits if k2 == 0 else []),
                                  inc=("cT" if k2 == 31 else None))
                stg = s3["ostg0"] if buf == 0 else s3["ostg1"]
                tev = P.op("scalar", "activation", stg[:], pk[:, 0:512],
                           AF.Identity, waits=[tok_mm] + ([ostg_rd[buf]] if ostg_rd[buf] else []),
                           inc="cS")
                p_ev2[pbuf] = tev
                wocb_w[buf] = tok_mm
                tso = P.op("sync", "dma_start",
                           out=sc["attnT_sp"][nb * 128:(nb + 1) * 128, q * 512:(q + 1) * 512],
                           in_=stg[:], waits=[tev], inc="qS")
                ostg_rd[buf] = tso
            ygT_rd = tok_mm
        t_attn_sp = P.tok("qS")
        xw = [None, None]
        ow = [None, None]
        for i in range(TIL):
            buf = i % 2
            xs = s3["xin0"] if buf == 0 else s3["xin1"]
            ob = s3["o0"] if buf == 0 else s3["o1"]
            tldx = P.op("sync", "dma_start", out=xs[:], in_=sc["xres"][i * 128:(i + 1) * 128, :],
                        waits=([xw[buf]] if xw[buf] else []), inc="qS")
            tldo = P.op("sync", "dma_start_transpose", out=ob[:],
                        in_=sc["attnT_sp"][:, i * 128:(i + 1) * 128],
                        waits=[t_attn_sp] + ([ow[buf]] if ow[buf] else []), inc="qS")
            tm = P.op("vector", "tensor_tensor", s3["x2_f32"][:], xs[:], ob[:], op=ALU.add,
                      waits=[tldx, tldo] + ([xw[buf]] if xw[buf] else []), inc="cV")
            two = P.op("sync", "dma_start", out=sc["xres_out"][i * 128:(i + 1) * 128, :],
                       in_=s3["x2_f32"][:], waits=[tm], inc="qS")
            xw[buf] = tm
            ow[buf] = tm
    return P.tok("qS")


import contextlib as _ctl
def contextlib_ExitStack():
    return _ctl.ExitStack()


def barrier_all(P):
    toks = [P.tok(s) for s in P.semnames if P.counts[s] > 0]
    for e in ENGINES:
        P.wait(e, *toks)


def alloc_p2_persist(nc, es, pfx="p2_"):
    sb = {}
    def S(name, shape, dt):
        sb[name] = es.enter_context(nc.sbuf_tensor(pfx + name, shape, dt))
    S("ident_f32", [128, 128], F32)
    S("ident_bf", [128, 128], BF16)
    S("A_col", [128, 1], F32); S("dtb_col", [128, 1], F32)
    S("D_b", [128, 64], F32)
    S("tril", [128, 128], F32)
    S("ones64", [128, 128], F32)
    S("onesrow", [128, 128], F32)
    S("dtT_raw", [128, T], F32)
    S("dtT", [128, T], F32)
    S("dAT", [128, T], F32)
    S("BmT", [128, T], BF16)
    S("CmT", [128, T], BF16)
    for tag in ("a",):
        for c in ("s1", "s2", "mn", "nv", "ve", "rs", "nm"):
            S(c + tag, [128, 1], F32)
    return sb


def emit_p3(P, nc, w, sc, sb, ps, es):
    EPSO = 1e-5
    EPSR = float(np.finfo(np.float32).eps)
    THR = 0.5 + 2.0 ** -10
    ln = LN(P, sb)
    t_id = build_identity(P, sb)

    with contextlib_ExitStack() as esA:
        sA = {}
        def S(name, shape, dt):
            sA[name] = esA.enter_context(nc.sbuf_tensor("p3a_" + name, shape, dt))
        S("xin0", [128, D], F32); S("xin1", [128, D], F32)
        S("u_f32", [128, D], F32)
        S("uT", [128, K * T], BF16)
        S("wraw0", [128, 2048], F32); S("wraw1", [128, 2048], F32)
        S("wbb", [128, 2048], BF16)
        S("wq", [128, 2048], F32)
        S("w1t", [128, K * 128], BF16)
        S("gsum", [128, 32], F32); S("gbf", [128, 32], BF16)
        S("scl", [128, 32], F32); S("thr", [128, 32], F32); S("nthr", [128, 32], F32)
        S("qp", [128, 2048], BF16); S("qn", [128, 2048], BF16)
        S("rl", [128, 512], BF16); S("sqv", [128, 512], BF16)
        S("hstg", [128, T], BF16)
        S("onec", [128, 1], BF16)
        S("rsrow", [128, 2048], F32)
        S("rs_cols", [128, 16], F32)
        t_one = P.op("vector", "memset", sA["onec"][:], 1.0, inc="cV")

        xw = [None, None]
        gu = None
        sp_rd = None
        for i in range(TIL):
            buf = i % 2
            xs = sA["xin0"] if buf == 0 else sA["xin1"]
            tld = P.op("sync", "dma_start", out=xs[:], in_=sc["xres"][i * 128:(i + 1) * 128, :],
                       waits=([xw[buf]] if xw[buf] else []), inc="qS")
            tpre, _ = ln.full(xs[:], sA["u_f32"][:], xs[:], sA["u_f32"][:], D, EPSO, "a",
                              [tld] + ([gu] if gu else []) + ([sp_rd] if sp_rd else []))
            tsp = P.op("gpsimd", "dma_start", out=sc["spill_a"][i * 128:(i + 1) * 128, :],
                       in_=sA["u_f32"][:], waits=[tpre], inc="qG")
            sp_rd = tsp
            gu = tpre
            xw[buf] = tpre
        t_sp_u = P.tok("qG")
        uT_tok = []
        for k in range(K):
            uT_tok.append(P.op("sync", "dma_start_transpose",
                               out=sA["uT"][:, k * T:(k + 1) * T],
                               in_=sc["spill_a"][:, k * 128:(k + 1) * 128],
                               waits=[t_sp_u], inc="qS"))

        def quantize(wsrc_ap, n_el, ngr, waits):
            ch = Chain(P)
            ch.op("vector", "tensor_copy", sA["wbb"][:, 0:n_el], wsrc_ap, waits=waits)
            ch.op("vector", "tensor_reduce", sA["gsum"][:, 0:ngr],
                  sA["wbb"][:, 0:n_el].rearrange("p (g e) -> p g e", e=64),
                  axis=AX.X, op=ALU.add, apply_absolute_value=True)
            ch.op("vector", "tensor_scalar", sA["gbf"][:, 0:ngr], sA["gsum"][:, 0:ngr],
                  1.0 / 64, None, op0=ALU.mult)
            ch.op("vector", "tensor_scalar", sA["scl"][:, 0:ngr], sA["gbf"][:, 0:ngr],
                  1.0, 1e-8, op0=ALU.mult, op1=ALU.max)
            ch.op("vector", "tensor_scalar", sA["thr"][:, 0:ngr], sA["scl"][:, 0:ngr],
                  THR, None, op0=ALU.mult)
            ch.op("vector", "tensor_scalar", sA["nthr"][:, 0:ngr], sA["thr"][:, 0:ngr],
                  -1.0, None, op0=ALU.mult)
            ch.op("vector", "tensor_tensor", sA["qp"][:, 0:n_el].rearrange("p (g e) -> p g e", e=64),
                  sA["wbb"][:, 0:n_el].rearrange("p (g e) -> p g e", e=64),
                  sA["thr"][:, 0:ngr].unsqueeze(2).broadcast_to([128, ngr, 64]),
                  op=ALU.is_gt)
            ch.op("vector", "tensor_tensor", sA["qn"][:, 0:n_el].rearrange("p (g e) -> p g e", e=64),
                  sA["wbb"][:, 0:n_el].rearrange("p (g e) -> p g e", e=64),
                  sA["nthr"][:, 0:ngr].unsqueeze(2).broadcast_to([128, ngr, 64]),
                  op=ALU.is_lt)
            ch.op("vector", "tensor_tensor", sA["qp"][:, 0:n_el], sA["qp"][:, 0:n_el],
                  sA["qn"][:, 0:n_el], op=ALU.subtract)
            ch.op("vector", "tensor_tensor", sA["wq"][:, 0:n_el].rearrange("p (g e) -> p g e", e=64),
                  sA["qp"][:, 0:n_el].rearrange("p (g e) -> p g e", e=64),
                  sA["scl"][:, 0:ngr].unsqueeze(2).broadcast_to([128, ngr, 64]),
                  op=ALU.mult)
            return ch.t

        wld_w = [None, None]
        w1t_rd = None
        p_ev = [None, None]
        hst_rd = None
        q_guard = None
        rl_guard = None
        sq_guard = None
        for blk in range(64):
            buf = blk % 2
            wraw = sA["wraw0"] if buf == 0 else sA["wraw1"]
            tld = P.op("sync", "dma_start", out=wraw[:],
                       in_=w["W_fc"][blk * 128:(blk + 1) * 128, :],
                       waits=([wld_w[buf]] if wld_w[buf] else []), inc="qS")
            tq = quantize(wraw[:], 2048, 32, [tld] + ([q_guard] if q_guard else []))
            wld_w[buf] = tq
            tev = None
            for k in range(K):
                ttr = P.op("tensor", "transpose", ps["B0"][:, 0:128],
                           sA["wq"][:, k * 128:(k + 1) * 128], sb["ident_f32"][:],
                           waits=[tq, t_id] + ([tev] if tev else []) +
                                 ([w1t_rd] if w1t_rd and k == 0 else []), inc="cT")
                tev = P.op("scalar", "activation", sA["w1t"][:, k * 128:(k + 1) * 128],
                           ps["B0"][:, 0:128], AF.Identity, waits=[ttr], inc="cS")
            q_guard = tev
            t_w1t = tev
            for g2 in range(G4):
                pbuf = g2 % 2
                pk = ps["A0"] if pbuf == 0 else ps["A1"]
                mwaits = [t_w1t] + uT_tok + ([p_ev[pbuf]] if p_ev[pbuf] else [])
                for k in range(K):
                    tok_mm = P.op("tensor", "matmul", pk[:, 0:512],
                                  sA["w1t"][:, k * 128:(k + 1) * 128],
                                  sA["uT"][:, k * T + g2 * 512: k * T + (g2 + 1) * 512],
                                  start=(k == 0), stop=(k == K - 1),
                                  waits=(mwaits if k == 0 else []),
                                  inc=("cT" if k == K - 1 else None))
                trl = P.op("scalar", "activation", sA["rl"][:], pk[:, 0:512], AF.Relu,
                           waits=[tok_mm] + ([p_ev[pbuf]] if p_ev[pbuf] else []) +
                                 ([rl_guard] if rl_guard else []), inc="cS")
                th = P.op("vector", "tensor_tensor", sA["hstg"][:, g2 * 512:(g2 + 1) * 512],
                          sA["rl"][:], sA["rl"][:], op=ALU.mult,
                          waits=[trl] + ([hst_rd] if hst_rd and g2 == 0 else []), inc="cV")
                rl_guard = th
                tsq = P.op("vector", "tensor_tensor", sA["sqv"][:],
                           sA["hstg"][:, g2 * 512:(g2 + 1) * 512],
                           sA["hstg"][:, g2 * 512:(g2 + 1) * 512], op=ALU.mult,
                           waits=[th] + ([sq_guard] if sq_guard else []), inc="cV")
                p_ev[pbuf] = tsq
                pS = ps["BIGa"] if g2 < 2 else ps["BIGb"]
                off = 512 * (g2 % 2)
                tmmq = P.op("tensor", "matmul", pS[0:1, off:off + 512],
                            sA["onec"][:], sA["sqv"][:],
                            start=(blk == 0), stop=(blk == 63), skip_group_check=True,
                            waits=[tsq, t_one], inc="cT")
                sq_guard = tmmq
            w1t_rd = tok_mm
            tspl = P.op("sync", "dma_start", out=sc["h_sp"][blk * 128:(blk + 1) * 128, :],
                        in_=sA["hstg"][:], waits=[P.tok("cV")], inc="qS")
            hst_rd = tspl
        t_h_done = P.tok("qS")
        t_ssq = P.tok("cT")
        ch = Chain(P)
        for g2 in range(G4):
            pS = ps["BIGa"] if g2 < 2 else ps["BIGb"]
            off = 512 * (g2 % 2)
            ch.op("vector", "tensor_copy", sA["rsrow"][0:1, g2 * 512:(g2 + 1) * 512],
                  pS[0:1, off:off + 512], waits=[t_ssq])
        ch.op("vector", "tensor_scalar", sA["rsrow"][0:1, :], sA["rsrow"][0:1, :],
              1.0 / 8192, EPSR, op0=ALU.mult, op1=ALU.add)
        t_ln = P.op("scalar", "activation", sA["rsrow"][0:1, :], sA["rsrow"][0:1, :],
                    AF.Ln, waits=[ch.t], inc="cS")
        t_rs = P.op("scalar", "activation", sA["rsrow"][0:1, :], sA["rsrow"][0:1, :],
                    AF.Exp, scale=-0.5, waits=[t_ln], inc="cS")
        tev = None
        for i in range(TIL):
            ttr = P.op("tensor", "transpose", ps["B1"][:, 0:1],
                       sA["rsrow"][0:1, i * 128:(i + 1) * 128], sb["ident_f32"][0:1, 0:1],
                       waits=[t_rs, t_id] + ([tev] if tev else []), inc="cT")
            tev = P.op("scalar", "activation", sA["rs_cols"][:, i:i + 1], ps["B1"][:, 0:1],
                       AF.Identity, waits=[ttr], inc="cS")
        t_rs_cols = tev
        t_rsp = P.op("vector", "tensor_copy", sb["rs_keep"][:], sA["rs_cols"][:],
                     waits=[t_rs_cols], inc="cV")

        w2stg_rd = w1t_rd
        for k2 in range(64):
            buf = k2 % 2
            wraw = sA["wraw0"] if buf == 0 else sA["wraw1"]
            tld = P.op("sync", "dma_start",
                       out=wraw[:].rearrange("p (k n) -> p k n", n=128),
                       in_=w["W_proj"][:, k2 * 128:(k2 + 1) * 128].rearrange("(k p) n -> p k n", p=128),
                       waits=([wld_w[buf]] if wld_w[buf] else []), inc="qS")
            tq = quantize(wraw[:], 2048, 32, [tld] + ([q_guard] if q_guard else []))
            wld_w[buf] = tq
            tev = None
            for k in range(K):
                ttr = P.op("tensor", "transpose", ps["B0"][:, 0:128],
                           sA["wq"][:, k * 128:(k + 1) * 128], sb["ident_f32"][:],
                           waits=[tq, t_id] + ([tev] if tev else []) +
                                 ([w2stg_rd] if w2stg_rd and k == 0 else []), inc="cT")
                tev = P.op("scalar", "activation", sA["w1t"][:, k * 128:(k + 1) * 128],
                           ps["B0"][:, 0:128], AF.Identity, waits=[ttr], inc="cS")
            q_guard = tev
            tspl = P.op("sync", "dma_start", out=sc["w2T_sp"][k2 * 128:(k2 + 1) * 128, :],
                        in_=sA["w1t"][:], waits=[tev], inc="qS")
            w2stg_rd = tspl
        t_w2t_done = P.tok("qS")

    barrier_all(P)
    with contextlib_ExitStack() as esB:
        sB = {}
        def SB(name, shape, dt):
            sB[name] = esB.enter_context(nc.sbuf_tensor("p3b_" + name, shape, dt))
        SB("w2t0", [128, 1024], BF16); SB("w2t1", [128, 1024], BF16)
        SB("ht0", [128, 512], BF16); SB("ht1", [128, 512], BF16)
        SB("ostg0", [128, D], F32); SB("ostg1", [128, D], F32)
        SB("ostg2", [128, D], F32); SB("ostg3", [128, D], F32)
        SB("xin0", [128, D], F32); SB("xin1", [128, D], F32)
        SB("x2_f32", [128, D], F32)
        ost = [sB["ostg0"], sB["ostg1"], sB["ostg2"], sB["ostg3"]]

        def pregion(tt, nch):
            if tt == 0:
                return ps["BIGa"][:, nch * 512:(nch + 1) * 512]
            if tt == 1:
                return ps["BIGb"][:, nch * 512:(nch + 1) * 512]
            if tt == 2:
                return (ps["A0"] if nch == 0 else ps["A1"])[:, 0:512]
            return (ps["B0"] if nch == 0 else ps["B1"])[:, 0:512]

        w2_w = [None, None]
        ht_w = [None, None]
        ost_rd = [None] * 4
        prev_evs = []
        final_done = []
        for quad in range(4):
            for half in range(2):
                for k2 in range(64):
                    buf = k2 % 2
                    w2t = sB["w2t0"] if buf == 0 else sB["w2t1"]
                    htb = sB["ht0"] if buf == 0 else sB["ht1"]
                    tw2 = P.op("gpsimd", "dma_start", out=w2t[:],
                               in_=sc["w2T_sp"][k2 * 128:(k2 + 1) * 128,
                                                half * 1024:(half + 1) * 1024],
                               waits=[t_w2t_done] + ([w2_w[buf]] if w2_w[buf] else []),
                               inc="qG")
                    tht = P.op("sync", "dma_start", out=htb[:],
                               in_=sc["h_sp"][k2 * 128:(k2 + 1) * 128,
                                              quad * 512:(quad + 1) * 512],
                               waits=([ht_w[buf]] if ht_w[buf] else []), inc="qS")
                    tok_mm = None
                    for tt in range(4):
                        for nch in range(2):
                            first = (tt == 0 and nch == 0)
                            last = (tt == 3 and nch == 1)
                            mwaits = []
                            if first:
                                mwaits = [tw2, tht]
                                if k2 == 0:
                                    mwaits += prev_evs
                            tok_mm = P.op("tensor", "matmul", pregion(tt, nch),
                                          htb[:, tt * 128:(tt + 1) * 128],
                                          w2t[:, nch * 512:(nch + 1) * 512],
                                          start=(k2 == 0), stop=(k2 == 63),
                                          skip_group_check=True,
                                          waits=mwaits,
                                          inc=("cT" if last else None))
                    w2_w[buf] = tok_mm
                    ht_w[buf] = tok_mm
                t_mm_all = P.tok("cT")
                evs = []
                for tt in range(4):
                    i_abs = quad * 4 + tt
                    for nch in range(2):
                        tev = P.op("scalar", "activation",
                                   ost[tt][:, half * 1024 + nch * 512: half * 1024 + (nch + 1) * 512],
                                   pregion(tt, nch), AF.Identity,
                                   scale=sb["rs_keep"][:, i_abs:i_abs + 1],
                                   waits=[t_mm_all, t_rsp] +
                                         ([ost_rd[tt]] if ost_rd[tt] and half == 0 and nch == 0 else []),
                                   inc="cS")
                        evs.append(tev)
                prev_evs = [P.tok("cS")]
            t_ev = P.tok("cS")
            for tt in range(4):
                i_abs = quad * 4 + tt
                buf = tt % 2
                xs = sB["xin0"] if buf == 0 else sB["xin1"]
                tldx = P.op("sync", "dma_start", out=xs[:],
                            in_=sc["xres"][i_abs * 128:(i_abs + 1) * 128, :],
                            waits=(final_done[-2:] if final_done else []), inc="qS")
                tm = P.op("vector", "tensor_tensor", sB["x2_f32"][:], xs[:], ost[tt][:],
                          op=ALU.add,
                          waits=[tldx, t_ev] + (final_done[-1:] if final_done else []),
                          inc="cV")
                two = P.op("sync", "dma_start", out=sc["y_out"][i_abs * 128:(i_abs + 1) * 128, :],
                           in_=sB["x2_f32"][:], waits=[tm], inc="qS")
                ost_rd[tt] = tm
                final_done.append(tm)
    return P.tok("qS")


def alloc_p3_persist(nc, es, pfx="p3_"):
    sb = {}
    def S(name, shape, dt):
        sb[name] = es.enter_context(nc.sbuf_tensor(pfx + name, shape, dt))
    S("ident_f32", [128, 128], F32)
    S("rs_keep", [128, 16], F32)
    for tag in ("a",):
        for c in ("s1", "s2", "mn", "nv", "ve", "rs", "nm"):
            S(c + tag, [128, 1], F32)
    return sb


import concourse.bass as _bass
from concourse.bass_utils import run_bass_kernel_spmd as _run
import contextlib as _ctl2

P0_W = ["ln1_g", "ln1_b", "ln2_g", "ln2_b", "tm_g", "tm_b", "time_decay",
        "time_first", "Wv", "Wr", "Wo", "bo", "cm_g", "cm_b", "Wck", "Wcv", "Wcr"]
P0_SHAPES = {"Wv": [2048, 2048], "Wr": [2048, 2048], "Wo": [2048, 2048],
             "Wcr": [2048, 2048], "Wck": [2048, 8192], "Wcv": [8192, 2048]}
P1_SHAPES = {"gate_w": [16, 2048], "w1": [16, 2048, 1408], "w2": [16, 1408, 2048],
             "sw1": [1, 2048, 1408], "sw2": [1, 1408, 2048]}
P2_SHAPES = {"W_in": [2048, 8512], "conv_w": [4352, 4], "conv_b": [4352],
             "A": [64], "D": [64], "dt_bias": [64], "norm_w": [4096],
             "W_out": [4096, 2048]}
P3_SHAPES = {"W_fc": [8192, 2048], "W_proj": [2048, 8192]}


def build_fused():
    nc = _bass.Bass("TRN2", target_bir_lowering=False)
    w0 = {"x_in": nc.dram_tensor("x_in", [T, D], F32, kind="ExternalInput").ap()}
    for k in P0_W:
        w0[k] = nc.dram_tensor("w0_" + k, P0_SHAPES.get(k, [2048]), F32,
                               kind="ExternalInput").ap()
    w1 = {k: nc.dram_tensor("w1_" + k, shp, F32, kind="ExternalInput").ap()
          for k, shp in P1_SHAPES.items()}
    w2 = {k: nc.dram_tensor("w2_" + k, shp, F32, kind="ExternalInput").ap()
          for k, shp in P2_SHAPES.items()}
    w3 = {k: nc.dram_tensor("w3_" + k, shp, F32, kind="ExternalInput").ap()
          for k, shp in P3_SHAPES.items()}
    sc = {}
    sc["xres"] = nc.dram_tensor("xres", [T, D], F32, kind="Internal").ap()
    sc["y_out"] = nc.dram_tensor("y_out", [T, D], F32, kind="ExternalOutput").ap()
    sc["spill_a"] = nc.dram_tensor("spill_a", [T, D], BF16, kind="Internal").ap()
    sc["spill_b"] = nc.dram_tensor("spill_b", [T, D], BF16, kind="Internal").ap()
    sc["tou_sp"] = nc.dram_tensor("tou_sp", [D, T], BF16, kind="Internal").ap()
    sc["rc_sp"] = nc.dram_tensor("rc_sp", [D, T], BF16, kind="Internal").ap()
    sc["vc_acc"] = nc.dram_tensor("vc_acc", [T, D], F32, kind="Internal").ap()
    sc["moe_acc"] = nc.dram_tensor("moe_acc", [T, D], F32, kind="Internal").ap()
    sc["zT_sp"] = nc.dram_tensor("zT_sp", [4096, T], BF16, kind="Internal").ap()
    sc["xbcT_sp"] = nc.dram_tensor("xbcT_sp", [4352, T], BF16, kind="Internal").ap()
    sc["xbcP_sp"] = nc.dram_tensor("xbcP_sp", [4096, T], BF16, kind="Internal").ap()
    sc["csrow_sp"] = nc.dram_tensor("csrow_sp", [16, 8192], F32, kind="Internal").ap()
    sc["yg_sp"] = nc.dram_tensor("yg_sp", [T, 4096], BF16, kind="Internal").ap()
    sc["attnT_sp"] = nc.dram_tensor("attnT_sp", [2048, T], BF16, kind="Internal").ap()
    sc["h_sp"] = nc.dram_tensor("h_sp", [8192, T], BF16, kind="Internal").ap()
    sc["w2T_sp"] = nc.dram_tensor("w2T_sp", [8192, 2048], BF16, kind="Internal").ap()
    sc["xres_out"] = sc["xres"]

    es = _ctl2.ExitStack()
    with es:
        ps = alloc_psum(nc, es)
        P = Prog(nc)
        with _ctl2.ExitStack() as e0:
            sb0 = alloc_p0(nc, e0)
            pre = emit_common(P, nc, sb0)
            emit_p0(P, nc, w0, sc, sb0, ps, pre)
        barrier_all(P)
        with _ctl2.ExitStack() as e1:
            sb1 = alloc_p1(nc, e1)
            emit_p1(P, nc, w1, sc, sb1, ps, [])
        barrier_all(P)
        with _ctl2.ExitStack() as e2:
            sb2 = alloc_p2_persist(nc, e2)
            emit_p2(P, nc, w2, sc, sb2, ps, e2)
        barrier_all(P)
        with _ctl2.ExitStack() as e3:
            sb3 = alloc_p3_persist(nc, e3)
            emit_p3(P, nc, w3, sc, sb3, ps, e3)
        tout = P.tok("qS")
        P.wait("sync", tout)
        semhs = {}
        for name in P.semnames:
            semhs[name] = es.enter_context(nc.semaphore(name))
        with nc.Block() as block:
            P.finalize(block, semhs)
    return nc, P.nops


def kernel(**inputs):
    NCORES = 8
    cast = np.ascontiguousarray
    _ = inputs.get("L0_attn_Wk")
    nc, nops = build_fused()
    wmap = {}
    for k in P0_W:
        wmap["w0_" + k] = cast(inputs["L0_attn_" + k])
    for k in P1_SHAPES:
        wmap["w1_" + k] = cast(inputs["L0_ffn_" + k])
    for k in P2_SHAPES:
        wmap["w2_" + k] = cast(inputs["L1_attn_" + k])
    for k in P3_SHAPES:
        wmap["w3_" + k] = cast(inputs["L1_ffn_" + k])
    maps = []
    for b in range(NCORES):
        m = dict(wmap)
        m["x_in"] = cast(inputs["L0_attn_x"][b])
        maps.append(m)
    r = _run(nc, maps, core_ids=list(range(NCORES)))
    out = np.stack([r.results[b]["y_out"] for b in range(NCORES)], axis=0)
    return out.astype(np.float32)


# version 03
# speedup vs baseline: 1.0248x; 1.0248x over previous
import numpy as np
import concourse.bass as bass
import concourse.mybir as mybir
from concourse.alu_op_type import AluOpType as ALU

F32 = mybir.dt.float32
BF16 = mybir.dt.bfloat16
I32 = mybir.dt.int32
AF = mybir.ActivationFunctionType
AX = mybir.AxisListType

ENGINES = ("sync", "tensor", "vector", "scalar", "gpsimd")
DMA_METHODS = {"dma_start", "dma_start_transpose"}


class Prog:
    def __init__(self, nc):
        self.nc = nc
        self.streams = {e: [] for e in ENGINES}
        self.counts = {}
        self.semnames = []
        self.last_tok = {e: None for e in ENGINES}
        self.nops = 0

    def sem(self, name):
        if name not in self.counts:
            self.counts[name] = 0
            self.semnames.append(name)
        return name

    def tok(self, name):
        return (name, self.counts[name])

    def op(self, eng, meth, *args, waits=(), inc=None, incn=None, **kw):
        self.nops += 1
        if incn is None:
            incn = 16 if meth in DMA_METHODS else 1
        tok = None
        if inc is not None:
            self.sem(inc)
            self.counts[inc] += incn
            tok = (inc, self.counts[inc])
            self.last_tok[eng] = tok
        wmax = {}
        for (s, v) in waits:
            if v > wmax.get(s, -1):
                wmax[s] = v
        wlist = tuple(wmax.items())
        incv = incn if inc is not None else None

        def f(h, meth=meth, args=args, kw=kw, wlist=wlist, inc=inc, incv=incv, sems=None):
            for (s, v) in wlist:
                h.wait_ge(self._semh[s], v)
            r = getattr(h, meth)(*args, **kw)
            if inc is not None:
                r.then_inc(self._semh[inc], incv)
        self.streams[eng].append(f)
        return tok

    def raw(self, eng, fn):
        self.streams[eng].append(lambda h, fn=fn: fn(h))

    def wait(self, eng, *toks):
        wmax = {}
        for (s, v) in toks:
            if v > wmax.get(s, -1):
                wmax[s] = v
        wlist = tuple(wmax.items())

        def f(h, wlist=wlist):
            for (s, v) in wlist:
                h.wait_ge(self._semh[s], v)
        self.streams[eng].append(f)

    def finalize(self, block, sem_handles):
        self._semh = sem_handles
        for e in ENGINES:
            stream = self.streams[e]
            dec = getattr(block, e)

            def runner(h, stream=stream):
                for f in stream:
                    f(h)
            dec(runner)


def ceil_div(a, b):
    return (a + b - 1) // b


"""Megakernel phases for mk_seq_preln_d2 (per-core programs, raw bass).

Per core: x [T=2048, D=2048] f32 (one batch element). Layouts:
token-major [tok128, d-free]; channel-major "T" [d128, tok-free];
guarded transposed tiles [128, GP+T], cols 0..GP-1 zero, data at +GP.

Sync: framework tokens; same-engine FIFO trusted; every cross-engine RAW/WAR
edge gets an explicit token wait.
"""
import numpy as np


T = 2048
D = 2048
K = D // 128
NB = D // 128
TIL = T // 128
G4 = T // 512
GP = 16
GW = GP + T




def alloc_psum(nc, es):
    ps = {}
    for nm in ("A0", "A1", "B0", "B1"):
        ps[nm] = es.enter_context(nc.psum_tensor("ps" + nm, [128, 512], F32))
    ps["BIGa"] = es.enter_context(nc.psum_tensor("psBIGa", [128, 1024], F32))
    ps["BIGb"] = es.enter_context(nc.psum_tensor("psBIGb", [128, 1024], F32))
    return ps

def emit_common(P, nc, sb):
    t_io = P.op("gpsimd", "iota", sb["m_bcastT"][:].bitcast(I32), pattern=[[-1, T]],
                base=T - 1, channel_multiplier=0, inc="cG")
    t_mb = P.op("vector", "tensor_copy", sb["m_bcastT"][:], sb["m_bcastT"][:].bitcast(I32),
                waits=[t_io], inc="cV")
    P.op("vector", "tensor_scalar", sb["s1a"][:], sb["s1a"][:], 0.0, 0.0,
         op0=ALU.mult, op1=ALU.add, accum_out=sb["s2a"][:], inc="cV")
    t_fl = P.op("vector", "scalar_tensor_tensor", sb["s1a"][:], sb["s1a"][:], 0.0,
                sb["s1a"][:], op0=ALU.mult, op1=ALU.mult, accum_out=sb["s2a"][:],
                inc="cV")
    return [t_mb, t_fl]


def load_cols(P, vec_dram, dst, n_cols, queue="sync", inc="qS", waits=()):
    toks = []
    for k in range(n_cols):
        toks.append(P.op(queue, "dma_start", out=dst[:, k:k + 1],
                         in_=vec_dram[k * 128:(k + 1) * 128].rearrange("(p one) -> p one", one=1),
                         waits=(waits if k == 0 else ()), inc=inc))
    return toks


def bcast_row(P, vec_dram, dst, n, queue="gpsimd", inc="qG", waits=()):
    src = vec_dram[0:n].partition_broadcast(128)
    return P.op(queue, "dma_start", out=dst[:, 0:n], in_=src, waits=waits, inc=inc)


class LN:
    def __init__(self, P, sb):
        self.P = P
        self.sb = sb
        self.g = {}

    def full(self, x_ap, out_ap, sc1_ap, sc2_ap, width, eps, tag, waits):
        P, sb = self.P, self.sb
        s1 = sb["s1" + tag]; s2 = sb["s2" + tag]; mean = sb["mn" + tag]
        nv = sb["nv" + tag]; ve = sb["ve" + tag]; rstd = sb["rs" + tag]; nmrs = sb["nm" + tag]
        g = self.g.get(tag, {})
        waits = list(waits)
        t1 = P.op("vector", "tensor_scalar", sc1_ap, x_ap, 1.0, 0.0,
                  op0=ALU.mult, op1=ALU.add, accum_out=s1[:], waits=waits, inc="cV")
        t2 = P.op("scalar", "activation", sc2_ap, x_ap, AF.Square, accum_out=s2[:],
                  waits=waits + ([g["tn"]] if g.get("tn") else []), inc="cS")
        tm = P.op("vector", "tensor_scalar", mean[:], s1[:], 1.0 / width, None,
                  op0=ALU.mult, waits=[t1], inc="cV")
        tq = P.op("scalar", "mul", s2[:], s2[:], 1.0 / width, waits=[t2], inc="cS")
        tn = P.op("vector", "scalar_tensor_tensor", nv[:], mean[:], mean[:], s2[:],
                  op0=ALU.mult, op1=ALU.subtract, waits=[tq, tm], inc="cV")
        tv = P.op("vector", "tensor_scalar", ve[:], nv[:], -1.0, eps,
                  op0=ALU.mult, op1=ALU.add,
                  waits=[tn] + ([g["tr"]] if g.get("tr") else []), inc="cV")
        tl = P.op("scalar", "activation", ve[:], ve[:], AF.Ln, waits=[tv], inc="cS")
        tr = P.op("scalar", "activation", rstd[:], ve[:], AF.Exp, scale=-0.5,
                  waits=[tl] + ([g["tz"]] if g.get("tz") else []), inc="cS")
        tz = P.op("vector", "tensor_scalar", nmrs[:], mean[:], rstd[:], -1.0,
                  op0=ALU.mult, op1=ALU.mult,
                  waits=[tr] + ([g["tu"]] if g.get("tu") else []), inc="cV")
        tu = P.op("scalar", "activation", out_ap, x_ap, AF.Identity,
                  scale=rstd[:], bias=nmrs[:], waits=[tz, tr], inc="cS")
        self.g[tag] = {"tn": tn, "tr": tr, "tz": tz, "tu": tu}
        return tu, tz


def emit_p0(P, nc, w, sc, sb, ps, pre_toks):
    EPSI = 1e-12
    EPSO = 1e-5
    ln = LN(P, sb)

    tg1 = bcast_row(P, w["ln1_g"], sb["gA"], D)
    tb1 = bcast_row(P, w["ln1_b"], sb["bA"], D)
    tg2 = bcast_row(P, w["tm_g"], sb["gB"], D)
    tb2 = bcast_row(P, w["tm_b"], sb["bB"], D)
    t_tf = load_cols(P, w["time_first"], sb["tf_cols"], K)
    t_td = load_cols(P, w["time_decay"], sb["td_cols"], K)
    t_bo = load_cols(P, w["bo"], sb["boT_cols"], K)
    t_itf = P.op("vector", "tensor_scalar", sb["itf_cols"][:], sb["tf_cols"][:],
                 -1.0, 1.0, op0=ALU.mult, op1=ALU.add, waits=t_tf, inc="cV")

    def ln_affine_tail(gts, guards, w_first):
        tgA, tbA, tgB, tbB = gts
        tub, _ = ln.full(sb["x2_f32"][:], sb["u_f32"][:], sb["x2_f32"][:], sb["u_f32"][:],
                         D, EPSI, "b",
                         w_first + ([guards["u_rd"]] if guards.get("u_rd") else []))
        tx1 = P.op("vector", "scalar_tensor_tensor", sb["x1_f32"][:], sb["u_f32"][:],
                   1.0, sb["gA"][:], op0=ALU.mult, op1=ALU.mult,
                   waits=[tub, tgA] + ([guards["x1_rd"]] if guards.get("x1_rd") else []),
                   inc="cV")
        tx1b = P.op("vector", "tensor_tensor", sb["x1_f32"][:], sb["x1_f32"][:],
                    sb["bA"][:], op=ALU.add, waits=[tbA], inc="cV")
        tuc, _ = ln.full(sb["x1_f32"][:], sb["u_f32"][:], sb["x1_f32"][:], sb["u_f32"][:],
                         D, EPSI, "a", [tx1b, tub])
        t3 = P.op("vector", "scalar_tensor_tensor", sb["x1_f32"][:], sb["u_f32"][:],
                  1.0, sb["gB"][:], op0=ALU.mult, op1=ALU.mult,
                  waits=[tuc, tgB], inc="cV")
        t4 = P.op("vector", "tensor_tensor", sb["x1_f32"][:], sb["x1_f32"][:], sb["bB"][:],
                  op=ALU.add,
                  waits=[tbB] + ([guards["sp_rd"]] if guards.get("sp_rd") else []),
                  inc="cV")
        return t4, t3

    xw = [None, None]
    guards = {"u_rd": None, "x1_rd": None, "sp_rd": None}
    for i in range(TIL):
        buf = i % 2
        xs = sb["xin0"] if buf == 0 else sb["xin1"]
        tld = P.op("sync", "dma_start", out=xs[:], in_=w["x_in"][i * 128:(i + 1) * 128, :],
                   waits=([xw[buf]] if xw[buf] else []), inc="qS")
        tpre, _ = ln.full(xs[:], sb["x2_f32"][:], xs[:], sb["x2_f32"][:], D, EPSO, "a",
                          [tld] + ([guards["u_rd"]] if guards.get("u_rd") else []))
        t4, t3 = ln_affine_tail((tg1, tb1, tg2, tb2), guards, [tpre])
        guards["u_rd"] = t3
        guards["x1_rd"] = t4
        tsp = P.op("gpsimd", "dma_start", out=sc["spill_a"][i * 128:(i + 1) * 128, :],
                   in_=sb["x1_f32"][:], waits=[t4], inc="qG")
        guards["sp_rd"] = tsp
        xw[buf] = ln.g["a"]["tu"]
    t_spill_a_done = P.tok("qG")
    t_stageB_dve = P.tok("cV")
    t_stageB_act = P.tok("cS")

    tz0 = P.op("vector", "memset", sb["xT_g"][:], 0.0, inc="cV")
    xT_tok = []
    for k in range(K):
        view = sb["xT_g"][:, k * GW + GP: k * GW + GP + T]
        xT_tok.append(P.op("sync", "dma_start_transpose", out=view,
                           in_=sc["spill_a"][:, k * 128:(k + 1) * 128],
                           waits=[t_spill_a_done, tz0], inc="qS"))

    def xlnT(k, c0, c1):
        return sb["xT_g"][:, k * GW + GP + c0: k * GW + GP + c1]

    def xsT(k, c0, c1):
        return sb["xT_g"][:, k * GW + GP + c0 - 1: k * GW + GP + c1 - 1]

    wbuf_w = [None, None, None, None]

    def load_colblock(Wdram, nb, buf, extra=()):
        wcb = sb["wcb" + str(buf)]
        cbsrc = Wdram[:, nb * 128:(nb + 1) * 128].rearrange("(k p) n -> p k n", p=128)
        tw = P.op("gpsimd", "dma_start", out=wcb[:].rearrange("p (k n) -> p k n", n=128), in_=cbsrc,
                  waits=([wbuf_w[buf]] if wbuf_w[buf] else []) + list(extra), inc="qG")
        return wcb, tw

    pv_ev = [None, None]
    tdec_guard = None
    for nb in range(NB):
        buf = nb % 4
        wcb, tw = load_colblock(w["Wv"], nb, buf)
        for g in range(G4):
            pbuf = g % 2
            pv = ps["A0"] if pbuf == 0 else ps["A1"]
            sig = sb["sig0"] if pbuf == 0 else sb["sig1"]
            tdec = P.op("scalar", "activation", sb["decT"][:], sb["m_bcastT"][:, g * 512:(g + 1) * 512],
                        AF.Exp, scale=sb["td_cols"][:, nb:nb + 1],
                        waits=pre_toks + t_td + ([tdec_guard] if tdec_guard else []), inc="cS")
            mwaits = [tw] + xT_tok + ([pv_ev[pbuf]] if pv_ev[pbuf] else [])
            for k in range(K):
                tok_mm = P.op("tensor", "matmul", pv[:, 0:512],
                              wcb[:, k * 128:(k + 1) * 128], xlnT(k, g * 512, (g + 1) * 512),
                              start=(k == 0), stop=(k == K - 1),
                              waits=(mwaits if k == 0 else []),
                              inc=("cT" if k == K - 1 else None))
            tev = P.op("vector", "scalar_tensor_tensor", sig[:],
                       sb["decT"][:], 1.0, pv[:, 0:512],
                       op0=ALU.mult, op1=ALU.mult,
                       accum_out=sb["vmixP"][:, nb * G4 + g: nb * G4 + g + 1],
                       waits=[tok_mm, tdec], inc="cV")
            pv_ev[pbuf] = tev
            tdec_guard = tev
        wbuf_w[buf] = tok_mm
    t_red = P.op("vector", "tensor_reduce",
                 sb["vmixT"][:].rearrange("p (nb one) -> p nb one", one=1),
                 sb["vmixP"][:].rearrange("p (nb g) -> p nb g", g=G4),
                 axis=AX.X, op=ALU.add, waits=[tdec_guard], inc="cV")
    t_vbf = P.op("vector", "tensor_copy", sb["vmixT_bf"][:], sb["vmixT"][:],
                 waits=[t_red], inc="cV")

    p2_ev = [None, None]
    for nb in range(NB):
        buf = nb % 4
        wcb, tw = load_colblock(w["Wo"], nb, buf)
        pbuf = nb % 2
        pt = ps["B0"] if pbuf == 0 else ps["B1"]
        mwaits = [tw, t_vbf] + ([p2_ev[pbuf]] if p2_ev[pbuf] else [])
        for k in range(K):
            tok_mm = P.op("tensor", "matmul", pt[:, 0:1],
                          wcb[:, k * 128:(k + 1) * 128], sb["vmixT_bf"][:, k:k + 1],
                          start=(k == 0), stop=(k == K - 1),
                          waits=(mwaits if k == 0 else []),
                          inc=("cT" if k == K - 1 else None))
        tev = P.op("scalar", "activation", sb["twoT_cols"][:, nb:nb + 1], pt[:, 0:1],
                   AF.Identity, bias=sb["boT_cols"][:, nb:nb + 1],
                   waits=[tok_mm] + t_bo, inc="cS")
        p2_ev[pbuf] = tev
        wbuf_w[buf] = tok_mm
    t_two = P.tok("cS")

    pr_ev = [None, None]
    tou_guard = None
    wrp_guard = None
    for nb in range(NB):
        buf = nb % 4
        wcb, tw = load_colblock(w["Wr"], nb, buf)
        tdr = None
        for k in range(K):
            tdr = P.op("vector", "tensor_scalar", sb["wrp"][:, k * 128:(k + 1) * 128],
                       wcb[:, k * 128:(k + 1) * 128], sb["tf_cols"][:, k:k + 1], None,
                       op0=ALU.mult,
                       waits=(([tw, t_itf] + ([wrp_guard] if wrp_guard else [])) if k == 0 else []))
        tdr2 = P.op("vector", "tensor_tensor", wcb[:], wcb[:], sb["wrp"][:],
                    op=ALU.subtract, inc="cV")
        for g in range(G4):
            pbuf = g % 2
            pr = ps["A0"] if pbuf == 0 else ps["A1"]
            mwaits = [tdr2] + ([pr_ev[pbuf]] if pr_ev[pbuf] else [])
            for k in range(K):
                P.op("tensor", "matmul", pr[:, 0:512],
                     sb["wrp"][:, k * 128:(k + 1) * 128], xlnT(k, g * 512, (g + 1) * 512),
                     start=(k == 0), stop=False,
                     waits=(mwaits if k == 0 else []))
                tok_mm = P.op("tensor", "matmul", pr[:, 0:512],
                              wcb[:, k * 128:(k + 1) * 128], xsT(k, g * 512, (g + 1) * 512),
                              start=False, stop=(k == K - 1),
                              inc=("cT" if k == K - 1 else None))
            sig = sb["sig0"] if pbuf == 0 else sb["sig1"]
            tsg = P.op("scalar", "activation", sig[:], pr[:, 0:512], AF.Sigmoid,
                       waits=[tok_mm] + ([pr_ev[pbuf]] if pr_ev[pbuf] else []), inc="cS")
            ttou = P.op("vector", "tensor_scalar", sb["touT_buf"][:, g * 512:(g + 1) * 512],
                        sig[:], sb["twoT_cols"][:, nb:nb + 1], None,
                        op0=ALU.mult,
                        waits=[tsg, t_two] + ([tou_guard] if (tou_guard and g == 0) else []),
                        inc="cV")
            pr_ev[pbuf] = ttou
        tsp = P.op("sync", "dma_start", out=sc["tou_sp"][nb * 128:(nb + 1) * 128, :],
                   in_=sb["touT_buf"][:], waits=[P.tok("cV")], inc="qS")
        tou_guard = tsp
        wbuf_w[buf] = tok_mm
        wrp_guard = tok_mm
    t_tou_done = P.tok("qS")

    tg3 = bcast_row(P, w["ln2_g"], sb["gA"], D, waits=[t_stageB_dve])
    tb3 = bcast_row(P, w["ln2_b"], sb["bA"], D, waits=[t_stageB_dve])
    tg4 = bcast_row(P, w["cm_g"], sb["gB"], D, waits=[t_stageB_dve])
    tb4 = bcast_row(P, w["cm_b"], sb["bB"], D, waits=[t_stageB_dve])
    xw = [None, None]
    tuw = [None, None]
    guards = {"u_rd": None, "x1_rd": None, "sp_rd": None}
    twr_prev = None
    for i in range(TIL):
        buf = i % 2
        xs = sb["xin0"] if buf == 0 else sb["xin1"]
        tub = sb["tou0"] if buf == 0 else sb["tou1"]
        tld = P.op("sync", "dma_start", out=xs[:], in_=w["x_in"][i * 128:(i + 1) * 128, :],
                   waits=([xw[buf]] if xw[buf] else [t_stageB_act]), inc="qS")
        tto = P.op("sync", "dma_start_transpose", out=tub[:],
                   in_=sc["tou_sp"][:, i * 128:(i + 1) * 128],
                   waits=[t_tou_done] + ([tuw[buf]] if tuw[buf] else []), inc="qS")
        tpre, _ = ln.full(xs[:], sb["u_f32"][:], xs[:], sb["u_f32"][:], D, EPSO, "a",
                          [tld] + ([guards["u_rd"]] if guards.get("u_rd") else []) +
                          ([twr_prev] if twr_prev else []))
        tx2 = P.op("vector", "tensor_tensor", sb["x2_f32"][:], sb["u_f32"][:], tub[:],
                   op=ALU.add, waits=[tpre, tto] +
                   ([guards["x1_rd"]] if guards.get("x1_rd") else []), inc="cV")
        txr = P.op("vector", "tensor_tensor", sb["u_f32"][:], xs[:], sb["x2_f32"][:],
                   op=ALU.add, waits=[tx2], inc="cV")
        twr = P.op("sync", "dma_start", out=sc["xres"][i * 128:(i + 1) * 128, :],
                   in_=sb["u_f32"][:], waits=[txr], inc="qS")
        twr_prev = twr
        t4, t3 = ln_affine_tail((tg3, tb3, tg4, tb4), guards, [txr, twr])
        guards["u_rd"] = t3
        guards["x1_rd"] = t4
        tsp = P.op("gpsimd", "dma_start", out=sc["spill_b"][i * 128:(i + 1) * 128, :],
                   in_=sb["x1_f32"][:], waits=[t4], inc="qG")
        guards["sp_rd"] = tsp
        xw[buf] = txr
        tuw[buf] = tx2
    t_spill_b_done = P.tok("qG")

    t_allmm = P.tok("cT")
    tz1 = P.op("vector", "memset", sb["xT_g"][:], 0.0, waits=[t_allmm], inc="cV")
    xT_tok = []
    for k in range(K):
        view = sb["xT_g"][:, k * GW + GP: k * GW + GP + T]
        xT_tok.append(P.op("sync", "dma_start_transpose", out=view,
                           in_=sc["spill_b"][:, k * 128:(k + 1) * 128],
                           waits=[t_spill_b_done, tz1], inc="qS"))

    pr_ev = [None, None]
    tou_guard = P.tok("qS")
    for nb in range(NB):
        buf = nb % 4
        wcb, tw = load_colblock(w["Wcr"], nb, buf)
        for g in range(G4):
            pbuf = g % 2
            pr = ps["A0"] if pbuf == 0 else ps["A1"]
            mwaits = [tw] + xT_tok + ([pr_ev[pbuf]] if pr_ev[pbuf] else [])
            for k in range(K):
                tok_mm = P.op("tensor", "matmul", pr[:, 0:512],
                              wcb[:, k * 128:(k + 1) * 128], xlnT(k, g * 512, (g + 1) * 512),
                              start=(k == 0), stop=(k == K - 1),
                              waits=(mwaits if k == 0 else []),
                              inc=("cT" if k == K - 1 else None))
            tsg = P.op("scalar", "activation", sb["touT_buf"][:, g * 512:(g + 1) * 512],
                       pr[:, 0:512], AF.Sigmoid,
                       waits=[tok_mm] + ([tou_guard] if g == 0 else []) +
                             ([pr_ev[pbuf]] if pr_ev[pbuf] else []),
                       inc="cS")
            pr_ev[pbuf] = tsg
        tsp = P.op("sync", "dma_start", out=sc["rc_sp"][nb * 128:(nb + 1) * 128, :],
                   in_=sb["touT_buf"][:], waits=[P.tok("cS")], inc="qS")
        tou_guard = tsp
        wbuf_w[buf] = tok_mm
    t_rc_done = P.tok("qS")

    NS = 16
    NBH = 4
    KH = 4
    kc_guard = None
    wcv_guard = None
    vst_w = [None, None]
    for e in range(NS):
        for nbh in range(NBH):
            nb_abs = e * NBH + nbh
            buf = nbh % 4
            wcb, tw = load_colblock(w["Wck"], nb_abs, buf)
            for g in range(G4):
                pbuf = g % 2
                pk = ps["B0"] if pbuf == 0 else ps["B1"]
                mwaits = [tw] + xT_tok + ([p2_ev[pbuf]] if p2_ev[pbuf] else [])
                for k in range(K):
                    tok_mm = P.op("tensor", "matmul", pk[:, 0:512],
                                  wcb[:, k * 128:(k + 1) * 128], xsT(k, g * 512, (g + 1) * 512),
                                  start=(k == 0), stop=(k == K - 1),
                                  waits=(mwaits if k == 0 else []),
                                  inc=("cT" if k == K - 1 else None))
                sig = sb["sig0"] if pbuf == 0 else sb["sig1"]
                trl = P.op("scalar", "activation", sig[:], pk[:, 0:512], AF.Relu,
                           waits=[tok_mm] + ([p2_ev[pbuf]] if p2_ev[pbuf] else []), inc="cS")
                tsq = P.op("vector", "tensor_tensor",
                           sb["kcT"][:, nbh * T + g * 512: nbh * T + (g + 1) * 512],
                           sig[:], sig[:], op=ALU.mult,
                           waits=[trl] + ([kc_guard] if (kc_guard and nbh == 0 and g == 0) else []),
                           inc="cV")
                p2_ev[pbuf] = tsq
            wbuf_w[buf] = tok_mm
        t_kc_ready = P.tok("cV")
        twv = P.op("gpsimd", "dma_start", out=sb["wcv_s"][:].rearrange("p (k n) -> p k n", n=2048),
                   in_=w["Wcv"][e * 512:(e + 1) * 512, :].rearrange("(k p) n -> p k n", p=128),
                   waits=([wcv_guard] if wcv_guard else []), inc="qG")
        pv_ev2 = [None, None]
        for i in range(TIL):
            pbuf = i % 2
            pvc = ps["BIGa"] if pbuf == 0 else ps["BIGb"]
            vstage = sb["vst0"] if pbuf == 0 else sb["vst1"]
            tevs = []
            for half in range(2):
                mwaits = [twv, t_kc_ready] + ([pv_ev2[pbuf]] if pv_ev2[pbuf] else [])
                for k2 in range(KH):
                    for nchunk in range(2):
                        nc_abs = half * 2 + nchunk
                        tok_mm = P.op("tensor", "matmul", pvc[:, nchunk * 512:(nchunk + 1) * 512],
                                      sb["kcT"][:, k2 * T + i * 128: k2 * T + (i + 1) * 128],
                                      sb["wcv_s"][:, k2 * 2048 + nc_abs * 512: k2 * 2048 + (nc_abs + 1) * 512],
                                      start=(k2 == 0), stop=(k2 == KH - 1),
                                      waits=(mwaits if (k2 == 0 and nchunk == 0) else []),
                                      inc=("cT" if (k2 == KH - 1 and nchunk == 1) else None))
                tev = P.op("scalar", "activation", vstage[:, half * 1024:(half + 1) * 1024],
                           pvc[:], AF.Identity,
                           waits=[tok_mm] + ([vst_w[pbuf]] if vst_w[pbuf] and half == 0 else []),
                           inc="cS")
                pv_ev2[pbuf] = tev
                tevs.append(tev)
            kw = dict(accum_op=ALU.add) if e > 0 else {}
            tacc = P.op("gpsimd", "dma_start", out=sc["vc_acc"][i * 128:(i + 1) * 128, :],
                        in_=vstage[:], waits=tevs, inc="qG", **kw)
            vst_w[pbuf] = tacc
        kc_guard = tok_mm
        wcv_guard = tok_mm
    t_vc_done = P.tok("qG")

    xw = [None, None]
    rw = [None, None]
    vw = [None, None]
    for i in range(TIL):
        buf = i % 2
        xs = sb["u_f32"] if buf == 0 else sb["x1_f32"]
        rcb = sb["tou0"] if buf == 0 else sb["tou1"]
        vcb = sb["vst0"] if buf == 0 else sb["vst1"]
        tldx = P.op("sync", "dma_start", out=xs[:], in_=sc["xres"][i * 128:(i + 1) * 128, :],
                    waits=([xw[buf]] if xw[buf] else [P.tok("cV"), P.tok("cS")]), inc="qS")
        tldr = P.op("sync", "dma_start_transpose", out=rcb[:],
                    in_=sc["rc_sp"][:, i * 128:(i + 1) * 128],
                    waits=[t_rc_done] + ([rw[buf]] if rw[buf] else []), inc="qS")
        tldv = P.op("sync", "dma_start", out=vcb[:], in_=sc["vc_acc"][i * 128:(i + 1) * 128, :],
                    waits=[t_vc_done] + ([vw[buf]] if vw[buf] else [P.tok("cS")]), inc="qS")
        tm1 = P.op("vector", "tensor_tensor", sb["x2_f32"][:], vcb[:], rcb[:], op=ALU.mult,
                   waits=[tldr, tldv] + ([xw[buf]] if xw[buf] else []), inc="cV")
        tm2 = P.op("vector", "tensor_tensor", sb["x2_f32"][:], xs[:], sb["x2_f32"][:],
                   op=ALU.add, waits=[tldx], inc="cV")
        two = P.op("sync", "dma_start", out=sc["xres"][i * 128:(i + 1) * 128, :],
                   in_=sb["x2_f32"][:], waits=[tm2], inc="qS")
        xw[buf] = tm2
        rw[buf] = tm1
        vw[buf] = tm1
    return P.tok("qS")


def alloc_p0(nc, es, pfx="p0_"):
    sb = {}
    def S(name, shape, dt):
        sb[name] = es.enter_context(nc.sbuf_tensor(pfx + name, shape, dt))
        return sb[name]
    S("m_bcastT", [128, T], F32)
    S("gA", [128, D], BF16); S("bA", [128, D], BF16)
    S("gB", [128, D], BF16); S("bB", [128, D], BF16)
    S("tf_cols", [128, K], F32); S("itf_cols", [128, K], F32)
    S("td_cols", [128, K], F32); S("boT_cols", [128, K], F32)
    S("xin0", [128, D], F32); S("xin1", [128, D], F32)
    S("u_f32", [128, D], F32); S("x1_f32", [128, D], F32); S("x2_f32", [128, D], F32)
    S("xT_g", [128, K * GW], BF16)
    S("decT", [128, 512], F32)
    S("wcb0", [128, K * 128], BF16); S("wcb1", [128, K * 128], BF16)
    S("wcb2", [128, K * 128], BF16); S("wcb3", [128, K * 128], BF16)
    S("wrp", [128, K * 128], BF16)
    S("vmixP", [128, NB * G4], F32); S("vmixT", [128, K], F32); S("vmixT_bf", [128, K], BF16)
    S("twoT_cols", [128, NB], F32)
    S("sig0", [128, 512], F32); S("sig1", [128, 512], F32)
    S("touT_buf", [128, T], BF16)
    S("tou0", [128, D], BF16); S("tou1", [128, D], BF16)
    S("kcT", [128, 4 * T], BF16)
    S("wcv_s", [128, 4 * 2048], BF16)
    S("vst0", [128, D], BF16); S("vst1", [128, D], BF16)
    for tag in ("a", "b"):
        for c in ("s1", "s2", "mn", "nv", "ve", "rs", "nm"):
            S(c + tag, [128, 1], F32)
    return sb


class Chain:
    def __init__(self, P, eng_default="vector"):
        self.P = P
        self.t = None

    def op(self, eng, meth, *args, waits=(), **kw):
        ws = list(waits) + ([self.t] if self.t else [])
        inc = {"vector": "cV", "scalar": "cS", "gpsimd": "cG", "sync": "qS",
               "tensor": "cT"}[eng]
        self.t = self.P.op(eng, meth, *args, waits=ws, inc=inc, **kw)
        return self.t


def build_identity(P, sb):
    t1 = P.op("gpsimd", "iota", sb["ident_f32"][:].bitcast(I32), pattern=[[1, 128]],
              base=0, channel_multiplier=-1, inc="cG")
    t2 = P.op("vector", "tensor_scalar", sb["ident_f32"][:], sb["ident_f32"][:].bitcast(I32),
              0, None, op0=ALU.is_equal, waits=[t1], inc="cV")
    return t2


def emit_p1(P, nc, w, sc, sb, ps, pre_toks):
    EPSO = 1e-5
    ln = LN(P, sb)
    NSW = 1
    SWT = T // NSW
    KH2 = 1408 // 128

    t_id = build_identity(P, sb)
    t_gw = P.op("sync", "dma_start", out=sb["u_f32"][0:16, :], in_=w["gate_w"][:, :], inc="qS")
    gw_toks = []
    for k in range(K):
        tt = P.op("tensor", "transpose", ps["A0"][:, 0:16], sb["u_f32"][0:16, k * 128:(k + 1) * 128],
                  sb["ident_f32"][0:16, 0:16], waits=[t_gw, t_id] + gw_toks[-1:], inc="cT")
        te = P.op("vector", "tensor_copy", sb["gwT"][:, k * 16:(k + 1) * 16], ps["A0"][:, 0:16],
                  waits=[tt], inc="cV")
        gw_toks.append(te)

    xw = [None, None]
    guards_u = None
    sp_rd = None
    for i in range(TIL):
        buf = i % 2
        xs = sb["xin0"] if buf == 0 else sb["xin1"]
        tld = P.op("sync", "dma_start", out=xs[:], in_=sc["xres"][i * 128:(i + 1) * 128, :],
                   waits=([xw[buf]] if xw[buf] else []), inc="qS")
        tpre, _ = ln.full(xs[:], sb["u_f32"][:], xs[:], sb["u_f32"][:], D, EPSO, "a",
                          [tld] + gw_toks[-1:] +
                          ([guards_u] if guards_u else []) + ([sp_rd] if sp_rd else []))
        tsp = P.op("gpsimd", "dma_start", out=sc["spill_a"][i * 128:(i + 1) * 128, :],
                   in_=sb["u_f32"][:], waits=[tpre], inc="qG")
        sp_rd = tsp
        guards_u = tpre
        xw[buf] = tpre
    t_spill_u_done = P.tok("qG")

    vst_w = [None, None]
    hT_guard = None
    w2_guard = None
    route_guard = None
    wbuf_w = [None, None]
    p_ev = {"A0": None, "A1": None, "Ba": None, "Bb": None}
    for s in range(NSW):
        uT_tok = []
        for k in range(K):
            tt = P.op("sync", "dma_start_transpose",
                      out=sb["uT"][:, k * SWT:(k + 1) * SWT],
                      in_=sc["spill_a"][s * SWT:(s + 1) * SWT, k * 128:(k + 1) * 128],
                      waits=[t_spill_u_done] + ([hT_guard] if hT_guard and k == 0 else []),
                      inc="qS")
            uT_tok.append(tt)
        for ii in range(SWT // 128):
            i_abs = s * (SWT // 128) + ii
            pbuf = ii % 2
            pr = ps["A0"] if pbuf == 0 else ps["A1"]
            prname = "A0" if pbuf == 0 else "A1"
            mwaits = uT_tok + gw_toks + ([p_ev[prname]] if p_ev[prname] else [])
            for k in range(K):
                tok_mm = P.op("tensor", "matmul", pr[:, 0:16],
                              sb["uT"][:, k * SWT + ii * 128: k * SWT + (ii + 1) * 128],
                              sb["gwT"][:, k * 16:(k + 1) * 16],
                              start=(k == 0), stop=(k == K - 1),
                              waits=(mwaits if k == 0 else []),
                              inc=("cT" if k == K - 1 else None))
            ch = Chain(P)
            sco = sb["rsc0"]
            ch.op("scalar", "activation", sco[:], pr[:, 0:16], AF.Sigmoid,
                  waits=[tok_mm] + ([route_guard] if route_guard else []))
            p_ev[prname] = ch.t
            sg = sco[:].rearrange("p (g j) -> p g j", j=4)
            ev = sco[:].rearrange("p (g two) -> p g two", two=2)
            ch.op("vector", "tensor_tensor", sb["r8a"][:].rearrange("p (g one) -> p g one", one=1),
                  ev[:, :, 0:1], ev[:, :, 1:2], op=ALU.max)
            ch.op("vector", "tensor_tensor", sb["r8b"][:].rearrange("p (g one) -> p g one", one=1),
                  ev[:, :, 0:1], ev[:, :, 1:2], op=ALU.min)
            mxp = sb["r8a"][:].rearrange("p (g two) -> p g two", two=2)
            mnp = sb["r8b"][:].rearrange("p (g two) -> p g two", two=2)
            ch.op("vector", "tensor_tensor", sb["r4a"][:].rearrange("p (g one) -> p g one", one=1),
                  mxp[:, :, 0:1], mxp[:, :, 1:2], op=ALU.max)
            ch.op("vector", "tensor_tensor", sb["r4b"][:].rearrange("p (g one) -> p g one", one=1),
                  mxp[:, :, 0:1], mxp[:, :, 1:2], op=ALU.min)
            ch.op("vector", "tensor_tensor", sb["r4c"][:].rearrange("p (g one) -> p g one", one=1),
                  mnp[:, :, 0:1], mnp[:, :, 1:2], op=ALU.max)
            ch.op("vector", "tensor_tensor", sb["r4b"][:], sb["r4b"][:], sb["r4c"][:], op=ALU.max)
            ch.op("vector", "tensor_tensor", sb["r4a"][:], sb["r4a"][:], sb["r4b"][:], op=ALU.add)
            for gi in range(4):
                ch.op("vector", "tensor_scalar", sb["r4b"][:], sb["r4a"][:],
                      sb["r4a"][:, gi:gi + 1], None, op0=ALU.is_gt)
                ch.op("vector", "tensor_reduce", sb["r1a"][:], sb["r4b"][:], axis=AX.X, op=ALU.add)
                if gi > 0:
                    ch.op("vector", "tensor_scalar", sb["r4c"][:, 0:gi], sb["r4a"][:, 0:gi],
                          sb["r4a"][:, gi:gi + 1], None, op0=ALU.is_equal)
                    ch.op("vector", "tensor_reduce", sb["r1b"][:], sb["r4c"][:, 0:gi],
                          axis=AX.X, op=ALU.add)
                    ch.op("vector", "tensor_tensor", sb["r1a"][:], sb["r1a"][:], sb["r1b"][:],
                          op=ALU.add)
                ch.op("vector", "tensor_scalar", sb["gmask"][:, gi:gi + 1], sb["r1a"][:],
                      1.5, None, op0=ALU.is_lt)
            ch.op("vector", "tensor_tensor", sb["mske"][:].rearrange("p (g j) -> p g j", j=4),
                  sg, sb["gmask"][:].unsqueeze(2).broadcast_to([128, 4, 4]), op=ALU.mult)
            for e in range(16):
                ch.op("vector", "tensor_scalar", sb["r16a"][:], sb["mske"][:],
                      sb["mske"][:, e:e + 1], None, op0=ALU.is_gt)
                ch.op("vector", "tensor_reduce", sb["r1a"][:], sb["r16a"][:], axis=AX.X, op=ALU.add)
                if e > 0:
                    ch.op("vector", "tensor_scalar", sb["r16a"][:, 0:e], sb["mske"][:, 0:e],
                          sb["mske"][:, e:e + 1], None, op0=ALU.is_equal)
                    ch.op("vector", "tensor_reduce", sb["r1b"][:], sb["r16a"][:, 0:e],
                          axis=AX.X, op=ALU.add)
                    ch.op("vector", "tensor_tensor", sb["r1a"][:], sb["r1a"][:], sb["r1b"][:],
                          op=ALU.add)
                ch.op("vector", "tensor_scalar", sb["sel"][:, e:e + 1], sb["r1a"][:],
                      1.5, None, op0=ALU.is_lt)
            ch.op("vector", "tensor_tensor", sb["mske"][:], sb["mske"][:], sb["sel"][:], op=ALU.mult)
            ch.op("vector", "tensor_reduce", sb["r1a"][:], sb["mske"][:], axis=AX.X, op=ALU.add)
            ch.op("vector", "tensor_scalar", sb["r1a"][:], sb["r1a"][:], 1e-20, None, op0=ALU.add)
            ch.op("vector", "reciprocal", sb["r1b"][:], sb["r1a"][:])
            ch.op("vector", "tensor_scalar", sb["combine"][:, i_abs * 16:(i_abs + 1) * 16],
                  sb["mske"][:], sb["r1b"][:], None, op0=ALU.mult)
            route_guard = ch.t
        t_routing = P.tok("cV")

        for ei in range(17):
            shared = (ei == 0)
            e = ei - 1
            w1d = w["sw1"][0] if shared else w["w1"][e]
            w2d = w["sw2"][0] if shared else w["w2"][e]
            for nb in range(KH2):
                buf = nb % 2
                wcb = sb["wcb0"] if buf == 0 else sb["wcb1"]
                cbsrc = w1d[:, nb * 128:(nb + 1) * 128].rearrange("(k p) n -> p k n", p=128)
                tw = P.op("gpsimd", "dma_start",
                          out=wcb[:].rearrange("p (k n) -> p k n", n=128), in_=cbsrc,
                          waits=([wbuf_w[buf]] if wbuf_w[buf] else []), inc="qG")
                for g2 in range(SWT // 512):
                    pbuf = g2 % 2
                    pk = ps["B0"] if pbuf == 0 else ps["B1"]
                    pkn = "Ba" if pbuf == 0 else "Bb"
                    mwaits = [tw] + uT_tok + ([p_ev[pkn]] if p_ev[pkn] else []) + \
                             ([hT_guard] if hT_guard and nb == 0 and g2 == 0 else [])
                    for k in range(K):
                        tok_mm = P.op("tensor", "matmul", pk[:, 0:512],
                                      wcb[:, k * 128:(k + 1) * 128],
                                      sb["uT"][:, k * SWT + g2 * 512: k * SWT + (g2 + 1) * 512],
                                      start=(k == 0), stop=(k == K - 1),
                                      waits=(mwaits if k == 0 else []),
                                      inc=("cT" if k == K - 1 else None))
                    tsl = P.op("scalar", "activation",
                               sb["hT"][:, nb * SWT + g2 * 512: nb * SWT + (g2 + 1) * 512],
                               pk[:, 0:512], AF.Silu, waits=[tok_mm], inc="cS")
                    p_ev[pkn] = tsl
                wbuf_w[buf] = tok_mm
            t_hT = P.tok("cS")
            tw2 = P.op("gpsimd", "dma_start",
                       out=sb["w2sb"][:].rearrange("p (k n) -> p k n", n=2048),
                       in_=w2d[:, :].rearrange("(k p) n -> p k n", p=128),
                       waits=([w2_guard] if w2_guard else []), inc="qG")
            for ii in range(SWT // 128):
                i_abs = s * (SWT // 128) + ii
                pbuf = ii % 2
                pvc = ps["BIGa"] if pbuf == 0 else ps["BIGb"]
                vstage = sb["vst0"] if pbuf == 0 else sb["vst1"]
                tevs = []
                for half in range(2):
                    mwaits = [tw2, t_hT] + ([p_ev["G" + str(pbuf)]] if p_ev.get("G" + str(pbuf)) else [])
                    for k2 in range(KH2):
                        for nch in range(2):
                            nc_abs = half * 2 + nch
                            tok_mm = P.op("tensor", "matmul", pvc[:, nch * 512:(nch + 1) * 512],
                                          sb["hT"][:, k2 * SWT + ii * 128: k2 * SWT + (ii + 1) * 128],
                                          sb["w2sb"][:, k2 * 2048 + nc_abs * 512: k2 * 2048 + (nc_abs + 1) * 512],
                                          start=(k2 == 0), stop=(k2 == KH2 - 1),
                                          waits=(mwaits if (k2 == 0 and nch == 0) else []),
                                          inc=("cT" if (k2 == KH2 - 1 and nch == 1) else None))
                    if shared:
                        tev = P.op("scalar", "activation", vstage[:, half * 1024:(half + 1) * 1024],
                                   pvc[:], AF.Identity,
                                   waits=[tok_mm] + ([vst_w[pbuf]] if vst_w[pbuf] and half == 0 else []),
                                   inc="cS")
                    else:
                        tev = P.op("scalar", "activation", vstage[:, half * 1024:(half + 1) * 1024],
                                   pvc[:], AF.Identity,
                                   scale=sb["combine"][:, i_abs * 16 + e: i_abs * 16 + e + 1],
                                   waits=[tok_mm, t_routing] +
                                         ([vst_w[pbuf]] if vst_w[pbuf] and half == 0 else []),
                                   inc="cS")
                    p_ev["G" + str(pbuf)] = tev
                    tevs.append(tev)
                kw = {} if shared else dict(accum_op=ALU.add)
                tacc = P.op("gpsimd", "dma_start",
                            out=sc["moe_acc"][i_abs * 128:(i_abs + 1) * 128, :],
                            in_=vstage[:], waits=tevs, inc="qG", **kw)
                vst_w[pbuf] = tacc
            hT_guard = tok_mm
            w2_guard = tok_mm
    t_acc_done = P.tok("qG")

    xw = [None, None]
    vw = [None, None]
    for i in range(TIL):
        buf = i % 2
        xs = sb["xin0"] if buf == 0 else sb["xin1"]
        vb = sb["vst0"] if buf == 0 else sb["vst1"]
        tldx = P.op("sync", "dma_start", out=xs[:], in_=sc["xres"][i * 128:(i + 1) * 128, :],
                    waits=([xw[buf]] if xw[buf] else [P.tok("cS")]), inc="qS")
        tldv = P.op("sync", "dma_start", out=vb[:], in_=sc["moe_acc"][i * 128:(i + 1) * 128, :],
                    waits=[t_acc_done] + ([vw[buf]] if vw[buf] else [P.tok("cS")]), inc="qS")
        tm = P.op("vector", "tensor_tensor", sb["x2_f32"][:], xs[:], vb[:], op=ALU.add,
                  waits=[tldx, tldv] + ([xw[buf]] if xw[buf] else []), inc="cV")
        two = P.op("sync", "dma_start", out=sc["xres_out"][i * 128:(i + 1) * 128, :],
                   in_=sb["x2_f32"][:], waits=[tm], inc="qS")
        xw[buf] = tm
        vw[buf] = tm
    return P.tok("qS")


def alloc_p1(nc, es, pfx="p1_"):
    sb = {}
    def S(name, shape, dt):
        sb[name] = es.enter_context(nc.sbuf_tensor(pfx + name, shape, dt))
        return sb[name]
    S("ident_f32", [128, 128], F32)
    S("gwT", [128, 16 * 16], BF16)
    S("xin0", [128, D], F32); S("xin1", [128, D], F32)
    S("u_f32", [128, D], F32); S("x2_f32", [128, D], F32)
    S("uT", [128, 16 * 2048], BF16)
    S("hT", [128, 11 * 2048], BF16)
    S("wcb0", [128, 16 * 128], BF16); S("wcb1", [128, 16 * 128], BF16)
    S("w2sb", [128, 11 * 2048], BF16)
    S("vst0", [128, D], BF16); S("vst1", [128, D], BF16)
    S("rsc0", [128, 16], F32)
    S("r8a", [128, 8], F32); S("r8b", [128, 8], F32)
    S("r4a", [128, 4], F32); S("r4b", [128, 4], F32); S("r4c", [128, 4], F32)
    S("r1a", [128, 1], F32); S("r1b", [128, 1], F32)
    S("gmask", [128, 4], F32); S("mske", [128, 16], F32); S("r16a", [128, 16], F32)
    S("sel", [128, 16], F32)
    S("combine", [128, 16 * 16], F32)
    for tag in ("a", "b"):
        for c in ("s1", "s2", "mn", "nv", "ve", "rs", "nm"):
            S(c + tag, [128, 1], F32)
    return sb


NH = 64
HD = 64
DST = 128
DI = 4096
CONVD = 4352
DINPROJ = 8512
CH = 128
NCH = T // CH


def emit_p2(P, nc, w, sc, sb, ps, es):
    EPSO = 1e-5
    EPSR = 1e-5
    ln = LN(P, sb)
    t_id = build_identity(P, sb)
    t_idb = P.op("vector", "tensor_copy", sb["ident_bf"][:], sb["ident_f32"][:],
                 waits=[t_id], inc="cV")

    t_Ac = P.op("sync", "dma_start", out=sb["A_col"][0:64, :],
                in_=w["A"][0:64].rearrange("(p one) -> p one", one=1), inc="qS")
    t_db = P.op("sync", "dma_start", out=sb["dtb_col"][0:64, :],
                in_=w["dt_bias"][0:64].rearrange("(p one) -> p one", one=1), inc="qS")
    t_Db = bcast_row(P, w["D"], sb["D_b"], 64)
    t_tr = P.op("gpsimd", "iota", sb["tril"][:].bitcast(I32), pattern=[[1, 128]],
                base=0, channel_multiplier=-1, inc="cG")
    t_tr = P.op("vector", "tensor_scalar", sb["tril"][:], sb["tril"][:].bitcast(I32),
                0, None, op0=ALU.is_lt, waits=[t_tr], inc="cV")
    t_tr = P.op("vector", "tensor_scalar", sb["tril"][:], sb["tril"][:],
                -1e30, None, op0=ALU.mult, waits=[t_tr], inc="cV")
    t_on = P.op("vector", "memset", sb["ones64"][:], 1.0, inc="cV")
    t_o1 = P.op("vector", "memset", sb["onesrow"][:], 1.0, inc="cV")

    with contextlib_ExitStack() as es1:
        s1 = {}
        def S(name, shape, dt):
            s1[name] = es1.enter_context(nc.sbuf_tensor("p2a_" + name, shape, dt))
        S("xin0", [128, D], F32); S("xin1", [128, D], F32)
        S("u_f32", [128, D], F32)
        S("uT", [128, K * T], BF16)
        S("wcb0", [128, K * 128], BF16); S("wcb1", [128, K * 128], BF16)
        S("stg0", [128, 2048], BF16); S("stg1", [128, 2048], BF16)
        S("cv0", [128, GP + T], BF16); S("cv1", [128, GP + T], BF16)
        S("cw4", [128, 34 * 4], F32); S("cb1", [128, 34], F32)
        S("cvc", [128, T], BF16)
        S("cvo0", [128, T], BF16); S("cvo1", [128, T], BF16)

        xw = [None, None]
        gu = None
        sp_rd = None
        for i in range(TIL):
            buf = i % 2
            xs = s1["xin0"] if buf == 0 else s1["xin1"]
            tld = P.op("sync", "dma_start", out=xs[:], in_=sc["xres"][i * 128:(i + 1) * 128, :],
                       waits=([xw[buf]] if xw[buf] else []), inc="qS")
            tpre, _ = ln.full(xs[:], s1["u_f32"][:], xs[:], s1["u_f32"][:], D, EPSO, "a",
                              [tld] + ([gu] if gu else []) + ([sp_rd] if sp_rd else []))
            tsp = P.op("gpsimd", "dma_start", out=sc["spill_a"][i * 128:(i + 1) * 128, :],
                       in_=s1["u_f32"][:], waits=[tpre], inc="qG")
            sp_rd = tsp
            gu = tpre
            xw[buf] = tpre
        t_sp_u = P.tok("qG")

        uT_tok = []
        for k in range(K):
            uT_tok.append(P.op("sync", "dma_start_transpose",
                               out=s1["uT"][:, k * T:(k + 1) * T],
                               in_=sc["spill_a"][:, k * 128:(k + 1) * 128],
                               waits=[t_sp_u], inc="qS"))

        wbuf_w = [None, None]
        p_ev = [None, None]
        stg_rd = [None, None]
        for nb in range(66):
            buf = nb % 2
            wcb = s1["wcb0"] if buf == 0 else s1["wcb1"]
            tw = P.op("gpsimd", "dma_start",
                      out=wcb[:].rearrange("p (k n) -> p k n", n=128),
                      in_=w["W_in"][:, nb * 128:(nb + 1) * 128].rearrange("(k p) n -> p k n", p=128),
                      waits=([wbuf_w[buf]] if wbuf_w[buf] else []), inc="qG")
            stg = s1["stg0"] if buf == 0 else s1["stg1"]
            evs = []
            for g in range(G4):
                pbuf = g % 2
                pk = ps["A0"] if pbuf == 0 else ps["A1"]
                mwaits = [tw] + uT_tok + ([p_ev[pbuf]] if p_ev[pbuf] else []) + \
                         ([stg_rd[buf]] if stg_rd[buf] and g == 0 else [])
                for k in range(K):
                    tok_mm = P.op("tensor", "matmul", pk[:, 0:512],
                                  wcb[:, k * 128:(k + 1) * 128],
                                  s1["uT"][:, k * T + g * 512: k * T + (g + 1) * 512],
                                  start=(k == 0), stop=(k == K - 1),
                                  waits=(mwaits if k == 0 else []),
                                  inc=("cT" if k == K - 1 else None))
                tev = P.op("scalar", "activation", stg[:, g * 512:(g + 1) * 512],
                           pk[:, 0:512], AF.Identity, waits=[tok_mm], inc="cS")
                p_ev[pbuf] = tev
                evs.append(tev)
            dstt = sc["zT_sp"] if nb < 32 else sc["xbcT_sp"]
            roff = nb * 128 if nb < 32 else (nb - 32) * 128
            tsp = P.op("sync", "dma_start", out=dstt[roff:roff + 128, :], in_=stg[:],
                       waits=evs, inc="qS")
            stg_rd[buf] = tsp
            wbuf_w[buf] = tok_mm
        tw = P.op("gpsimd", "dma_start",
                  out=s1["wcb0"][:, 0:1024].rearrange("p (k n) -> p k n", n=64),
                  in_=w["W_in"][:, 8448:8512].rearrange("(k p) n -> p k n", p=128),
                  waits=([wbuf_w[0]] if wbuf_w[0] else []), inc="qG")
        for g in range(G4):
            pbuf = g % 2
            pk = ps["A0"] if pbuf == 0 else ps["A1"]
            mwaits = [tw] + uT_tok + ([p_ev[pbuf]] if p_ev[pbuf] else [])
            for k in range(K):
                tok_mm = P.op("tensor", "matmul", pk[0:64, 0:512],
                              s1["wcb0"][:, k * 64:(k + 1) * 64],
                              s1["uT"][:, k * T + g * 512: k * T + (g + 1) * 512],
                              start=(k == 0), stop=(k == K - 1),
                              waits=(mwaits if k == 0 else []),
                              inc=("cT" if k == K - 1 else None))
            tev = P.op("scalar", "activation", sb["dtT_raw"][0:64, g * 512:(g + 1) * 512],
                       pk[0:64, 0:512], AF.Identity, waits=[tok_mm], inc="cS")
            p_ev[pbuf] = tev
        t_dt_raw = P.tok("cS")
        t_zxbc_sp = P.tok("qS")

        t_cwall = P.op("sync", "dma_start", out=s1["cw4"][:].rearrange("p (r k) -> p r k", k=4),
                       in_=w["conv_w"][0:4352, :].rearrange("(r p) k -> p r k", p=128), inc="qS")
        t_cball = P.op("sync", "dma_start", out=s1["cb1"][:],
                       in_=w["conv_b"][0:4352].rearrange("(r p) -> p r", p=128), inc="qS",
                       allow_slow_non_contiguous=True)
        cvw = [None, None]
        cvw2 = [None, None]
        cvo_rd = None
        ch_dve = None
        prev_tsl = None
        for r in range(34):
            buf = r % 2
            xg = s1["cv0"] if buf == 0 else s1["cv1"]
            tzz = P.op("vector", "memset", xg[:, 0:GP], 0.0,
                       waits=([cvw[buf]] if cvw[buf] else []), inc="cV")
            tld = P.op("sync", "dma_start", out=xg[:, GP:GP + T],
                       in_=sc["xbcT_sp"][r * 128:(r + 1) * 128, :],
                       waits=[t_zxbc_sp, tzz] + ([cvw[buf]] if cvw[buf] else []), inc="qS")
            ch = Chain(P)
            ch.op("vector", "tensor_scalar", s1["cvc"][:], xg[:, GP - 3:GP - 3 + T],
                  s1["cw4"][:, r * 4:r * 4 + 1], None, op0=ALU.mult,
                  waits=[tzz, tld, t_cwall] + ([cvo_rd] if cvo_rd else []))
            for kk in range(1, 4):
                ch.op("vector", "scalar_tensor_tensor", s1["cvc"][:],
                      xg[:, GP - 3 + kk:GP - 3 + kk + T], s1["cw4"][:, r * 4 + kk:r * 4 + kk + 1],
                      s1["cvc"][:], op0=ALU.mult, op1=ALU.add)
            ch_dve = ch.t
            cvo = s1["cvo0"] if buf == 0 else s1["cvo1"]
            tsl = P.op("scalar", "activation", cvo[:], s1["cvc"][:], AF.Silu,
                       bias=s1["cb1"][:, r:r + 1],
                       waits=[ch_dve, t_cball] + ([cvw2[buf]] if cvw2[buf] else []), inc="cS")
            prev_tsl = tsl
            if r < 32:
                tso = P.op("sync", "dma_start", out=sc["xbcP_sp"][r * 128:(r + 1) * 128, :],
                           in_=cvo[:], waits=[tsl], inc="qS")
            elif r == 32:
                tso = P.op("vector", "tensor_copy", sb["BmT"][:], cvo[:],
                           waits=[tsl], inc="cV")
            else:
                tso = P.op("vector", "tensor_copy", sb["CmT"][:], cvo[:],
                           waits=[tsl], inc="cV")
            cvw[buf] = ch_dve
            cvw2[buf] = tso
            cvo_rd = tso
        t_conv = [P.tok("qS"), P.tok("cV"), P.tok("cS")]

        ch = Chain(P)
        ch.op("vector", "tensor_scalar", sb["dtT"][0:64, :], sb["dtT_raw"][0:64, :],
              sb["dtb_col"][0:64, :], None, op0=ALU.add, waits=[t_dt_raw, t_db])
        t_ab = P.op("scalar", "activation", sb["dAT"][0:64, :], sb["dtT"][0:64, :],
                    AF.Abs, waits=[ch.t], inc="cS")
        t_e = P.op("scalar", "activation", sb["dAT"][0:64, :], sb["dAT"][0:64, :],
                   AF.Exp, scale=-1.0, waits=[t_ab], inc="cS")
        t_l = P.op("scalar", "activation", sb["dAT"][0:64, :], sb["dAT"][0:64, :],
                   AF.Ln, bias=1.0, waits=[t_e], inc="cS")
        ch.op("vector", "tensor_scalar", sb["dtT"][0:64, :], sb["dtT"][0:64, :],
              0.0, None, op0=ALU.max, waits=[t_l])
        ch.op("vector", "tensor_tensor", sb["dtT"][0:64, :], sb["dtT"][0:64, :],
              sb["dAT"][0:64, :], op=ALU.add)
        ch.op("vector", "tensor_scalar", sb["dAT"][0:64, :], sb["dtT"][0:64, :],
              sb["A_col"][0:64, :], None, op0=ALU.mult, waits=[t_Ac])
        t_dA = ch.t

    barrier_all(P)
    with contextlib_ExitStack() as es2:
        s2 = {}
        def S2(name, shape, dt):
            s2[name] = es2.enter_context(nc.sbuf_tensor("p2b_" + name, shape, dt))
        S2("x_ch0", [128, DI], BF16); S2("x_ch1", [128, DI], BF16)
        S2("z_ch0", [128, DI], BF16); S2("z_ch1", [128, DI], BF16)
        S2("dtx", [128, DI], BF16); S2("ddtx", [128, DI], BF16)
        S2("Y_sb", [128, DI], F32)
        S2("state", [128, DI], F32); S2("state_bf", [128, DI], BF16)
        S2("Lb0", [128, 1024], F32); S2("Lb1", [128, 1024], F32)
        S2("Lb2", [128, 1024], F32); S2("Lb3", [128, 1024], F32)
        S2("Mp0", [128, 1024], BF16); S2("Mp1", [128, 1024], BF16)
        S2("S_sb", [128, 128], BF16)
        S2("Bm_tm", [128, 128], BF16)
        S2("Bm_f", [128, 128], F32)
        S2("csT_c", [128, 128], F32)
        S2("decT_c", [128, 128], F32)
        S2("dt_tm", [128, 64], F32)
        S2("cs_tm", [128, 64], F32)
        S2("negcs_tm", [128, 64], F32)
        S2("expcs_tm", [128, 64], F32)
        S2("dec_tm", [128, 64], F32)
        S2("cd_b", [128, 64], F32)
        S2("cdrow", [128, 64], F32)
        S2("junk4", [128, DI], BF16)
        S2("ssq", [128, 1], F32); S2("rs_c", [128, 1], F32)
        t_z0 = P.op("vector", "memset", s2["state"][:], 0.0, inc="cV")
        t_z1 = P.op("vector", "memset", s2["state_bf"][:], 0.0, inc="cV")
        xz_w = [None, None, None, None]
        yg_rd = None
        pv = {}
        for c in range(NCH):
            cb = c % 2
            x_ch = s2["x_ch0"] if cb == 0 else s2["x_ch1"]
            z_ch = s2["z_ch0"] if cb == 0 else s2["z_ch1"]
            tldx = P.op("sync", "dma_start_transpose", out=x_ch[:],
                        in_=sc["xbcP_sp"][0:DI, c * CH:(c + 1) * CH],
                        waits=t_conv + ([xz_w[cb]] if xz_w[cb] else []), inc="qS")
            tldz = P.op("sync", "dma_start_transpose", out=z_ch[:],
                        in_=sc["zT_sp"][0:DI, c * CH:(c + 1) * CH],
                        waits=[t_zxbc_sp] + ([xz_w[2 + cb]] if xz_w[2 + cb] else []), inc="qS")
            t_cs = P.op("vector", "tensor_tensor_scan", s2["csT_c"][0:64, :],
                        sb["ones64"][0:64, :], sb["dAT"][0:64, c * CH:(c + 1) * CH],
                        0.0, op0=ALU.mult, op1=ALU.add,
                        waits=[t_dA, t_on] + pv.get("cs_rd", []), inc="cV")
            t_dc = P.op("vector", "tensor_scalar", s2["decT_c"][0:64, :], s2["csT_c"][0:64, :],
                        s2["csT_c"][0:64, CH - 1:CH], -1.0, op0=ALU.subtract, op1=ALU.mult,
                        waits=[t_cs] + pv.get("dec_rd", []), inc="cV")
            t_dec = P.op("scalar", "activation", s2["decT_c"][0:64, :], s2["decT_c"][0:64, :],
                         AF.Exp, waits=[t_dc], inc="cS")
            t_csw = P.op("sync", "dma_start",
                         out=sc["csrow_sp"][c, :].rearrange("(p f) -> p f", p=64),
                         in_=s2["csT_c"][0:64, :], waits=[t_cs], inc="qS")
            trg = pv.get("pC0", pv.get("trB0"))
            trw = []
            for (src, dst, wtok) in ((sb["dtT"][0:64, c * CH:(c + 1) * CH], s2["dt_tm"], [t_dA]),
                                     (s2["csT_c"][0:64, :], s2["cs_tm"], [t_cs]),
                                     (s2["decT_c"][0:64, :], s2["dec_tm"], [t_dec])):
                tt = P.op("tensor", "transpose", ps["B0"][:, 0:64], src,
                          sb["ident_f32"][0:64, 0:64],
                          waits=wtok + [t_id] + ([pv.get("trB0")] if pv.get("trB0") else []) +
                                ([trg] if trg else []),
                          inc="cT")
                trg = None
                ew = []
                if dst is s2["cs_tm"] and pv.get("cste_g"):
                    ew = [pv["cste_g"]]
                te = P.op("vector", "tensor_copy", dst[:], ps["B0"][:, 0:64],
                          waits=[tt] + ew, inc="cV")
                pv["trB0"] = te
                trw.append(te)
            t_dttm, t_cstm, t_dectm = trw
            t_ecs = P.op("scalar", "activation", s2["expcs_tm"][:], s2["cs_tm"][:], AF.Exp,
                         waits=[t_cstm] + pv.get("ecs_rd", []), inc="cS")
            pv["cste_g"] = t_ecs
            t_ncs = P.op("vector", "tensor_scalar", s2["negcs_tm"][:], s2["cs_tm"][:],
                         -1.0, None, op0=ALU.mult,
                         waits=[t_cstm] + pv.get("ncs_rd", []), inc="cV")
            tt = P.op("tensor", "transpose", ps["B1"][0:1, 0:64],
                      s2["csT_c"][0:64, CH - 1:CH], sb["ident_f32"][0:64, 0:64],
                      waits=[t_cs] + ([pv.get("trB1")] if pv.get("trB1") else []) +
                            ([pv.get("pC1")] if pv.get("pC1") else []),
                      inc="cT")
            t_cdr = P.op("scalar", "activation", s2["cdrow"][0:1, 0:64], ps["B1"][0:1, 0:64],
                         AF.Exp, waits=[tt] + pv.get("cdrow_rd", []), inc="cS")
            tt2 = P.op("tensor", "matmul", ps["B1"][:, 64:128],
                       sb["onesrow"][0:1, :], s2["cdrow"][0:1, 0:64],
                       start=True, stop=True,
                       waits=[t_cdr, t_o1] + ([pv.get("cdb_ev")] if pv.get("cdb_ev") else []),
                       inc="cT")
            pv["cdrow_rd"] = [tt2]
            t_cdb = P.op("vector", "tensor_copy", s2["cd_b"][:], ps["B1"][:, 64:128],
                         waits=[tt2] + pv.get("cdb_rd", []), inc="cV")
            pv["cdb_ev"] = t_cdb
            pv["trB1"] = t_cdb
            tmmS = P.op("tensor", "matmul", ps["A0"][:, 0:128],
                        sb["BmT"][:, c * CH:(c + 1) * CH], sb["CmT"][:, c * CH:(c + 1) * CH],
                        start=True, stop=True,
                        waits=t_conv + ([pv.get("S_ev")] if pv.get("S_ev") else []), inc="cT")
            t_S = P.op("vector", "tensor_copy", s2["S_sb"][:], ps["A0"][:, 0:128],
                       waits=[tmmS] + pv.get("S_rd", []), inc="cV")
            pv["S_ev"] = t_S
            t_bf = P.op("vector", "tensor_copy", s2["Bm_f"][:], sb["BmT"][:, c * CH:(c + 1) * CH],
                        waits=t_conv + pv.get("Bmf_rd", []), inc="cV")
            tmmB = P.op("tensor", "transpose", ps["A1"][:, 0:128], s2["Bm_f"][:],
                        sb["ident_f32"][:],
                        waits=[t_bf] + ([pv.get("Bm_ev")] if pv.get("Bm_ev") else []), inc="cT")
            pv["Bmf_rd"] = [tmmB]
            t_Bm = P.op("vector", "tensor_copy", s2["Bm_tm"][:], ps["A1"][:, 0:128],
                        waits=[tmmB] + pv.get("Bm_rd", []), inc="cV")
            pv["Bm_ev"] = t_Bm
            t_dtx = P.op("vector", "tensor_tensor",
                         s2["dtx"][:].rearrange("p (h q) -> p h q", q=HD),
                         x_ch[:].rearrange("p (h q) -> p h q", q=HD),
                         s2["dt_tm"][:].unsqueeze(2).broadcast_to([128, NH, HD]),
                         op=ALU.mult, waits=[tldx, t_dttm] + pv.get("dtx_rd", []), inc="cV")
            t_ddtx = P.op("vector", "tensor_tensor",
                          s2["ddtx"][:].rearrange("p (h q) -> p h q", q=HD),
                          s2["dtx"][:].rearrange("p (h q) -> p h q", q=HD),
                          s2["dec_tm"][:].unsqueeze(2).broadcast_to([128, NH, HD]),
                          op=ALU.mult, waits=[t_dtx, t_dectm] + pv.get("ddtx_rd", []), inc="cV")
            yg_evs = []
            last_texp = None
            for g8 in range(8):
                gb = g8 % 2
                gb4 = g8 % 4
                Lb = s2["Lb" + str(gb4)]
                Mp = s2["Mp0"] if gb == 0 else s2["Mp1"]
                tbc = P.op("gpsimd", "dma_start", out=Lb[:],
                           in_=sc["csrow_sp"][c, g8 * 1024:(g8 + 1) * 1024].partition_broadcast(128),
                           waits=[t_csw] + ([pv.get("Lb" + str(gb4))] if pv.get("Lb" + str(gb4)) else []),
                           inc="qG")
                t_msk = P.op("vector", "tensor_tensor",
                             Lb[:].rearrange("p (h i) -> p h i", i=CH),
                             Lb[:].rearrange("p (h i) -> p h i", i=CH),
                             sb["tril"][:].unsqueeze(1).broadcast_to([128, 8, CH]),
                             op=ALU.add, waits=[tbc, t_tr], inc="cV")
                exps = []
                for h8 in range(8):
                    h = g8 * 8 + h8
                    texp = P.op("scalar", "activation", Mp[:, h8 * CH:(h8 + 1) * CH],
                                Lb[:, h8 * CH:(h8 + 1) * CH], AF.Exp,
                                bias=s2["negcs_tm"][:, h:h + 1],
                                waits=[t_msk, t_ncs] +
                                      ([pv.get("Mp" + str(gb))] if pv.get("Mp" + str(gb)) and h8 == 0 else []),
                                inc="cS")
                    exps.append(texp)
                last_texp = exps[-1]
                t_mp = P.op("vector", "tensor_tensor",
                            Mp[:].rearrange("p (h i) -> p h i", i=CH),
                            Mp[:].rearrange("p (h i) -> p h i", i=CH),
                            s2["S_sb"][:].unsqueeze(1).broadcast_to([128, 8, CH]),
                            op=ALU.mult, waits=exps + [t_S], inc="cV")
                pv["Lb" + str(gb4)] = t_mp
                pA = ps["BIGa"] if gb == 0 else ps["BIGb"]
                mwA = [t_mp, t_dtx] + ([pv.get("pA" + str(gb))] if pv.get("pA" + str(gb)) else [])
                for h8 in range(8):
                    h = g8 * 8 + h8
                    P.op("tensor", "matmul", pA[:, h8 * HD:(h8 + 1) * HD],
                         Mp[:, h8 * CH:(h8 + 1) * CH],
                         s2["dtx"][:, h * HD:(h + 1) * HD],
                         start=(h8 == 0), stop=False,
                         waits=(mwA if h8 == 0 else []), skip_group_check=True)
                for h8 in range(8):
                    h = g8 * 8 + h8
                    tmmA = P.op("tensor", "matmul", pA[:, 512 + h8 * HD: 512 + (h8 + 1) * HD],
                                sb["CmT"][:, c * CH:(c + 1) * CH],
                                s2["state_bf"][:, h * HD:(h + 1) * HD],
                                start=(h8 == 0), stop=(h8 == 7), skip_group_check=True,
                                waits=(([t_z1] if c == 0 else [pv["sbf"]]) if h8 == 0 else []),
                                inc=("cT" if h8 == 7 else None))
                pv["Mp" + str(gb)] = tmmA
                t_y1 = P.op("vector", "tensor_tensor",
                            s2["Y_sb"][:, g8 * 512:(g8 + 1) * 512].rearrange("p (h q) -> p h q", q=HD),
                            pA[:, 512:1024].rearrange("p (h q) -> p h q", q=HD),
                            s2["expcs_tm"][:, g8 * 8:(g8 + 1) * 8].unsqueeze(2).broadcast_to([128, 8, HD]),
                            op=ALU.mult, waits=[tmmA, t_ecs], inc="cV")
                t_y2 = P.op("vector", "tensor_tensor", s2["Y_sb"][:, g8 * 512:(g8 + 1) * 512],
                            s2["Y_sb"][:, g8 * 512:(g8 + 1) * 512], pA[:, 0:512],
                            op=ALU.add, waits=[t_y1], inc="cV")
                pv["pA" + str(gb)] = t_y2
                yg_evs.append(t_y2)
            pv["ecs_rd"] = [yg_evs[-1]]
            pv["ncs_rd"] = [last_texp]
            st_upds = []
            for sk in range(8):
                skb = sk % 2
                pC = ps["B0"] if skb == 0 else ps["B1"]
                extra = [pv.get("pC" + str(skb))] if pv.get("pC" + str(skb)) else []
                if skb == 0:
                    extra += [pv["trB0"]]
                else:
                    extra += [t_cdb, t_cdr]
                tmmC = P.op("tensor", "matmul", pC[:, 0:512],
                            s2["Bm_tm"][:], s2["ddtx"][:, sk * 512:(sk + 1) * 512],
                            start=True, stop=True,
                            waits=[t_Bm, t_ddtx] + extra, inc="cT")
                t_su1 = P.op("vector", "tensor_tensor",
                             s2["state"][:, sk * 512:(sk + 1) * 512].rearrange("p (h q) -> p h q", q=HD),
                             s2["state"][:, sk * 512:(sk + 1) * 512].rearrange("p (h q) -> p h q", q=HD),
                             s2["cd_b"][:, sk * 8:(sk + 1) * 8].unsqueeze(2).broadcast_to([128, 8, HD]),
                             op=ALU.mult,
                             waits=[t_cdb, tmmC, pv["Mp0"], pv["Mp1"]] + ([t_z0] if c == 0 else []),
                             inc="cV")
                t_su2 = P.op("vector", "tensor_tensor", s2["state"][:, sk * 512:(sk + 1) * 512],
                             s2["state"][:, sk * 512:(sk + 1) * 512], pC[:, 0:512],
                             op=ALU.add, waits=[t_su1], inc="cV")
                pv["pC" + str(skb)] = t_su2
                st_upds.append(t_su2)
            t_sbf = P.op("vector", "tensor_copy", s2["state_bf"][:], s2["state"][:],
                         waits=st_upds + [pv["Mp0"], pv["Mp1"]], inc="cV")
            pv["sbf"] = t_sbf
            pv["cs_rd"] = [t_sbf, t_csw, t_cdb]
            pv["dec_rd"] = [t_dectm]
            pv["dtx_rd"] = [t_sbf]
            pv["ddtx_rd"] = [t_sbf]
            pv["cdb_rd"] = [t_sbf]
            pv["Bm_rd"] = [t_sbf]
            pv["S_rd"] = [t for t in (pv.get("Lb0"), pv.get("Lb1"), pv.get("Lb2"), pv.get("Lb3")) if t]
            ch = Chain(P)
            ch.op("vector", "tensor_tensor",
                  s2["junk4"][:].rearrange("p (h q) -> p h q", q=HD),
                  x_ch[:].rearrange("p (h q) -> p h q", q=HD),
                  sb["D_b"][:, 0:64].unsqueeze(2).broadcast_to([128, NH, HD]),
                  op=ALU.mult, waits=[t_Db, tldx] + yg_evs + ([yg_rd] if yg_rd else []))
            ch.op("vector", "tensor_tensor", s2["Y_sb"][:], s2["Y_sb"][:], s2["junk4"][:],
                  op=ALU.add)
            t_slz = P.op("scalar", "activation", s2["junk4"][:], z_ch[:], AF.Silu,
                         waits=[tldz, ch.t], inc="cS")
            ch.op("vector", "tensor_tensor", s2["Y_sb"][:], s2["Y_sb"][:], s2["junk4"][:],
                  op=ALU.mult, waits=[t_slz])
            t_yg = ch.t
            xz_w[cb] = t_yg
            xz_w[2 + cb] = t_slz
            t_sq = P.op("scalar", "activation", s2["junk4"][:], s2["Y_sb"][:], AF.Square,
                        accum_out=s2["ssq"][:], waits=[t_yg], inc="cS")
            t_ms = P.op("vector", "tensor_scalar", s2["ssq"][:], s2["ssq"][:], 1.0 / DI, EPSR,
                        op0=ALU.mult, op1=ALU.add, waits=[t_sq], inc="cV")
            t_ln2 = P.op("scalar", "activation", s2["ssq"][:], s2["ssq"][:], AF.Ln,
                         waits=[t_ms], inc="cS")
            t_rs = P.op("scalar", "activation", s2["rs_c"][:], s2["ssq"][:], AF.Exp,
                        scale=-0.5, waits=[t_ln2] + pv.get("rs_rd", []), inc="cS")
            t_ygs = P.op("vector", "tensor_scalar", s2["junk4"][:], s2["Y_sb"][:],
                         s2["rs_c"][:], None, op0=ALU.mult, waits=[t_rs, t_sq], inc="cV")
            pv["rs_rd"] = [t_ygs]
            t_ygw = P.op("sync", "dma_start", out=sc["yg_sp"][c * CH:(c + 1) * CH, :],
                         in_=s2["junk4"][:], waits=[t_ygs], inc="qS")
            yg_rd = t_ygw
        if "dbg2" in sc:
            tv = P.tok("cV"); tsx = P.tok("cS"); ttx = P.tok("cT")
            P.op("gpsimd", "dma_start", out=sc["dbg2"][:, 0:1024], in_=s2["Mp1"][:],
                 waits=[tv, tsx, ttx], inc="qG")
            P.op("gpsimd", "dma_start", out=sc["dbg2"][:, 1024:5120], in_=s2["dtx"][:],
                 waits=[tv, tsx, ttx], inc="qG")
            P.op("gpsimd", "dma_start", out=sc["dbg2"][:, 5120:9216], in_=s2["Y_sb"][:],
                 waits=[tv, tsx, ttx], inc="qG")
            P.op("gpsimd", "dma_start", out=sc["dbg2"][:, 9216:9280], in_=s2["negcs_tm"][:],
                 waits=[tv, tsx, ttx], inc="qG")
            P.op("gpsimd", "dma_start", out=sc["dbg2"][:, 9280:9344], in_=s2["expcs_tm"][:],
                 waits=[tv, tsx, ttx], inc="qG")
            P.op("gpsimd", "dma_start", out=sc["dbg2"][:, 9344:10368], in_=s2["Lb1"][:],
                 waits=[tv, tsx, ttx], inc="qG")
            P.wait("gpsimd", P.tok("qG"))
        t_yg_done = P.tok("qS")

    barrier_all(P)
    with contextlib_ExitStack() as es3:
        s3 = {}
        def S3(name, shape, dt):
            s3[name] = es3.enter_context(nc.sbuf_tensor("p2c_" + name, shape, dt))
        S3("ygT", [128, 32 * 512], BF16)
        S3("wocb0", [128, 32 * 128], BF16); S3("wocb1", [128, 32 * 128], BF16)
        S3("wocb2", [128, 32 * 128], BF16); S3("wocb3", [128, 32 * 128], BF16)
        S3("nwcol", [128, 32], F32)
        S3("ostg0", [128, 512], BF16); S3("ostg1", [128, 512], BF16)
        S3("xin0", [128, D], F32); S3("xin1", [128, D], F32)
        S3("o0", [128, D], BF16); S3("o1", [128, D], BF16)
        S3("x2_f32", [128, D], F32)
        t_nw = load_cols(P, w["norm_w"], s3["nwcol"], 32)
        wocb_w = [None, None, None, None]
        p_ev2 = [None, None]
        ostg_rd = [None, None]
        ygT_rd = None
        for q in range(4):
            ygT_tok = []
            for k2 in range(32):
                tt = P.op("sync", "dma_start_transpose",
                          out=s3["ygT"][:, k2 * 512:(k2 + 1) * 512],
                          in_=sc["yg_sp"][q * 512:(q + 1) * 512, k2 * 128:(k2 + 1) * 128],
                          waits=[t_yg_done] + ([ygT_rd] if ygT_rd and k2 == 0 else []),
                          inc="qS")
                ygT_tok.append(tt)
            for nb in range(16):
                buf = nb % 4
                wocb = s3["wocb" + str(buf)]
                tw = P.op("gpsimd", "dma_start",
                          out=wocb[:].rearrange("p (k n) -> p k n", n=128),
                          in_=w["W_out"][:, nb * 128:(nb + 1) * 128].rearrange("(k p) n -> p k n", p=128),
                          waits=([wocb_w[buf]] if wocb_w[buf] else []), inc="qG")
                tnw = None
                for k2 in range(32):
                    tnw = P.op("vector", "tensor_scalar", wocb[:, k2 * 128:(k2 + 1) * 128],
                               wocb[:, k2 * 128:(k2 + 1) * 128], s3["nwcol"][:, k2:k2 + 1], None,
                               op0=ALU.mult,
                               waits=(([tw] + t_nw) if k2 == 0 else []),
                               inc=("cV" if k2 == 31 else None))
                pbuf = nb % 2
                pk = ps["A0"] if pbuf == 0 else ps["A1"]
                mwaits = [tnw] + ygT_tok + ([p_ev2[pbuf]] if p_ev2[pbuf] else [])
                for k2 in range(32):
                    tok_mm = P.op("tensor", "matmul", pk[:, 0:512],
                                  wocb[:, k2 * 128:(k2 + 1) * 128],
                                  s3["ygT"][:, k2 * 512:(k2 + 1) * 512],
                                  start=(k2 == 0), stop=(k2 == 31),
                                  waits=(mwaits if k2 == 0 else []),
                                  inc=("cT" if k2 == 31 else None))
                stg = s3["ostg0"] if nb % 2 == 0 else s3["ostg1"]
                tev = P.op("scalar", "activation", stg[:], pk[:, 0:512],
                           AF.Identity, waits=[tok_mm] + ([ostg_rd[nb % 2]] if ostg_rd[nb % 2] else []),
                           inc="cS")
                p_ev2[pbuf] = tev
                wocb_w[buf] = tok_mm
                tso = P.op("sync", "dma_start",
                           out=sc["attnT_sp"][nb * 128:(nb + 1) * 128, q * 512:(q + 1) * 512],
                           in_=stg[:], waits=[tev], inc="qS")
                ostg_rd[nb % 2] = tso
            ygT_rd = tok_mm
        t_attn_sp = P.tok("qS")
        xw = [None, None]
        ow = [None, None]
        for i in range(TIL):
            buf = i % 2
            xs = s3["xin0"] if buf == 0 else s3["xin1"]
            ob = s3["o0"] if buf == 0 else s3["o1"]
            tldx = P.op("sync", "dma_start", out=xs[:], in_=sc["xres"][i * 128:(i + 1) * 128, :],
                        waits=([xw[buf]] if xw[buf] else []), inc="qS")
            tldo = P.op("sync", "dma_start_transpose", out=ob[:],
                        in_=sc["attnT_sp"][:, i * 128:(i + 1) * 128],
                        waits=[t_attn_sp] + ([ow[buf]] if ow[buf] else []), inc="qS")
            tm = P.op("vector", "tensor_tensor", s3["x2_f32"][:], xs[:], ob[:], op=ALU.add,
                      waits=[tldx, tldo] + ([xw[buf]] if xw[buf] else []), inc="cV")
            two = P.op("sync", "dma_start", out=sc["xres_out"][i * 128:(i + 1) * 128, :],
                       in_=s3["x2_f32"][:], waits=[tm], inc="qS")
            xw[buf] = tm
            ow[buf] = tm
    return P.tok("qS")


import contextlib as _ctl
def contextlib_ExitStack():
    return _ctl.ExitStack()


def barrier_all(P):
    toks = [P.tok(s) for s in P.semnames if P.counts[s] > 0]
    for e in ENGINES:
        P.wait(e, *toks)


def alloc_p2_persist(nc, es, pfx="p2_"):
    sb = {}
    def S(name, shape, dt):
        sb[name] = es.enter_context(nc.sbuf_tensor(pfx + name, shape, dt))
    S("ident_f32", [128, 128], F32)
    S("ident_bf", [128, 128], BF16)
    S("A_col", [128, 1], F32); S("dtb_col", [128, 1], F32)
    S("D_b", [128, 64], F32)
    S("tril", [128, 128], F32)
    S("ones64", [128, 128], F32)
    S("onesrow", [128, 128], F32)
    S("dtT_raw", [128, T], F32)
    S("dtT", [128, T], F32)
    S("dAT", [128, T], F32)
    S("BmT", [128, T], BF16)
    S("CmT", [128, T], BF16)
    for tag in ("a",):
        for c in ("s1", "s2", "mn", "nv", "ve", "rs", "nm"):
            S(c + tag, [128, 1], F32)
    return sb


def emit_p3(P, nc, w, sc, sb, ps, es):
    EPSO = 1e-5
    EPSR = float(np.finfo(np.float32).eps)
    THR = 0.5 + 2.0 ** -10
    ln = LN(P, sb)
    t_id = build_identity(P, sb)

    with contextlib_ExitStack() as esA:
        sA = {}
        def S(name, shape, dt):
            sA[name] = esA.enter_context(nc.sbuf_tensor("p3a_" + name, shape, dt))
        S("xin0", [128, D], F32); S("xin1", [128, D], F32)
        S("u_f32", [128, D], F32)
        S("uT", [128, K * T], BF16)
        S("wraw0", [128, 2048], F32); S("wraw1", [128, 2048], F32)
        S("wbb", [128, 2048], BF16)
        S("wq", [128, 2048], F32)
        S("w1t", [128, K * 128], BF16)
        S("w1tb", [128, K * 128], BF16)
        S("gsum", [128, 32], F32); S("gbf", [128, 32], BF16)
        S("scl", [128, 32], F32); S("thr", [128, 32], F32); S("nthr", [128, 32], F32)
        S("qp", [128, 2048], BF16); S("qn", [128, 2048], BF16)
        S("rl", [128, 512], BF16); S("sqv", [128, 512], BF16)
        S("hstg", [128, T], BF16)
        S("onec", [128, 1], BF16)
        S("rsrow", [128, 2048], F32)
        S("rs_cols", [128, 16], F32)
        t_one = P.op("vector", "memset", sA["onec"][:], 1.0, inc="cV")

        xw = [None, None]
        gu = None
        sp_rd = None
        for i in range(TIL):
            buf = i % 2
            xs = sA["xin0"] if buf == 0 else sA["xin1"]
            tld = P.op("sync", "dma_start", out=xs[:], in_=sc["xres"][i * 128:(i + 1) * 128, :],
                       waits=([xw[buf]] if xw[buf] else []), inc="qS")
            tpre, _ = ln.full(xs[:], sA["u_f32"][:], xs[:], sA["u_f32"][:], D, EPSO, "a",
                              [tld] + ([gu] if gu else []) + ([sp_rd] if sp_rd else []))
            tsp = P.op("gpsimd", "dma_start", out=sc["spill_a"][i * 128:(i + 1) * 128, :],
                       in_=sA["u_f32"][:], waits=[tpre], inc="qG")
            sp_rd = tsp
            gu = tpre
            xw[buf] = tpre
        t_sp_u = P.tok("qG")
        uT_tok = []
        for k in range(K):
            uT_tok.append(P.op("sync", "dma_start_transpose",
                               out=sA["uT"][:, k * T:(k + 1) * T],
                               in_=sc["spill_a"][:, k * 128:(k + 1) * 128],
                               waits=[t_sp_u], inc="qS"))

        def quantize(wsrc_ap, n_el, ngr, waits):
            ch = Chain(P)
            ch.op("vector", "tensor_copy", sA["wbb"][:, 0:n_el], wsrc_ap, waits=waits)
            ch.op("vector", "tensor_reduce", sA["gsum"][:, 0:ngr],
                  sA["wbb"][:, 0:n_el].rearrange("p (g e) -> p g e", e=64),
                  axis=AX.X, op=ALU.add, apply_absolute_value=True)
            ch.op("vector", "tensor_scalar", sA["gbf"][:, 0:ngr], sA["gsum"][:, 0:ngr],
                  1.0 / 64, None, op0=ALU.mult)
            ch.op("vector", "tensor_scalar", sA["scl"][:, 0:ngr], sA["gbf"][:, 0:ngr],
                  1.0, 1e-8, op0=ALU.mult, op1=ALU.max)
            ch.op("vector", "tensor_scalar", sA["thr"][:, 0:ngr], sA["scl"][:, 0:ngr],
                  THR, None, op0=ALU.mult)
            ch.op("vector", "tensor_scalar", sA["nthr"][:, 0:ngr], sA["thr"][:, 0:ngr],
                  -1.0, None, op0=ALU.mult)
            ch.op("vector", "tensor_tensor", sA["qp"][:, 0:n_el].rearrange("p (g e) -> p g e", e=64),
                  sA["wbb"][:, 0:n_el].rearrange("p (g e) -> p g e", e=64),
                  sA["thr"][:, 0:ngr].unsqueeze(2).broadcast_to([128, ngr, 64]),
                  op=ALU.is_gt)
            ch.op("vector", "tensor_tensor", sA["qn"][:, 0:n_el].rearrange("p (g e) -> p g e", e=64),
                  sA["wbb"][:, 0:n_el].rearrange("p (g e) -> p g e", e=64),
                  sA["nthr"][:, 0:ngr].unsqueeze(2).broadcast_to([128, ngr, 64]),
                  op=ALU.is_lt)
            ch.op("vector", "tensor_tensor", sA["qp"][:, 0:n_el], sA["qp"][:, 0:n_el],
                  sA["qn"][:, 0:n_el], op=ALU.subtract)
            ch.op("vector", "tensor_tensor", sA["wq"][:, 0:n_el].rearrange("p (g e) -> p g e", e=64),
                  sA["qp"][:, 0:n_el].rearrange("p (g e) -> p g e", e=64),
                  sA["scl"][:, 0:ngr].unsqueeze(2).broadcast_to([128, ngr, 64]),
                  op=ALU.mult)
            return ch.t

        wld_w = [None, None]
        w1t_rd = None
        p_ev = [None, None]
        hst_rd = None
        q_guard = None
        rl_guard = None
        sq_guard = None
        for blk in range(64):
            buf = blk % 2
            wraw = sA["wraw0"] if buf == 0 else sA["wraw1"]
            tld = P.op("sync", "dma_start", out=wraw[:],
                       in_=w["W_fc"][blk * 128:(blk + 1) * 128, :],
                       waits=([wld_w[buf]] if wld_w[buf] else []), inc="qS")
            tq = quantize(wraw[:], 2048, 32, [tld] + ([q_guard] if q_guard else []))
            wld_w[buf] = tq
            w1t = sA["w1t"] if blk % 2 == 0 else sA["w1tb"]
            tev = None
            tevp = None
            for k in range(K):
                pbk = ps["B0"] if k % 2 == 0 else ps["B1"]
                ttr = P.op("tensor", "transpose", pbk[:, 0:128],
                           sA["wq"][:, k * 128:(k + 1) * 128], sb["ident_f32"][:],
                           waits=[tq, t_id] + ([tevp] if tevp else []) +
                                 ([w1t_rd] if w1t_rd and k <= 1 else []), inc="cT")
                tevp = tev
                tev = P.op("scalar", "activation", w1t[:, k * 128:(k + 1) * 128],
                           pbk[:, 0:128], AF.Identity, waits=[ttr], inc="cS")
            q_guard = tev
            t_w1t = tev
            for g2 in range(G4):
                pbuf = g2 % 2
                pk = ps["A0"] if pbuf == 0 else ps["A1"]
                mwaits = [t_w1t] + uT_tok + ([p_ev[pbuf]] if p_ev[pbuf] else [])
                for k in range(K):
                    tok_mm = P.op("tensor", "matmul", pk[:, 0:512],
                                  w1t[:, k * 128:(k + 1) * 128],
                                  sA["uT"][:, k * T + g2 * 512: k * T + (g2 + 1) * 512],
                                  start=(k == 0), stop=(k == K - 1),
                                  waits=(mwaits if k == 0 else []),
                                  inc=("cT" if k == K - 1 else None))
                trl = P.op("scalar", "activation", sA["rl"][:], pk[:, 0:512], AF.Relu,
                           waits=[tok_mm] + ([p_ev[pbuf]] if p_ev[pbuf] else []) +
                                 ([rl_guard] if rl_guard else []), inc="cS")
                th = P.op("vector", "tensor_tensor", sA["hstg"][:, g2 * 512:(g2 + 1) * 512],
                          sA["rl"][:], sA["rl"][:], op=ALU.mult,
                          waits=[trl] + ([hst_rd] if hst_rd and g2 == 0 else []), inc="cV")
                rl_guard = th
                tsq = P.op("vector", "tensor_tensor", sA["sqv"][:],
                           sA["hstg"][:, g2 * 512:(g2 + 1) * 512],
                           sA["hstg"][:, g2 * 512:(g2 + 1) * 512], op=ALU.mult,
                           waits=[th] + ([sq_guard] if sq_guard else []), inc="cV")
                p_ev[pbuf] = tsq
                pS = ps["BIGa"] if g2 < 2 else ps["BIGb"]
                off = 512 * (g2 % 2)
                tmmq = P.op("tensor", "matmul", pS[0:1, off:off + 512],
                            sA["onec"][:], sA["sqv"][:],
                            start=(blk == 0), stop=(blk == 63), skip_group_check=True,
                            waits=[tsq, t_one], inc="cT")
                sq_guard = tmmq
            w1t_rd = tok_mm
            tspl = P.op("sync", "dma_start", out=sc["h_sp"][blk * 128:(blk + 1) * 128, :],
                        in_=sA["hstg"][:], waits=[P.tok("cV")], inc="qS")
            hst_rd = tspl
        t_h_done = P.tok("qS")
        t_ssq = P.tok("cT")
        ch = Chain(P)
        for g2 in range(G4):
            pS = ps["BIGa"] if g2 < 2 else ps["BIGb"]
            off = 512 * (g2 % 2)
            ch.op("vector", "tensor_copy", sA["rsrow"][0:1, g2 * 512:(g2 + 1) * 512],
                  pS[0:1, off:off + 512], waits=[t_ssq])
        ch.op("vector", "tensor_scalar", sA["rsrow"][0:1, :], sA["rsrow"][0:1, :],
              1.0 / 8192, EPSR, op0=ALU.mult, op1=ALU.add)
        t_ln = P.op("scalar", "activation", sA["rsrow"][0:1, :], sA["rsrow"][0:1, :],
                    AF.Ln, waits=[ch.t], inc="cS")
        t_rs = P.op("scalar", "activation", sA["rsrow"][0:1, :], sA["rsrow"][0:1, :],
                    AF.Exp, scale=-0.5, waits=[t_ln], inc="cS")
        tev = None
        for i in range(TIL):
            ttr = P.op("tensor", "transpose", ps["B1"][:, 0:1],
                       sA["rsrow"][0:1, i * 128:(i + 1) * 128], sb["ident_f32"][0:1, 0:1],
                       waits=[t_rs, t_id] + ([tev] if tev else []), inc="cT")
            tev = P.op("scalar", "activation", sA["rs_cols"][:, i:i + 1], ps["B1"][:, 0:1],
                       AF.Identity, waits=[ttr], inc="cS")
        t_rs_cols = tev
        t_rsp = P.op("vector", "tensor_copy", sb["rs_keep"][:], sA["rs_cols"][:],
                     waits=[t_rs_cols], inc="cV")

        w2stg_rd = w1t_rd
        for k2 in range(64):
            buf = k2 % 2
            wraw = sA["wraw0"] if buf == 0 else sA["wraw1"]
            tld = P.op("sync", "dma_start",
                       out=wraw[:].rearrange("p (k n) -> p k n", n=128),
                       in_=w["W_proj"][:, k2 * 128:(k2 + 1) * 128].rearrange("(k p) n -> p k n", p=128),
                       waits=([wld_w[buf]] if wld_w[buf] else []), inc="qS")
            tq = quantize(wraw[:], 2048, 32, [tld] + ([q_guard] if q_guard else []))
            wld_w[buf] = tq
            w1t = sA["w1t"] if k2 % 2 == 0 else sA["w1tb"]
            tev = None
            tevp = None
            for k in range(K):
                pbk = ps["B0"] if k % 2 == 0 else ps["B1"]
                ttr = P.op("tensor", "transpose", pbk[:, 0:128],
                           sA["wq"][:, k * 128:(k + 1) * 128], sb["ident_f32"][:],
                           waits=[tq, t_id] + ([tevp] if tevp else []) +
                                 ([w2stg_rd] if w2stg_rd and k <= 1 else []), inc="cT")
                tevp = tev
                tev = P.op("scalar", "activation", w1t[:, k * 128:(k + 1) * 128],
                           pbk[:, 0:128], AF.Identity, waits=[ttr], inc="cS")
            q_guard = tev
            tspl = P.op("sync", "dma_start", out=sc["w2T_sp"][k2 * 128:(k2 + 1) * 128, :],
                        in_=w1t[:], waits=[tev], inc="qS")
            w2stg_rd = tspl
        t_w2t_done = P.tok("qS")

    barrier_all(P)
    with contextlib_ExitStack() as esB:
        sB = {}
        def SB(name, shape, dt):
            sB[name] = esB.enter_context(nc.sbuf_tensor("p3b_" + name, shape, dt))
        SB("w2t0", [128, 1024], BF16); SB("w2t1", [128, 1024], BF16)
        SB("w2t2", [128, 1024], BF16); SB("w2t3", [128, 1024], BF16)
        SB("ht0", [128, 512], BF16); SB("ht1", [128, 512], BF16)
        SB("ht2", [128, 512], BF16); SB("ht3", [128, 512], BF16)
        SB("ostg0", [128, D], F32); SB("ostg1", [128, D], F32)
        SB("ostg2", [128, D], F32); SB("ostg3", [128, D], F32)
        SB("xin0", [128, D], F32); SB("xin1", [128, D], F32)
        SB("x2_f32", [128, D], F32)
        ost = [sB["ostg0"], sB["ostg1"], sB["ostg2"], sB["ostg3"]]

        def pregion(tt, nch):
            if tt == 0:
                return ps["BIGa"][:, nch * 512:(nch + 1) * 512]
            if tt == 1:
                return ps["BIGb"][:, nch * 512:(nch + 1) * 512]
            if tt == 2:
                return (ps["A0"] if nch == 0 else ps["A1"])[:, 0:512]
            return (ps["B0"] if nch == 0 else ps["B1"])[:, 0:512]

        w2_w = [None, None, None, None]
        ht_w = [None, None, None, None]
        ost_rd = [None] * 4
        prev_evs = []
        final_done = []
        for quad in range(4):
            for half in range(2):
                for k2 in range(64):
                    buf = k2 % 4
                    w2t = sB["w2t" + str(buf)]
                    htb = sB["ht" + str(buf)]
                    tw2 = P.op("gpsimd", "dma_start", out=w2t[:],
                               in_=sc["w2T_sp"][k2 * 128:(k2 + 1) * 128,
                                                half * 1024:(half + 1) * 1024],
                               waits=[t_w2t_done] + ([w2_w[buf]] if w2_w[buf] else []),
                               inc="qG")
                    tht = P.op("sync", "dma_start", out=htb[:],
                               in_=sc["h_sp"][k2 * 128:(k2 + 1) * 128,
                                              quad * 512:(quad + 1) * 512],
                               waits=([ht_w[buf]] if ht_w[buf] else []), inc="qS")
                    tok_mm = None
                    for tt in range(4):
                        for nch in range(2):
                            first = (tt == 0 and nch == 0)
                            last = (tt == 3 and nch == 1)
                            mwaits = []
                            if first:
                                mwaits = [tw2, tht]
                                if k2 == 0:
                                    mwaits += prev_evs
                            tok_mm = P.op("tensor", "matmul", pregion(tt, nch),
                                          htb[:, tt * 128:(tt + 1) * 128],
                                          w2t[:, nch * 512:(nch + 1) * 512],
                                          start=(k2 == 0), stop=(k2 == 63),
                                          skip_group_check=True,
                                          waits=mwaits,
                                          inc=("cT" if last else None))
                    w2_w[buf] = tok_mm
                    ht_w[buf] = tok_mm
                t_mm_all = P.tok("cT")
                evs = []
                for tt in range(4):
                    i_abs = quad * 4 + tt
                    for nch in range(2):
                        tev = P.op("scalar", "activation",
                                   ost[tt][:, half * 1024 + nch * 512: half * 1024 + (nch + 1) * 512],
                                   pregion(tt, nch), AF.Identity,
                                   scale=sb["rs_keep"][:, i_abs:i_abs + 1],
                                   waits=[t_mm_all, t_rsp] +
                                         ([ost_rd[tt]] if ost_rd[tt] and half == 0 and nch == 0 else []),
                                   inc="cS")
                        evs.append(tev)
                prev_evs = [P.tok("cS")]
            t_ev = P.tok("cS")
            for tt in range(4):
                i_abs = quad * 4 + tt
                buf = tt % 2
                xs = sB["xin0"] if buf == 0 else sB["xin1"]
                tldx = P.op("sync", "dma_start", out=xs[:],
                            in_=sc["xres"][i_abs * 128:(i_abs + 1) * 128, :],
                            waits=(final_done[-2:] if final_done else []), inc="qS")
                tm = P.op("vector", "tensor_tensor", sB["x2_f32"][:], xs[:], ost[tt][:],
                          op=ALU.add,
                          waits=[tldx, t_ev] + (final_done[-1:] if final_done else []),
                          inc="cV")
                two = P.op("sync", "dma_start", out=sc["y_out"][i_abs * 128:(i_abs + 1) * 128, :],
                           in_=sB["x2_f32"][:], waits=[tm], inc="qS")
                ost_rd[tt] = tm
                final_done.append(tm)
    return P.tok("qS")


def alloc_p3_persist(nc, es, pfx="p3_"):
    sb = {}
    def S(name, shape, dt):
        sb[name] = es.enter_context(nc.sbuf_tensor(pfx + name, shape, dt))
    S("ident_f32", [128, 128], F32)
    S("rs_keep", [128, 16], F32)
    for tag in ("a",):
        for c in ("s1", "s2", "mn", "nv", "ve", "rs", "nm"):
            S(c + tag, [128, 1], F32)
    return sb


import concourse.bass as _bass
from concourse.bass_utils import run_bass_kernel_spmd as _run
import contextlib as _ctl2

P0_W = ["ln1_g", "ln1_b", "ln2_g", "ln2_b", "tm_g", "tm_b", "time_decay",
        "time_first", "Wv", "Wr", "Wo", "bo", "cm_g", "cm_b", "Wck", "Wcv", "Wcr"]
P0_SHAPES = {"Wv": [2048, 2048], "Wr": [2048, 2048], "Wo": [2048, 2048],
             "Wcr": [2048, 2048], "Wck": [2048, 8192], "Wcv": [8192, 2048]}
P1_SHAPES = {"gate_w": [16, 2048], "w1": [16, 2048, 1408], "w2": [16, 1408, 2048],
             "sw1": [1, 2048, 1408], "sw2": [1, 1408, 2048]}
P2_SHAPES = {"W_in": [2048, 8512], "conv_w": [4352, 4], "conv_b": [4352],
             "A": [64], "D": [64], "dt_bias": [64], "norm_w": [4096],
             "W_out": [4096, 2048]}
P3_SHAPES = {"W_fc": [8192, 2048], "W_proj": [2048, 8192]}


def build_fused():
    nc = _bass.Bass("TRN2", target_bir_lowering=False)
    w0 = {"x_in": nc.dram_tensor("x_in", [T, D], F32, kind="ExternalInput").ap()}
    for k in P0_W:
        w0[k] = nc.dram_tensor("w0_" + k, P0_SHAPES.get(k, [2048]), F32,
                               kind="ExternalInput").ap()
    w1 = {k: nc.dram_tensor("w1_" + k, shp, F32, kind="ExternalInput").ap()
          for k, shp in P1_SHAPES.items()}
    w2 = {k: nc.dram_tensor("w2_" + k, shp, F32, kind="ExternalInput").ap()
          for k, shp in P2_SHAPES.items()}
    w3 = {k: nc.dram_tensor("w3_" + k, shp, F32, kind="ExternalInput").ap()
          for k, shp in P3_SHAPES.items()}
    sc = {}
    sc["xres"] = nc.dram_tensor("xres", [T, D], F32, kind="Internal").ap()
    sc["y_out"] = nc.dram_tensor("y_out", [T, D], F32, kind="ExternalOutput").ap()
    sc["spill_a"] = nc.dram_tensor("spill_a", [T, D], BF16, kind="Internal").ap()
    sc["spill_b"] = nc.dram_tensor("spill_b", [T, D], BF16, kind="Internal").ap()
    sc["tou_sp"] = nc.dram_tensor("tou_sp", [D, T], BF16, kind="Internal").ap()
    sc["rc_sp"] = nc.dram_tensor("rc_sp", [D, T], BF16, kind="Internal").ap()
    sc["vc_acc"] = nc.dram_tensor("vc_acc", [T, D], BF16, kind="Internal").ap()
    sc["moe_acc"] = nc.dram_tensor("moe_acc", [T, D], BF16, kind="Internal").ap()
    sc["zT_sp"] = nc.dram_tensor("zT_sp", [4096, T], BF16, kind="Internal").ap()
    sc["xbcT_sp"] = nc.dram_tensor("xbcT_sp", [4352, T], BF16, kind="Internal").ap()
    sc["xbcP_sp"] = nc.dram_tensor("xbcP_sp", [4096, T], BF16, kind="Internal").ap()
    sc["csrow_sp"] = nc.dram_tensor("csrow_sp", [16, 8192], F32, kind="Internal").ap()
    sc["yg_sp"] = nc.dram_tensor("yg_sp", [T, 4096], BF16, kind="Internal").ap()
    sc["attnT_sp"] = nc.dram_tensor("attnT_sp", [2048, T], BF16, kind="Internal").ap()
    sc["h_sp"] = nc.dram_tensor("h_sp", [8192, T], BF16, kind="Internal").ap()
    sc["w2T_sp"] = nc.dram_tensor("w2T_sp", [8192, 2048], BF16, kind="Internal").ap()
    sc["xres_out"] = sc["xres"]

    es = _ctl2.ExitStack()
    with es:
        ps = alloc_psum(nc, es)
        P = Prog(nc)
        with _ctl2.ExitStack() as e0:
            sb0 = alloc_p0(nc, e0)
            pre = emit_common(P, nc, sb0)
            emit_p0(P, nc, w0, sc, sb0, ps, pre)
        barrier_all(P)
        with _ctl2.ExitStack() as e1:
            sb1 = alloc_p1(nc, e1)
            emit_p1(P, nc, w1, sc, sb1, ps, [])
        barrier_all(P)
        with _ctl2.ExitStack() as e2:
            sb2 = alloc_p2_persist(nc, e2)
            emit_p2(P, nc, w2, sc, sb2, ps, e2)
        barrier_all(P)
        with _ctl2.ExitStack() as e3:
            sb3 = alloc_p3_persist(nc, e3)
            emit_p3(P, nc, w3, sc, sb3, ps, e3)
        tout = P.tok("qS")
        P.wait("sync", tout)
        semhs = {}
        for name in P.semnames:
            semhs[name] = es.enter_context(nc.semaphore(name))
        with nc.Block() as block:
            P.finalize(block, semhs)
    return nc, P.nops


def kernel(**inputs):
    NCORES = 8
    cast = np.ascontiguousarray
    _ = inputs.get("L0_attn_Wk")
    nc, nops = build_fused()
    wmap = {}
    for k in P0_W:
        wmap["w0_" + k] = cast(inputs["L0_attn_" + k])
    for k in P1_SHAPES:
        wmap["w1_" + k] = cast(inputs["L0_ffn_" + k])
    for k in P2_SHAPES:
        wmap["w2_" + k] = cast(inputs["L1_attn_" + k])
    for k in P3_SHAPES:
        wmap["w3_" + k] = cast(inputs["L1_ffn_" + k])
    maps = []
    for b in range(NCORES):
        m = dict(wmap)
        m["x_in"] = cast(inputs["L0_attn_x"][b])
        maps.append(m)
    r = _run(nc, maps, core_ids=list(range(NCORES)))
    out = np.stack([r.results[b]["y_out"] for b in range(NCORES)], axis=0)
    return out.astype(np.float32)
